# version 03
# baseline (speedup 1.0000x reference)
import contextlib
import numpy as np
import concourse.bass as bass
import concourse.mybir as mybir
from concourse.bass_utils import run_bass_kernel_spmd

F32 = mybir.dt.float32
BF16 = mybir.dt.bfloat16
AF = mybir.ActivationFunctionType
ALU = mybir.AluOpType

D = 1024
H = 4
DK = 256
CHUNK = 32
CDEV = 128
NCORES = 8

P3_EXPERTS = list(range(8)) + [8]
P3_QUARTERS = 4
P3_TRAP = None
P3_EB2 = True
P3_SKIP = ()
P3_W2MAX = None
DEBUG_SCRATCH = False
DEBUG_FEED = False
DEBUG_P3 = False
PARANOID = False
MISC_VALS = [1e-6, 1e-5, 1e-8, -0.5, 1.0 / D, 1.0 / DK, 1e-4, 0.05, 2.0, 1.0,
             0.5, -1.0, 0.0]
MI = {v: i for i, v in enumerate(MISC_VALS)}


class Sem:
    def __init__(self, nc, es, name):
        self.h = es.enter_context(nc.semaphore(name))
        self.n = 0


DEBUG_SYNC = False


class Phase:
    ENGINES = ("sync", "gpsimd", "vector", "scalar", "tensor")

    def __init__(self, name):
        self.name = name
        self.q = {k: [] for k in self.ENGINES}
        self.hwm = {k: {} for k in self.ENGINES}
        self.auto = None
        self.last_done = {}

    def emit(self, eng, f, wait=None, inc=None):
        if DEBUG_SYNC and self.auto and f is not None:
            a = self.auto[eng]
            d = 16 if eng in ("sync", "gpsimd") else 1
            if self.last_done.get(eng):
                wait = list(wait or []) + [self.last_done[eng]]
            if inc:
                s0, d0 = inc[0]
                self.last_done[eng] = (s0, s0.n + d0)
            else:
                inc = [(a, d)]
                self.last_done[eng] = (a, a.n + d)
        q = self.q[eng]
        if wait:
            hw = self.hwm[eng]
            for s, v in wait:
                if v > 0 and v > hw.get(id(s), -1):
                    hw[id(s)] = v
                    q.append(lambda e, s=s, v=v: e.wait_ge(s.h, v))
        if inc:
            for s, d in inc:
                s.n += d
            def g(e, f=f, incs=tuple(inc)):
                ins = f(e)
                for s, d in incs:
                    ins = ins.then_inc(s.h, d)
            q.append(g)
        elif f is not None:
            q.append(f)

    def wait(self, eng, sem, val):
        self.emit(eng, None, wait=[(sem, val)])

    def run(self, nc):
        with nc.Block(self.name) as block:
            for eng in self.ENGINES:
                cmds = self.q[eng]
                if not cmds:
                    continue
                def body(e, cmds=cmds):
                    for cf in cmds:
                        cf(e)
                getattr(block, eng)(body)


def host_layout(inputs):
    g = lambda k: np.ascontiguousarray(np.asarray(inputs[k], dtype=np.float32))
    d = {}
    d["wqT"] = np.ascontiguousarray(g("L0_attn_Wq").T)
    d["wkT"] = np.ascontiguousarray(g("L0_attn_Wk").T)
    d["wvT"] = np.ascontiguousarray(g("L0_attn_Wv").T)
    d["woT"] = np.ascontiguousarray(g("L0_attn_Wo").T)
    g1T = np.ascontiguousarray(g("L0_attn_gate_w1").T)
    d["g1T"] = np.ascontiguousarray(g1T[:D])
    d["g1Ts"] = np.ascontiguousarray(g1T[D:])
    d["g2T"] = np.ascontiguousarray(g("L0_attn_gate_w2").T)
    d["wbT"] = np.ascontiguousarray(g("L0_attn_Wb").T)
    d["rwT"] = np.ascontiguousarray(g("L0_ffn_router_w").T)
    d["sw1"] = g("L0_ffn_sw1")
    d["sw2"] = g("L0_ffn_sw2")
    d["ew1"] = g("L0_ffn_ew1")
    d["ew2"] = g("L0_ffn_ew2")
    d["eb2"] = g("L0_ffn_eb2")
    d["eb1c"] = np.ascontiguousarray(
        g("L0_ffn_eb1").reshape(8, 32, 128).transpose(2, 0, 1))
    d["sb1c"] = np.ascontiguousarray(g("L0_ffn_sb1").reshape(32, 128).T)
    d["sb2c"] = np.ascontiguousarray(g("L0_ffn_sb2").reshape(8, 128).T)
    d["gb1c"] = np.ascontiguousarray(g("L0_attn_gate_b1").reshape(16, 128).T)
    for k, nm in (("qcw", "L0_attn_qconv_w"), ("kcw", "L0_attn_kconv_w"),
                  ("vcw", "L0_attn_vconv_w")):
        d[k] = np.ascontiguousarray(g(nm).reshape(8, 128, 4).transpose(1, 0, 2))
    d["fsw"] = np.ascontiguousarray(
        g("L0_attn_fir_short_w").reshape(8, 128, 3).transpose(1, 0, 2))
    d["flw"] = np.ascontiguousarray(
        g("L0_attn_fir_long_w").reshape(8, 128, 63).transpose(1, 0, 2))
    d["nwb"] = np.ascontiguousarray(
        g("L0_ffn_norm_weight").reshape(8, 128).T)
    d["onb"] = np.ascontiguousarray(
        np.broadcast_to(g("L0_attn_onorm_w")[None, :], (128, DK)))
    d["bbb"] = np.ascontiguousarray(
        np.broadcast_to(g("L0_attn_base_bias").reshape(-1)[None, :], (128, 16)))
    d["lt"] = g("L0_attn_log_temp").reshape(1, 4)
    d["fr"] = g("L0_attn_floor_raw").reshape(1, 16)
    d["misc"] = np.ascontiguousarray(
        np.broadcast_to(np.array(MISC_VALS, dtype=np.float32)[None, :],
                        (128, len(MISC_VALS))))
    d["id128"] = np.eye(128, dtype=np.float32)
    tri = np.triu(np.ones((CDEV, CDEV), dtype=np.float32))
    d["mLSn"] = np.ascontiguousarray(-(1.0 - tri))
    d["mUSn"] = np.ascontiguousarray(-np.triu(tri, 1))
    d["mUI"] = tri.copy()
    d["ones1"] = np.ones((1, 128), dtype=np.float32)
    d["id4"] = np.eye(4, dtype=np.float32)
    d["id16"] = np.eye(16, dtype=np.float32)
    sel8 = np.zeros((8, 8, 128), dtype=np.float32)
    for e in range(8):
        sel8[e, e, :] = 1.0
    d["sel8"] = sel8
    sel4 = np.zeros((4, 4, CDEV), dtype=np.float32)
    for h in range(4):
        sel4[h, h, :] = 1.0
    d["sel4"] = sel4
    nsel = np.zeros((128, 4, 4), dtype=np.float32)
    for h in range(4):
        nsel[:, h, h] = 1.0
    d["nsel"] = nsel
    return d


class Core:
    def __init__(self, nc, es, L):
        self.nc = nc
        self.es = es
        self.L = L
        self.sems = {}
        self.gjk = es.enter_context(nc.sbuf_tensor("gjk", [128, 128], F32))

    def vpad(self, ph, n=4):
        for k in range(n):
            ph.emit("vector", lambda e: e.memset(self.gjk[:, 0:128], 0.0))

    def sem(self, name):
        if name not in self.sems:
            self.sems[name] = Sem(self.nc, self.es, name)
        return self.sems[name]

    def sb(self, es, name, shape, dtype):
        return es.enter_context(self.nc.sbuf_tensor(name, list(shape), dtype))

    def ps(self, es, name, shape, dtype=F32):
        return es.enter_context(self.nc.psum_tensor(name, list(shape), dtype))


def declare_io(nc, L, host):
    t = {}
    t["x"] = nc.dram_tensor("x", [L, D], F32, kind="ExternalInput").ap()
    for k, v in host.items():
        t[k] = nc.dram_tensor(k, list(v.shape), F32, kind="ExternalInput").ap()
    t["out"] = nc.dram_tensor("out", [L, D], F32, kind="ExternalOutput").ap()
    kind = "ExternalOutput" if DEBUG_SCRATCH else "Internal"
    if DEBUG_FEED:
        kind = "ExternalInput"
    if DEBUG_SCRATCH:
        t["dbg_d"] = nc.dram_tensor("dbg_d", [128, 32, 8], F32, kind="ExternalOutput").ap()
    if DEBUG_P3:
        t["dbg3_d"] = nc.dram_tensor("dbg3_d", [8, 4, 8, 128], BF16,
                                     kind="ExternalOutput").ap()
        t["dbg3p_d"] = nc.dram_tensor("dbg3p_d", [128, 4, 8, 32], F32,
                                      kind="ExternalOutput").ap()
    t["xn_d"] = nc.dram_tensor("xn_d", [8, 128, L], BF16, kind=kind).ap()
    t["nr_d"] = nc.dram_tensor("nr_d", [8, 128, L], BF16, kind=kind).ap()
    t["s_d"] = nc.dram_tensor("s_d", [8, 128, L], BF16, kind=kind).ap()
    return t



def build_consts(co, io, es):
    c = {}
    mk = lambda n, shp, dt=F32: c.__setitem__(n, co.sb(co.es, "c_" + n, shp, dt))
    mk("misc", [128, len(MISC_VALS)])
    mk("id128b", [128, 128], BF16)
    mk("id128f", [128, 128])
    mk("mLSn", [CDEV, CDEV]); mk("mUSn", [CDEV, CDEV]); mk("mUI", [CDEV, CDEV])
    mk("id4f", [4, 4]); mk("id16f", [16, 16])
    mk("ones1b", [1, 128], BF16); mk("ones1f", [1, 128])
    mk("sel8", [8, 8, 128], BF16); mk("sel4", [4, 4, CDEV]); mk("nsel", [128, 4, 4], BF16)
    mk("onb", [128, DK]); mk("bbb", [128, 16])
    mk("gb1c", [128, 16]); mk("sb1c", [128, 32]); mk("sb2c", [128, 8])
    mk("eb1c", [128, 8, 32]); mk("eb2", [8, 1024], BF16)
    mk("qcw", [128, 8, 4]); mk("kcw", [128, 8, 4]); mk("vcw", [128, 8, 4])
    mk("fsw", [128, 8, 3]); mk("flw", [128, 8, 63])
    mk("wbT", [128, 8, 4], BF16); mk("rwT", [128, 8, 8], BF16)
    mk("g2T", [128, 16, 16], BF16)
    mk("lt", [1, 4]); mk("fr", [1, 16])
    mk("itb", [128, 16]); mk("flb", [128, 16])
    mk("it16", [1, 16]); mk("fl16", [1, 16]); mk("sp_t", [1, 4])

    ph = Phase("consts")
    if DEBUG_SYNC:
        ph.auto = {e_: co.sem("consts_a" + e_) for e_ in Phase.ENGINES}
    ld = co.sem("c_ld")
    ldg = co.sem("c_ldg")
    cv = co.sem("c_cv")
    cs = co.sem("c_cs")
    mmc = co.sem("c_mmc")
    m = c["misc"]

    plain = ["misc", "id128f", "id4f", "id16f", "mLSn", "mUSn", "mUI", "onb", "bbb",
             "gb1c", "sb1c", "sb2c", "eb1c", "qcw", "kcw", "vcw", "fsw", "flw",
             "sel4", "lt", "fr"]
    for nm in plain:
        srcmap = {"ones1f": "ones1", "id128f": "id128", "id4f": "id4", "id16f": "id16"}
        src = io[srcmap.get(nm, nm)]
        ph.emit("sync", lambda e, o=c[nm], i=src: e.dma_start(out=o[:], in_=i),
                inc=[(ld, 16)])
    ph.emit("sync", lambda e: e.dma_start(out=c["ones1f"][:], in_=io["ones1"]), inc=[(ld, 16)])
    casting = [("id128b", io["id128"]), ("ones1b", io["ones1"]),
               ("sel8", io["sel8"]), ("nsel", io["nsel"]), ("eb2", io["eb2"]),
               ("wbT", io["wbT"].rearrange("(kc p) h -> p kc h", p=128)),
               ("rwT", io["rwT"].rearrange("(kc p) h -> p kc h", p=128)),
               ("g2T", io["g2T"].rearrange("(kc p) h -> p kc h", p=128))]
    for nm, src in casting:
        ph.emit("gpsimd", lambda e, o=c[nm], i=src: e.dma_start(out=o[:], in_=i),
                inc=[(ldg, 16)])
    for fn in (AF.Identity, AF.Relu, AF.Square, AF.Abs, AF.Exp, AF.Ln,
               AF.Sigmoid, AF.Silu, AF.Gelu):
        ph.emit("scalar", lambda e, fn=fn: e.activation(
            out=c["sp_t"][:], in_=c["misc"][0:1, MI[1.0]:MI[1.0] + 4], func=fn),
            wait=[(ld, ld.n)])
    sp = c["sp_t"]
    ph.emit("scalar", lambda e: e.activation(out=sp[:], in_=c["lt"][:], func=AF.Abs),
            wait=[(ld, ld.n), (ldg, ldg.n)])
    ph.emit("scalar", lambda e: e.activation(out=sp[:], in_=sp[:], func=AF.Exp,
                                             scale=m[0:1, MI[-1.0]:MI[-1.0] + 1]))
    ph.emit("scalar", lambda e: e.activation(out=sp[:], in_=sp[:], func=AF.Ln,
                                             bias=m[0:1, MI[1.0]:MI[1.0] + 1]),
            inc=[(cs, 1)])
    ph.emit("vector", lambda e: e.tensor_scalar(
        out=c["it16"][:, 0:4], in0=c["lt"][:], scalar1=m[0:1, MI[0.0]:MI[0.0] + 1],
        scalar2=None, op0=ALU.max), wait=[(cs, cs.n)])
    co.vpad(ph)
    ph.emit("vector", lambda e: e.tensor_tensor(out=c["it16"][:, 0:4],
                                                in0=c["it16"][:, 0:4], in1=sp[:], op=ALU.add))
    co.vpad(ph)
    ph.emit("vector", lambda e: e.tensor_scalar(
        out=c["it16"][:, 0:4], in0=c["it16"][:, 0:4],
        scalar1=m[0:1, MI[1e-4]:MI[1e-4] + 1], scalar2=None, op0=ALU.add))
    co.vpad(ph)
    ph.emit("vector", lambda e: e.reciprocal(out=c["it16"][:, 0:4], in_=c["it16"][:, 0:4]))
    co.vpad(ph)
    for h in range(3, -1, -1):
        for p in range(4):
            if h == 0 and p == 0:
                continue
            last = (h == 0 and p == 3)
            ph.emit("vector", lambda e, h=h, p=p: e.tensor_copy(
                out=c["it16"][:, h * 4 + p:h * 4 + p + 1], in_=c["it16"][:, h:h + 1]),
                inc=([(cv, 1)] if last else None))
    co.vpad(ph)
    ph.emit("scalar", lambda e: e.activation(out=c["fl16"][:], in_=c["fr"][:],
                                             func=AF.Sigmoid), wait=[(ld, ld.n)])
    ph.emit("scalar", lambda e: e.mul(c["fl16"][:], c["fl16"][:],
                                      m[0:1, MI[0.05]:MI[0.05] + 1]), inc=[(cs, 1)])
    pb = co.ps(es, "c_pb", [128, 32], F32)
    ph.emit("tensor", lambda e: e.matmul(pb[:, 0:16], c["ones1f"][:], c["it16"][:],
                                         start=True, stop=True),
            wait=[(cv, cv.n), (cs, cs.n)])
    ph.emit("tensor", lambda e: e.matmul(pb[:, 16:32], c["ones1f"][:], c["fl16"][:],
                                         start=True, stop=True), inc=[(mmc, 1)])
    ph.emit("vector", lambda e: e.tensor_copy(out=c["itb"][:], in_=pb[:, 0:16]),
            wait=[(mmc, mmc.n)])
    ph.emit("vector", lambda e: e.tensor_copy(out=c["flb"][:], in_=pb[:, 16:32]),
            inc=[(cv, 1)])
    ph.wait("sync", cv, cv.n)
    ph.wait("sync", ld, ld.n)
    ph.wait("gpsimd", ldg, ldg.n)
    return c, ph



def build_p1(co, io, c, es):
    L = co.L
    ph = Phase("p1")
    if DEBUG_SYNC:
        ph.auto = {e_: co.sem("p1_a" + e_) for e_ in Phase.ENGINES}
    m = c["misc"]
    nt = L // 128
    nwb = co.sb(es, "p1_nwb", [128, 8], F32)
    nwl = co.sem("p1nwl")
    ph.emit("sync", lambda e: e.dma_start(out=nwb[:], in_=io["nwb"]), inc=[(nwl, 16)])
    ph.wait("vector", nwl, 16)

    xld = [co.sem("p1x0"), co.sem("p1x1")]
    vv = co.sem("p1v")
    ss = co.sem("p1s")
    tt = co.sem("p1t")
    wbx = [co.sem("p1wx0"), co.sem("p1wx1")]
    wbn = [co.sem("p1wn0"), co.sem("p1wn1")]

    xt = [co.sb(es, f"p1_x{i}", [128, D], F32) for i in range(2)]
    jnk = co.sb(es, "p1_jnk", [128, D], F32)
    sumall = co.sb(es, "p1_suma", [128, nt], F32)
    sqall = co.sb(es, "p1_sqa", [128, nt], F32)
    meanall = co.sb(es, "p1_mna", [128, nt], F32)
    varall = co.sb(es, "p1_vra", [128, nt], F32)
    rstdall = co.sb(es, "p1_rsa", [128, nt], F32)
    rrall = co.sb(es, "p1_rra", [128, nt], F32)
    scr = co.sb(es, "p1_scr", [128, nt], F32)
    dbg = co.sb(es, "p1_dbg", [128, nt, 8], F32) if DEBUG_SCRATCH else None
    xn = co.sb(es, "p1_xn", [128, D], F32)
    xnb = co.sb(es, "p1_xnb", [128, D], BF16)
    nrb = co.sb(es, "p1_nrb", [128, D], BF16)
    fmx = [co.sb(es, f"p1_fmx{i}", [128, 8, 128], BF16) for i in range(2)]
    fmn = [co.sb(es, f"p1_fmn{i}", [128, 8, 128], BF16) for i in range(2)]
    pt_ = [co.ps(es, f"p1_pt{i}", [128, 1024], BF16) for i in range(2)]

    acc_done = [0, 0]
    sq_done = [0, 0]
    for i in range(nt):
        par = i % 2
        ph.emit("sync", lambda e, i=i, par=par: e.dma_start(
            out=xt[par][:], in_=io["x"][i * 128:(i + 1) * 128, :]),
            wait=([(vv, acc_done[par])] if acc_done[par] else None),
            inc=[(xld[par], 16)])
        ph.emit("vector", lambda e, i=i, par=par: e.reduce_sum(
            out=sumall[:, i:i + 1], in_=xt[par][:], axis=mybir.AxisListType.X),
            wait=[(xld[par], xld[par].n)])
        ph.emit("scalar", lambda e, i=i, par=par: e.activation(
            out=(jnk if par == 0 else xn)[:], in_=xt[par][:], func=AF.Square),
            wait=[(xld[par], xld[par].n), (vv, sq_done[par])], inc=[(ss, 1)])
        ph.emit("vector", lambda e, i=i, par=par: e.reduce_sum(
            out=sqall[:, i:i + 1], in_=(jnk if par == 0 else xn)[:],
            axis=mybir.AxisListType.X),
            wait=[(ss, ss.n)], inc=[(vv, 1)])
        acc_done[par] = vv.n
        sq_done[par] = vv.n
    if DEBUG_SCRATCH:
        ph.emit("vector", lambda e: e.tensor_copy(out=dbg[:, :, 4], in_=sumall[:]),
                wait=[(ss, ss.n)])
        ph.emit("vector", lambda e: e.tensor_copy(out=dbg[:, :, 5], in_=sqall[:]))
    co.vpad(ph)
    ph.emit("vector", lambda e: e.tensor_scalar(
        out=meanall[:], in0=sumall[:], scalar1=m[:, MI[1.0 / D]:MI[1.0 / D] + 1],
        scalar2=None, op0=ALU.mult), wait=[(ss, ss.n)])
    co.vpad(ph)
    ph.emit("vector", lambda e: e.tensor_tensor(
        out=scr[:], in0=meanall[:], in1=meanall[:], op=ALU.mult))
    co.vpad(ph)
    ph.emit("vector", lambda e: e.scalar_tensor_tensor(
        out=varall[:], in0=sqall[:], scalar=m[:, MI[1.0 / D]:MI[1.0 / D] + 1],
        in1=scr[:], op0=ALU.mult, op1=ALU.subtract), inc=[(vv, 1)])
    ph.emit("scalar", lambda e: e.activation(out=rstdall[:], in_=varall[:], func=AF.Ln,
                                             bias=m[:, MI[1e-5]:MI[1e-5] + 1]),
            wait=[(vv, vv.n)])
    ph.emit("scalar", lambda e: e.activation(out=rstdall[:], in_=rstdall[:], func=AF.Exp,
                                             scale=m[:, MI[-0.5]:MI[-0.5] + 1]),
            inc=[(ss, 1)])
    co.vpad(ph)
    ph.emit("vector", lambda e: e.tensor_tensor(
        out=scr[:], in0=rstdall[:], in1=rstdall[:], op=ALU.mult), wait=[(ss, ss.n)])
    co.vpad(ph)
    ph.emit("vector", lambda e: e.tensor_tensor(
        out=scr[:], in0=scr[:], in1=varall[:], op=ALU.mult), inc=[(vv, 1)])
    co.vpad(ph)
    ph.emit("scalar", lambda e: e.activation(out=rrall[:], in_=scr[:], func=AF.Ln,
                                             bias=m[:, MI[1e-8]:MI[1e-8] + 1]),
            wait=[(vv, vv.n)])
    ph.emit("scalar", lambda e: e.activation(out=rrall[:], in_=rrall[:], func=AF.Exp,
                                             scale=m[:, MI[-0.5]:MI[-0.5] + 1]),
            inc=[(ss, 1)])
    batch_ss = ss.n
    ph.wait("vector", ss, batch_ss)
    if DEBUG_SCRATCH:
        dwb = co.sem("p1dwb")
        ph.emit("vector", lambda e: e.tensor_copy(out=dbg[:, :, 0], in_=meanall[:]))
        ph.emit("vector", lambda e: e.tensor_copy(out=dbg[:, :, 1], in_=varall[:]))
        ph.emit("vector", lambda e: e.tensor_copy(out=dbg[:, :, 2], in_=rstdall[:]))
        ph.emit("vector", lambda e: e.tensor_copy(out=dbg[:, :, 3], in_=sqall[:]))
        ph.emit("vector", lambda e: e.tensor_copy(out=dbg[:, :, 6], in_=rrall[:]),
                inc=[(vv, 1)])
        ph.emit("sync", lambda e: e.dma_start(out=io["dbg_d"], in_=dbg[:]),
                wait=[(vv, vv.n)], inc=[(dwb, 16)])
        ph.wait("sync", dwb, 16)

    for i in range(nt):
        par = i % 2
        x_t = xt[par]
        pt = pt_[par]
        ph.emit("sync", lambda e, i=i, par=par: e.dma_start(
            out=xt[par][:], in_=io["x"][i * 128:(i + 1) * 128, :]),
            wait=([(ss, batch_ss)] if i < 2 else None), inc=[(xld[par], 16)])
        ph.emit("vector", lambda e, x_t=x_t, i=i: e.tensor_scalar(
            out=xn[:], in0=x_t[:], scalar1=meanall[:, i:i + 1],
            scalar2=rstdall[:, i:i + 1], op0=ALU.subtract, op1=ALU.mult),
            wait=[(xld[par], xld[par].n)])
        ph.emit("vector", lambda e, i=i: e.tensor_scalar(
            out=nrb[:], in0=xn[:], scalar1=rrall[:, i:i + 1], scalar2=None,
            op0=ALU.mult))
        ph.emit("vector", lambda e: e.tensor_copy(out=xnb[:], in_=xn[:]), inc=[(vv, 1)])
        for kc in range(8):
            ph.emit("tensor", lambda e, kc=kc, pt=pt: e.transpose(
                pt[:, kc * 128:(kc + 1) * 128], xnb[:, kc * 128:(kc + 1) * 128],
                c["id128b"][:]),
                wait=([(vv, vv.n)] if kc == 0 else None),
                inc=([(tt, 1)] if kc == 7 else None))
        ph.emit("vector", lambda e, pt=pt, par=par: e.tensor_copy(
            out=fmx[par][:].rearrange("p a b -> p (a b)"), in_=pt[:]),
            wait=[(tt, tt.n), (wbx[par], wbx[par].n)], inc=[(vv, 1)])
        for kc in range(8):
            ph.emit("tensor", lambda e, kc=kc, pt=pt: e.transpose(
                pt[:, kc * 128:(kc + 1) * 128], nrb[:, kc * 128:(kc + 1) * 128],
                c["id128b"][:]),
                wait=([(vv, vv.n)] if kc == 0 else None),
                inc=([(tt, 1)] if kc == 7 else None))
        for kc in range(8):
            ph.emit("vector", lambda e, pt=pt, par=par, kc=kc: e.tensor_scalar(
                out=fmn[par][:, kc, :], in0=pt[:, kc * 128:(kc + 1) * 128],
                scalar1=nwb[:, kc:kc + 1], scalar2=None, op0=ALU.mult),
                wait=([(tt, tt.n), (wbn[par], wbn[par].n)] if kc == 0 else None),
                inc=([(vv, 1)] if kc == 7 else None))
        ph.emit("sync", lambda e, i=i, par=par: e.dma_start(
            out=io["xn_d"][:, :, i * 128:(i + 1) * 128].rearrange("a p b -> p a b"),
            in_=fmx[par][:]), wait=[(vv, vv.n - 1)], inc=[(wbx[par], 16)])
        ph.emit("sync", lambda e, i=i, par=par: e.dma_start(
            out=io["nr_d"][:, :, i * 128:(i + 1) * 128].rearrange("a p b -> p a b"),
            in_=fmn[par][:]), wait=[(vv, vv.n)], inc=[(wbn[par], 16)])
    for s in wbx + wbn:
        ph.wait("sync", s, s.n)
    return ph



def build_p2(co, io, c, es):
    L = co.L
    ph = Phase("p2")
    if DEBUG_SYNC:
        ph.auto = {e_: co.sem("p2_a" + e_) for e_ in Phase.ENGINES}
    m = c["misc"]
    TILE = 256
    ntile = L // TILE

    wld = co.sem("p2wld")
    xnl = [co.sem("p2xn0"), co.sem("p2xn1")]
    swb = [co.sem("p2swb0"), co.sem("p2swb1")]
    tv = co.sem("p2tv")
    vt = co.sem("p2vt")
    st = co.sem("p2st")

    wq = co.sb(es, "p2_wq", [128, 8, 1024], BF16)
    wk = co.sb(es, "p2_wk", [128, 8, 1024], BF16)
    wv = co.sb(es, "p2_wv", [128, 8, 1024], BF16)
    wosb = co.sb(es, "p2_wo", [128, 8, 1024], BF16)
    g1 = co.sb(es, "p2_g1", [128, 8, 2048], BF16)
    g1s = co.sb(es, "p2_g1s", [32, 2048], BF16)
    for sb_t, nm in ((wq, "wqT"), (wk, "wkT"), (wv, "wvT"), (wosb, "woT"), (g1, "g1T")):
        ph.emit("gpsimd", lambda e, o=sb_t, i=io[nm]: e.dma_start(
            out=o[:], in_=i.rearrange("(kc p) mm -> p kc mm", p=128)), inc=[(wld, 16)])
    ph.emit("gpsimd", lambda e: e.dma_start(out=g1s[:], in_=io["g1Ts"]), inc=[(wld, 16)])

    xnt = [co.sb(es, f"p2_xn{i}", [128, 8, TILE], BF16) for i in range(2)]
    qf = co.sb(es, "p2_qf", [128, 8, TILE], BF16)
    kf = co.sb(es, "p2_kf", [128, 8, TILE], BF16)
    vf = co.sb(es, "p2_vf", [128, 8, 64 + TILE], BF16)
    pre = co.sb(es, "p2_pre", [128, 2, TILE + 3], F32)
    cvx = co.sb(es, "p2_cvx", [128, 2, TILE], F32)
    usb = cvx[:, 0, :]
    lgF = cvx[0:16, 1, :]
    halos = {nm: co.sb(es, f"p2_h{nm}", [128, 8, 3], F32) for nm in "qkv"}

    bP = co.sb(es, "p2_bP", [4, TILE], F32)
    qnP = co.sb(es, "p2_qnP", [4, TILE], F32)
    knP = co.sb(es, "p2_knP", [4, TILE], F32)
    brnP = co.sb(es, "p2_brnP", [4, TILE], F32)
    qncol = co.sb(es, "p2_qnc", [128, 2, 4], F32)
    kncol = co.sb(es, "p2_knc", [128, 2, 4], F32)
    bcol = co.sb(es, "p2_bc", [128, 2, 4], F32)
    brncol = co.sb(es, "p2_brc", [128, 2, 4], F32)
    kntm = co.sb(es, "p2_kntm", [128, 2, 4, 256], BF16)
    kbtm = co.sb(es, "p2_kbtm", [128, 2, 4, 256], BF16)
    vbtm = co.sb(es, "p2_vbtm", [128, 2, 4, 256], BF16)
    vtmT = co.sb(es, "p2_vtm", [128, 2, 8, 128], BF16)
    dtm = co.sb(es, "p2_dtm", [128, 2, 8, 128], BF16)
    stmT = co.sb(es, "p2_stm", [128, 2, 8, 128], BF16)
    ltmT = co.sb(es, "p2_ltm", [128, 2, 8, 128], BF16)
    Sf = co.sb(es, "p2_Sf", [128, 2, 4, 256], F32)
    Sb = co.sb(es, "p2_Sb", [128, 2, 4, 256], BF16)
    firfm = co.sb(es, "p2_firfm", [128, 2, 8, TILE], BF16)
    SF_ = firfm[:, 0, :, :]
    LF_ = firfm[:, 1, :, :]
    qsq = firfm[:, 0, :, :]
    Nb = co.sb(es, "p2_Nb", [CDEV, CDEV], F32)
    Npb = co.sb(es, "p2_Npb", [CDEV, CDEV], F32)
    nt1 = co.sb(es, "p2_nt1", [CDEV, CDEV], F32)
    Msl = co.sb(es, "p2_Msl", [CDEV, CDEV], F32)
    Mpsl = co.sb(es, "p2_Mpsl", [CDEV, CDEV], F32)
    TPf = co.sb(es, "p2_TPf", [CDEV, CDEV], F32)
    TPb = co.sb(es, "p2_TPb", [CDEV, CDEV], BF16)
    attnb = co.sb(es, "p2_attnb", [CDEV, CDEV], BF16)

    ub = co.sb(es, "p2_ub", [CDEV, 256], BF16)
    Wfm = co.sb(es, "p2_Wfm", [128, 2, CDEV], BF16)
    stats = co.sb(es, "p2_stats", [128, 2, 32], F32)
    statsb = co.sb(es, "p2_statsb", [128, 32], BF16)
    statsF = co.sb(es, "p2_stF", [32, 2, 128], BF16)
    sjnk = co.sb(es, "p2_sjnk", [128, 256], F32)
    ssum = co.sb(es, "p2_ssum", [128, 2, 16], F32)
    ssq2 = co.sb(es, "p2_ssq2", [128, 2, 16], F32)
    msq = co.sb(es, "p2_msq", [128, 2, 16], F32)


    lgt = co.sb(es, "p2_lgt", [128, 16], F32)
    mx4 = co.sb(es, "p2_mx4", [128, 4], F32)
    sm4 = co.sb(es, "p2_sm4", [128, 4], F32)
    rsm4 = co.sb(es, "p2_rsm4", [128, 4], F32)
    exv = co.sb(es, "p2_exv", [128, 16], F32)
    prb = co.sb(es, "p2_prb", [128, 2, 16], F32)

    rms4 = co.sb(es, "p2_rms4", [128, 4], F32)
    rr4 = co.sb(es, "p2_rr4", [128, 4], F32)

    cmb2 = co.sb(es, "p2_cmb2", [128, 8, 128], BF16)
    cmb = kntm[:].rearrange("p a b c -> p (a b c)").bitcast(F32).rearrange(
        "p (a b) -> p a b", a=8)
    cfm = kbtm[:].rearrange("p a b c -> p (a b c)").rearrange(
        "p (a b c) -> p a b c", a=8, b=2)
    sf = [co.sb(es, "p2_sf0", [128, 8, TILE], BF16)]
    sf = [sf[0], sf[0]]

    mark = co.sb(es, "p2_mark", [1, 4], F32)
    pA = co.ps(es, "p2_pA", [128, 512], F32)
    pB = co.ps(es, "p2_pB", [128, 512], F32)
    pN = co.ps(es, "p2_pN", [128, 512], F32)
    pT = co.ps(es, "p2_pT", [128, 512], F32)
    pU = co.ps(es, "p2_pU", [128, 512], F32)
    pW = co.ps(es, "p2_pW", [128, 512], F32)
    pS = co.ps(es, "p2_pS", [128, 512], F32)
    ptp = co.ps(es, "p2_pt", [128, 1024], BF16)

    guard = {}

    def mmg(bank, mms, extra_wait=None, inc=True):
        w = [guard[id(bank)]] if id(bank) in guard else []
        if extra_wait:
            w += extra_wait
        for j, f in enumerate(mms):
            ph.emit("tensor", f, wait=(w if j == 0 else None),
                    inc=([(tv, 1)] if (inc and j == len(mms) - 1) else None))
        return tv.n

    def ev(eng, f, bank=None, wait=None, sem=None):
        sem = sem or (vt if eng == "vector" else st)
        ph.emit(eng, f, wait=wait, inc=[(sem, 1)])
        if bank is not None:
            guard[id(bank)] = (sem, sem.n)
        return sem.n

    ph.emit("vector", lambda e: e.memset(Sf[:].rearrange("p a b c -> p (a b c)"), 0.0))
    ph.emit("vector", lambda e: e.memset(Sb[:].rearrange("p a b c -> p (a b c)"), 0.0))
    ph.emit("vector", lambda e: e.memset(vf[:, :, 0:64], 0.0))
    for nm in "qkv":
        ph.emit("vector", lambda e, t=halos[nm]: e.memset(
            t[:].rearrange("p a b -> p (a b)"), 0.0))
    ph.emit("vector", lambda e: e.memset(mark[:], 0.0), inc=[(vt, 1)])
    halo_vt = vt.n

    for ti in range(ntile):
        par = ti % 2
        t0 = ti * TILE
        xb = xnt[par]
        sfb = sf[par]
        ph.emit("sync", lambda e, t0=t0, xb=xb: e.dma_start(
            out=xb[:], in_=io["xn_d"][:, :, t0:t0 + TILE].rearrange("a p b -> p a b")),
            inc=[(xnl[par], 16)])
        wxn = [(xnl[par], xnl[par].n), (wld, wld.n)]

        bv = mmg(pA, [(lambda e, kc=kc, xb=xb: e.matmul(
            pA[0:4, 0:TILE], c["wbT"][:, kc, :], xb[:, kc, :],
            start=(kc == 0), stop=(kc == 7))) for kc in range(8)], extra_wait=wxn)
        ev("scalar", lambda e: e.activation(out=bP[:], in_=pA[0:4, 0:TILE],
                                            func=AF.Sigmoid), bank=pA, wait=[(tv, bv)])

        silu_st = [0, 0]
        conv_done = [0, 0]
        for (wT, cwn, hk, wrdst) in ((wq, "qcw", "q", ("qf", 0)),
                                     (wk, "kcw", "k", ("kf", 0)),
                                     (wv, "vcw", "v", ("vf", 64))):
            halo = halos[hk]
            cw = c[cwn]
            dstn, doff = wrdst
            dst = {"qf": qf, "kf": kf, "vf": vf}[dstn]
            for ot in range(8):
                bank = pA if ot % 2 == 0 else pB
                cpar = ot % 2
                gv = mmg(bank, [(lambda e, kc=kc, ot=ot, wT=wT, bank=bank, xb=xb: e.matmul(
                    bank[:, 0:TILE], wT[:, kc, ot * 128:(ot + 1) * 128], xb[:, kc, :],
                    start=(kc == 0), stop=(kc == 7))) for kc in range(8)],
                    extra_wait=wxn)
                w_pre = [(vt, conv_done[cpar] or halo_vt)]
                ev("scalar", lambda e, ot=ot, bank=bank, cpar=cpar: e.copy(
                    pre[:, cpar, 3:3 + TILE], bank[:, 0:TILE]), bank=bank,
                    wait=[(tv, gv)] + w_pre)
                ph.emit("scalar", lambda e, ot=ot, halo=halo, cpar=cpar: e.copy(
                    pre[:, cpar, 0:3], halo[:, ot, :]), inc=[(st, 1)])
                w_slot = [(st, st.n)] + ([(st, silu_st[cpar])] if silu_st[cpar] else [])
                ph.emit("vector", lambda e, ot=ot, cw=cw, cpar=cpar: e.tensor_scalar(
                    out=cvx[:, cpar, :], in0=pre[:, cpar, 0:TILE], scalar1=cw[:, ot, 0:1],
                    scalar2=None, op0=ALU.mult), wait=w_slot)
                for j in (1, 2, 3):
                    ph.emit("vector", lambda e, ot=ot, j=j, cw=cw, cpar=cpar:
                            e.scalar_tensor_tensor(
                        out=cvx[:, cpar, :], in0=pre[:, cpar, j:j + TILE],
                        scalar=cw[:, ot, j:j + 1],
                        in1=cvx[:, cpar, :], op0=ALU.mult, op1=ALU.add))
                ph.emit("vector", lambda e, ot=ot, halo=halo, cpar=cpar: e.tensor_copy(
                    out=halo[:, ot, :], in_=pre[:, cpar, TILE:TILE + 3]), inc=[(vt, 1)])
                conv_done[cpar] = vt.n
                ph.emit("scalar", lambda e, ot=ot, dst=dst, doff=doff, cpar=cpar:
                        e.activation(
                    out=dst[:, ot, doff:doff + TILE], in_=cvx[:, cpar, :], func=AF.Silu),
                    wait=[(vt, vt.n)], inc=[(st, 1)])
                silu_st[cpar] = st.n
        qkv_st = st.n
        halo_vt = vt.n

        ph.emit("scalar", lambda e: e.activation(out=qsq.rearrange("p a b -> p (a b)"),
                                                 in_=qf[:].rearrange("p a b -> p (a b)"),
                                                 func=AF.Square), inc=[(st, 1)])
        sq_q = st.n
        qmm = []
        for h in range(4):
            for j in range(2):
                idx = len(qmm)
                qmm.append(lambda e, h=h, j=j, first=(idx == 0), last=(idx == 7): e.matmul(
                    pN[0:4, 0:TILE], c["nsel"][:, h, :], qsq[:, 2 * h + j, :],
                    start=first, stop=last))
        nv = mmg(pN, qmm, extra_wait=[(st, sq_q)])
        ev("scalar", lambda e: e.activation(out=qnP[:], in_=pN[0:4, 0:TILE], func=AF.Ln,
                                            bias=m[0:4, MI[1e-6]:MI[1e-6] + 1]),
           wait=[(tv, nv)])
        ph.emit("scalar", lambda e: e.activation(out=qnP[:], in_=qnP[:], func=AF.Exp,
                                                 scale=m[0:4, MI[-0.5]:MI[-0.5] + 1]))
        ph.emit("scalar", lambda e: e.activation(out=qsq.rearrange("p a b -> p (a b)"),
                                                 in_=kf[:].rearrange("p a b -> p (a b)"),
                                                 func=AF.Square), inc=[(st, 1)])
        sq_k = st.n
        kmm = []
        for h in range(4):
            for j in range(2):
                idx = len(kmm)
                kmm.append(lambda e, h=h, j=j, first=(idx == 0), last=(idx == 7): e.matmul(
                    pN[0:4, TILE:2 * TILE], c["nsel"][:, h, :], qsq[:, 2 * h + j, :],
                    start=first, stop=last))
        nv2 = mmg(pN, kmm, extra_wait=[(st, sq_k)])
        ev("scalar", lambda e: e.activation(out=knP[:], in_=pN[0:4, TILE:2 * TILE],
                                            func=AF.Ln,
                                            bias=m[0:4, MI[1e-6]:MI[1e-6] + 1]),
           bank=pN, wait=[(tv, nv2)])
        ph.emit("scalar", lambda e: e.activation(out=knP[:], in_=knP[:], func=AF.Exp,
                                                 scale=m[0:4, MI[-0.5]:MI[-0.5] + 1]),
                inc=[(st, 1)])
        rows_st = st.n
        ph.emit("vector", lambda e: e.tensor_tensor(out=brnP[:], in0=bP[:], in1=knP[:],
                                                    op=ALU.mult),
                wait=[(st, rows_st)], inc=[(vt, 1)])
        rows_vt = vt.n
        colmap = ((qnP, qncol), (knP, kncol), (bP, bcol), (brnP, brncol))
        for ca in range(2):
            for qi, (rowt, colt) in enumerate(colmap):
                cvv = mmg(pN, [lambda e, rowt=rowt, ca=ca, qi=qi: e.transpose(
                    pN[:, 256 + 16 * (4 * ca + qi):256 + 16 * (4 * ca + qi) + 4],
                    rowt[0:4, ca * 128:(ca + 1) * 128], c["id4f"][:])],
                    extra_wait=[(vt, rows_vt), (st, rows_st)])
        cols_tv = tv.n
        for ca in range(2):
            for qi, (rowt, colt) in enumerate(colmap):
                ev("scalar", lambda e, ca=ca, qi=qi, colt=colt: e.copy(
                    colt[:, ca, :],
                    pN[:, 256 + 16 * (4 * ca + qi):256 + 16 * (4 * ca + qi) + 4]),
                    bank=(pN if (ca == 1 and qi == 3) else None),
                    wait=[(tv, cols_tv)])
        cols_st = st.n

        for ca in range(2):
            ktv = mmg(ptp, [(lambda e, kc=kc, ca=ca: e.transpose(
                ptp[:, kc * 128:(kc + 1) * 128], kf[:, kc, ca * 128:(ca + 1) * 128],
                c["id128b"][:])) for kc in range(8)],
                extra_wait=[(st, max(qkv_st, cols_st))])
            for h in range(4):
                ph.emit("vector", lambda e, ca=ca, h=h: e.tensor_scalar(
                    out=kntm[:, ca, h, :],
                    in0=ptp[:, 2 * h * 128:(2 * h + 2) * 128], scalar1=kncol[:, ca, h:h + 1],
                    scalar2=None, op0=ALU.mult), wait=[(tv, ktv)])
                ph.emit("vector", lambda e, ca=ca, h=h: e.tensor_scalar(
                    out=kbtm[:, ca, h, :], in0=ptp[:, 2 * h * 128:(2 * h + 2) * 128],
                    scalar1=kncol[:, ca, h:h + 1], scalar2=bcol[:, ca, h:h + 1],
                    op0=ALU.mult, op1=ALU.mult))
            ph.emit("vector", lambda e: e.memset(mark[:], 0.0), inc=[(vt, 1)])
            guard[id(ptp)] = (vt, vt.n)
            vtv = mmg(ptp, [(lambda e, kc=kc, ca=ca: e.transpose(
                ptp[:, kc * 128:(kc + 1) * 128], vf[:, kc, 64 + ca * 128:64 + (ca + 1) * 128],
                c["id128b"][:])) for kc in range(8)])
            for h in range(4):
                ph.emit("vector", lambda e, ca=ca, h=h: e.tensor_scalar(
                    out=vbtm[:, ca, h, :], in0=ptp[:, 2 * h * 128:(2 * h + 2) * 128],
                    scalar1=bcol[:, ca, h:h + 1], scalar2=None, op0=ALU.mult),
                    wait=[(tv, vtv)])
            ph.emit("vector", lambda e, ca=ca: e.tensor_copy(
                out=vtmT[:, ca, :, :].rearrange("p a b -> p (a b)"), in_=ptp[:]),
                inc=[(vt, 1)])
            guard[id(ptp)] = (vt, vt.n)

            for h in range(4):
                gv1 = mmg(pN, [(lambda e, j=j, ca=ca, h=h: e.matmul(
                    pN[:, 0:128], kf[:, 2 * h + j, ca * 128:(ca + 1) * 128],
                    kf[:, 2 * h + j, ca * 128:(ca + 1) * 128],
                    start=(j == 0), stop=(j == 1))) for j in range(2)], inc=False)
                gv2 = mmg(pN, [(lambda e, j=j, ca=ca, h=h: e.matmul(
                    pN[:, 128:256], kf[:, 2 * h + j, ca * 128:(ca + 1) * 128],
                    qf[:, 2 * h + j, ca * 128:(ca + 1) * 128],
                    start=(j == 0), stop=(j == 1))) for j in range(2)], inc=False)
                gv3 = mmg(pN, [lambda e, ca=ca, h=h: e.matmul(
                    pN[:, 256:384], c["sel4"][:, h, :], knP[0:4, ca * 128:(ca + 1) * 128],
                    start=True, stop=True)], inc=False)
                gv4 = mmg(pN, [lambda e, ca=ca, h=h: e.matmul(
                    pN[:, 384:512], c["sel4"][:, h, :], brnP[0:4, ca * 128:(ca + 1) * 128],
                    start=True, stop=True)])
                gv = tv.n
                ph.emit("vector", lambda e, ca=ca, h=h: e.scalar_tensor_tensor(
                    out=attnb[:], in0=pN[:, 128:256], scalar=kncol[:, ca, h:h + 1],
                    in1=c["mUI"][:], op0=ALU.mult, op1=ALU.mult), wait=[(tv, gv)])
                ph.emit("vector", lambda e, ca=ca, h=h: e.scalar_tensor_tensor(
                    out=nt1[:], in0=pN[:, 0:128], scalar=brncol[:, ca, h:h + 1],
                    in1=c["mLSn"][:], op0=ALU.mult, op1=ALU.mult))
                ph.emit("vector", lambda e: e.tensor_tensor(
                    out=Nb[:], in0=nt1[:], in1=pN[:, 256:384], op=ALU.mult))
                ph.emit("vector", lambda e, ca=ca, h=h: e.scalar_tensor_tensor(
                    out=nt1[:], in0=pN[:, 0:128], scalar=kncol[:, ca, h:h + 1],
                    in1=c["mUSn"][:], op0=ALU.mult, op1=ALU.mult))
                ph.emit("vector", lambda e: e.tensor_tensor(
                    out=Npb[:], in0=nt1[:], in1=pN[:, 384:512], op=ALU.mult))
                ph.emit("vector", lambda e: e.tensor_tensor(
                    out=TPf[:], in0=Npb[:], in1=c["id128f"][:], op=ALU.add),
                        inc=[(vt, 1)])
                guard[id(pN)] = (vt, vt.n)
                nb_vt = vt.n
                curM = lambda: Nb[:]
                curMp = lambda: Npb[:]
                for lev in range(6):
                    sq_tv = mmg(pT, [
                        lambda e, curM=curM, curMp=curMp: e.matmul(
                            pT[:, 0:128], curMp(), curM(), start=True, stop=True),
                        lambda e, curM=curM, curMp=curMp: e.matmul(
                            pT[:, 128:256], curM(), curMp(), start=True, stop=True)],
                        extra_wait=[(vt, nb_vt)])
                    ph.emit("vector", lambda e: e.tensor_copy(
                        out=Msl[:], in_=pT[:, 0:128]), wait=[(tv, sq_tv)])
                    ph.emit("vector", lambda e: e.tensor_copy(
                        out=Mpsl[:], in_=pT[:, 128:256]), inc=[(vt, 1)])
                    mev = vt.n
                    acc_tv = mmg(pT, [lambda e: e.matmul(
                        pT[:, 256:384], Msl[:], TPf[:], start=True, stop=True)],
                        extra_wait=[(vt, mev)])
                    ph.emit("vector", lambda e: e.tensor_tensor(
                        out=TPf[:], in0=TPf[:], in1=pT[:, 256:384], op=ALU.add),
                        inc=[(vt, 1)], wait=[(tv, acc_tv)])
                    guard[id(pT)] = (vt, vt.n)
                    nb_vt = vt.n
                    curM = lambda: Msl[:]
                    curMp = lambda: Mpsl[:]
                ph.emit("vector", lambda e: e.tensor_copy(out=TPb[:], in_=TPf[:]),
                        inc=[(vt, 1)])
                tb_vt = vt.n
                u_tv = mmg(pU, [lambda e, ca=ca, h=h: e.matmul(
                    pU[:, 0:256], TPb[:], vbtm[:, ca, h, :], start=True, stop=True)],
                    extra_wait=[(vt, tb_vt)], inc=False)
                w_tv = mmg(pW, [
                    lambda e, ca=ca, h=h: e.matmul(
                        pW[:, 0:128], kbtm[:, ca, h, 0:128], TPb[:], start=True, stop=True),
                    lambda e, ca=ca, h=h: e.matmul(
                        pW[:, 128:256], kbtm[:, ca, h, 128:256], TPb[:],
                        start=True, stop=True)])
                ph.emit("vector", lambda e: e.tensor_copy(out=usb[:], in_=pU[:, 0:256]),
                        wait=[(tv, w_tv)])
                ph.emit("vector", lambda e: e.tensor_copy(
                    out=Wfm[:].rearrange("p a b -> p (a b)"), in_=pW[:, 0:256]),
                    inc=[(vt, 1)])
                guard[id(pU)] = (vt, vt.n)
                guard[id(pW)] = (vt, vt.n)
                wev = vt.n
                ws_tv = mmg(pW, [(lambda e, kc=kc, h=h: e.matmul(
                    pW[:, 256:512], Wfm[:, kc, :], Sb[:, kc, h, :],
                    start=(kc == 0), stop=(kc == 1))) for kc in range(2)],
                    extra_wait=[(vt, wev)])
                ph.emit("vector", lambda e: e.tensor_tensor(
                    out=ub[:], in0=usb[:], in1=pW[:, 256:512], op=ALU.subtract),
                    wait=[(tv, ws_tv)], inc=[(vt, 1)])
                guard[id(pW)] = (vt, vt.n)
                ub_vt = vt.n
                o_tv = mmg(pU, [
                    lambda e, ca=ca, h=h: e.matmul(
                        pU[:, 256:512], qf[:, 2 * h, ca * 128:(ca + 1) * 128],
                        Sb[:, 0, h, :], start=True, stop=False),
                    lambda e, ca=ca, h=h: e.matmul(
                        pU[:, 256:512], qf[:, 2 * h + 1, ca * 128:(ca + 1) * 128],
                        Sb[:, 1, h, :], start=False, stop=False),
                    lambda e: e.matmul(
                        pU[:, 256:512], attnb[:], ub[:], start=False, stop=True)],
                    extra_wait=[(vt, ub_vt)])
                ph.emit("vector", lambda e, ca=ca, h=h: e.tensor_scalar(
                    out=dtm[:, ca, 2 * h:2 * h + 2, :].rearrange("p a b -> p (a b)"),
                    in0=pU[:, 256:512], scalar1=qncol[:, ca, h:h + 1], scalar2=None,
                    op0=ALU.mult), wait=[(tv, o_tv)], inc=[(vt, 1)])
                guard[id(pU)] = (vt, vt.n)
                s_tv = mmg(pS, [
                    lambda e, ca=ca, h=h: e.matmul(
                        pS[:, 0:256], kntm[:, ca, h, 0:128], ub[:], start=True, stop=True),
                    lambda e, ca=ca, h=h: e.matmul(
                        pS[:, 256:512], kntm[:, ca, h, 128:256], ub[:],
                        start=True, stop=True)], extra_wait=[(vt, ub_vt)])
                for kc in range(2):
                    ph.emit("vector", lambda e, kc=kc, h=h: e.tensor_tensor(
                        out=Sf[:, kc, h, :], in0=Sf[:, kc, h, :],
                        in1=pS[:, kc * 256:(kc + 1) * 256], op=ALU.add),
                        wait=[(tv, s_tv)])
                for kc in range(2):
                    ph.emit("vector", lambda e, kc=kc, h=h: e.tensor_copy(
                        out=Sb[:, kc, h, :], in_=Sf[:, kc, h, :]),
                        inc=([(vt, 1)] if kc == 1 else None))
                guard[id(pS)] = (vt, vt.n)

        fir_wait = [(st, qkv_st), (tv, tv.n)]
        for kc in range(8):
            ph.emit("vector", lambda e, kc=kc: e.tensor_scalar(
                out=SF_[:, kc, :], in0=vf[:, kc, 62:62 + TILE],
                scalar1=c["fsw"][:, kc, 0:1], scalar2=None, op0=ALU.mult),
                wait=fir_wait)
            for j in (1, 2):
                ph.emit("vector", lambda e, kc=kc, j=j: e.scalar_tensor_tensor(
                    out=SF_[:, kc, :], in0=vf[:, kc, 62 + j:62 + j + TILE],
                    scalar=c["fsw"][:, kc, j:j + 1], in1=SF_[:, kc, :],
                    op0=ALU.mult, op1=ALU.add))
            ph.emit("vector", lambda e, kc=kc: e.tensor_scalar(
                out=LF_[:, kc, :], in0=vf[:, kc, 2:2 + TILE],
                scalar1=c["flw"][:, kc, 0:1], scalar2=None, op0=ALU.mult))
            for j in range(1, 63):
                ph.emit("vector", lambda e, kc=kc, j=j: e.scalar_tensor_tensor(
                    out=LF_[:, kc, :], in0=vf[:, kc, 2 + j:2 + j + TILE],
                    scalar=c["flw"][:, kc, j:j + 1], in1=LF_[:, kc, :],
                    op0=ALU.mult, op1=ALU.add))
        ph.emit("vector", lambda e: e.tensor_copy(
            out=vf[:, :, 0:64], in_=vf[:, :, TILE:TILE + 64]), inc=[(vt, 1)])
        fir_vt = vt.n

        for ca in range(2):
            for (src, dstt) in ((SF_, stmT), (LF_, ltmT)):
                ptv = mmg(ptp, [(lambda e, kc=kc, ca=ca, src=src: e.transpose(
                    ptp[:, kc * 128:(kc + 1) * 128], src[:, kc, ca * 128:(ca + 1) * 128],
                    c["id128b"][:])) for kc in range(8)],
                    extra_wait=[(vt, fir_vt)])
                ev("vector", lambda e, ca=ca, dstt=dstt: e.tensor_copy(
                    out=dstt[:, ca, :, :].rearrange("p a b -> p (a b)"), in_=ptp[:]),
                    bank=ptp, wait=[(tv, ptv)])
        paths_vt = vt.n

        for ca in range(2):
            for pi, patht in enumerate((stmT, ltmT, dtm, vtmT)):
                for h in range(4):
                    col = pi * 4 + h
                    ph.emit("scalar", lambda e, ca=ca, h=h, patht=patht, col=col:
                            e.activation(
                        out=sjnk[:], in_=patht[:, ca, 2 * h:2 * h + 2, :]
                        .rearrange("p a b -> p (a b)"), func=AF.Identity,
                        accum_out=ssum[:, ca, col:col + 1]), wait=[(vt, paths_vt)])
                    ph.emit("scalar", lambda e, ca=ca, h=h, patht=patht, col=col:
                            e.activation(
                        out=sjnk[:], in_=patht[:, ca, 2 * h:2 * h + 2, :]
                        .rearrange("p a b -> p (a b)"), func=AF.Square,
                        accum_out=ssq2[:, ca, col:col + 1]))
            ph.emit("scalar", lambda e, ca=ca: e.activation(
                out=sjnk[:, 0:16], in_=ssum[:, ca, :], func=AF.Identity))
            ph.emit("scalar", lambda e, ca=ca: e.activation(
                out=sjnk[:, 0:16], in_=ssq2[:, ca, :], func=AF.Identity),
                inc=[(st, 1)])
            ph.emit("vector", lambda e, ca=ca: e.tensor_scalar(
                out=stats[:, ca, 0:32:2], in0=ssum[:, ca, :],
                scalar1=m[:, MI[1.0 / DK]:MI[1.0 / DK] + 1], scalar2=None,
                op0=ALU.mult), wait=[(st, st.n)], inc=[(vt, 1)])
            ph.emit("scalar", lambda e, ca=ca: e.activation(
                out=msq[:, ca, :], in_=stats[:, ca, 0:32:2], func=AF.Square),
                wait=[(vt, vt.n)], inc=[(st, 1)])
            ph.emit("vector", lambda e, ca=ca: e.scalar_tensor_tensor(
                out=stats[:, ca, 1:32:2], in0=ssq2[:, ca, :],
                scalar=m[:, MI[1.0 / DK]:MI[1.0 / DK] + 1], in1=msq[:, ca, :],
                op0=ALU.mult, op1=ALU.subtract), wait=[(st, st.n)])
        co.vpad(ph, 1)
        ph.emit("vector", lambda e: e.memset(mark[:], 0.0), inc=[(vt, 1)])
        stats_vt = vt.n
        statsb2 = statsb
        for ca in range(2):
            prev_tv = tv.n
            ph.emit("vector", lambda e, ca=ca: e.tensor_copy(
                out=statsb2[:], in_=stats[:, ca, :]),
                wait=[(vt, stats_vt), (tv, prev_tv)], inc=[(vt, 1)])
            stv = mmg(ptp, [lambda e: e.transpose(ptp[0:32, 0:128], statsb2[:],
                                                  c["id128b"][:])],
                      extra_wait=[(vt, vt.n)])
            ev("vector", lambda e, ca=ca: e.tensor_copy(
                out=statsF[0:32, ca, :], in_=ptp[0:32, 0:128]), bank=ptp,
                wait=[(tv, stv)])
        stats_f_vt = vt.n

        hmid = firfm[:].rearrange("p f a b -> p (f a) b")
        for mt in range(16):
            bank = pA if mt % 2 == 0 else pB
            mms = [(lambda e, kc=kc, mt=mt, bank=bank, xb=xb: e.matmul(
                bank[:, 0:TILE], g1[:, kc, mt * 128:(mt + 1) * 128], xb[:, kc, :],
                start=(kc == 0), stop=False)) for kc in range(8)]
            mms.append(lambda e, mt=mt, bank=bank: e.matmul(
                bank[:, 0:TILE], g1s[:, mt * 128:(mt + 1) * 128],
                statsF[:].rearrange("p a b -> p (a b)"), start=False, stop=True))
            gv = mmg(bank, mms, extra_wait=[(vt, stats_f_vt)])
            ev("scalar", lambda e, mt=mt, bank=bank: e.activation(
                out=hmid[:, mt, :], in_=bank[:, 0:TILE], func=AF.Gelu,
                bias=c["gb1c"][:, mt:mt + 1]), bank=bank, wait=[(tv, gv)])
        hmid_st = st.n
        lmm = [(lambda e, kc=kc: e.matmul(
            pN[0:16, 0:TILE], c["g2T"][:, kc, :], hmid[:, kc, :],
            start=(kc == 0), stop=(kc == 15))) for kc in range(16)]
        lg_tv = mmg(pN, lmm, extra_wait=[(st, hmid_st)])
        ev("vector", lambda e: e.tensor_copy(out=lgF[:], in_=pN[0:16, 0:TILE]),
           bank=pN, wait=[(tv, lg_tv)])
        lg_vt = vt.n
        for ca in range(2):
            lt_tv = mmg(pU, [lambda e, ca=ca: e.transpose(
                pU[:, 0:16], lgF[0:16, ca * 128:(ca + 1) * 128], c["id16f"][:])],
                extra_wait=[(vt, lg_vt)])
            ph.emit("vector", lambda e: e.tensor_tensor(
                out=lgt[:], in0=pU[:, 0:16], in1=c["bbb"][:], op=ALU.add),
                wait=[(tv, lt_tv)], inc=[(vt, 1)])
            guard[id(pU)] = (vt, vt.n)
            co.vpad(ph, 1)
            ph.emit("vector", lambda e: e.tensor_tensor(
                out=lgt[:], in0=lgt[:], in1=c["itb"][:], op=ALU.mult))
            co.vpad(ph, 1)
            for h in range(4):
                ph.emit("vector", lambda e, h=h: e.reduce_max(
                    out=mx4[:, h:h + 1], in_=lgt[:, 4 * h:4 * h + 4],
                    axis=mybir.AxisListType.X))
            co.vpad(ph, 4)
            for h in range(4):
                ph.emit("vector", lambda e, h=h: e.tensor_scalar(
                    out=exv[:, 4 * h:4 * h + 4], in0=lgt[:, 4 * h:4 * h + 4],
                    scalar1=mx4[:, h:h + 1], scalar2=None, op0=ALU.subtract),
                    inc=([(vt, 1)] if h == 3 else None))
            ph.emit("scalar", lambda e: e.activation(out=exv[:], in_=exv[:], func=AF.Exp),
                    wait=[(vt, vt.n)], inc=[(st, 1)])
            ph.wait("vector", st, st.n)
            for h in range(4):
                ph.emit("vector", lambda e, h=h: e.reduce_sum(
                    out=sm4[:, h:h + 1], in_=exv[:, 4 * h:4 * h + 4],
                    axis=mybir.AxisListType.X), inc=([(vt, 1)] if h == 3 else None))
            ph.emit("scalar", lambda e: e.activation(out=rsm4[:], in_=sm4[:],
                                                     func=AF.Ln), wait=[(vt, vt.n)])
            ph.emit("scalar", lambda e: e.activation(
                out=rsm4[:], in_=rsm4[:], func=AF.Exp,
                scale=m[:, MI[-1.0]:MI[-1.0] + 1]), inc=[(st, 1)])
            ph.wait("vector", st, st.n)
            for h in range(4):
                ph.emit("vector", lambda e, h=h, ca=ca: e.scalar_tensor_tensor(
                    out=prb[:, ca, 4 * h:4 * h + 4], in0=exv[:, 4 * h:4 * h + 4],
                    scalar=rsm4[:, h:h + 1], in1=c["flb"][:, 4 * h:4 * h + 4],
                    op0=ALU.mult, op1=ALU.max))
            co.vpad(ph, 1)
            for h in range(4):
                ph.emit("vector", lambda e, h=h, ca=ca: e.reduce_sum(
                    out=sm4[:, h:h + 1], in_=prb[:, ca, 4 * h:4 * h + 4],
                    axis=mybir.AxisListType.X), inc=([(vt, 1)] if h == 3 else None))
            ph.emit("scalar", lambda e: e.activation(out=rsm4[:], in_=sm4[:],
                                                     func=AF.Ln), wait=[(vt, vt.n)])
            ph.emit("scalar", lambda e: e.activation(
                out=rsm4[:], in_=rsm4[:], func=AF.Exp,
                scale=m[:, MI[-1.0]:MI[-1.0] + 1]), inc=[(st, 1)])
            ph.wait("vector", st, st.n)
            co.vpad(ph, 1)
            for h in range(4):
                ph.emit("vector", lambda e, h=h, ca=ca: e.tensor_scalar(
                    out=prb[:, ca, 4 * h:4 * h + 4], in0=prb[:, ca, 4 * h:4 * h + 4],
                    scalar1=rsm4[:, h:h + 1], scalar2=None, op0=ALU.mult))
            co.vpad(ph, 1)
            for h in range(4):
                for pi, patht in enumerate((stmT, ltmT, dtm, vtmT)):
                    if pi == 0:
                        ph.emit("vector", lambda e, h=h, ca=ca, patht=patht: e.tensor_scalar(
                            out=cmb[:, 2 * h:2 * h + 2, :].rearrange("p a b -> p (a b)"),
                            in0=patht[:, ca, 2 * h:2 * h + 2, :].rearrange("p a b -> p (a b)"),
                            scalar1=prb[:, ca, 4 * h:4 * h + 1], scalar2=None,
                            op0=ALU.mult))
                    else:
                        ph.emit("vector", lambda e, h=h, ca=ca, pi=pi, patht=patht:
                                e.scalar_tensor_tensor(
                            out=cmb[:, 2 * h:2 * h + 2, :].rearrange("p a b -> p (a b)"),
                            in0=patht[:, ca, 2 * h:2 * h + 2, :].rearrange("p a b -> p (a b)"),
                            scalar=prb[:, ca, 4 * h + pi:4 * h + pi + 1],
                            in1=cmb[:, 2 * h:2 * h + 2, :].rearrange("p a b -> p (a b)"),
                            op0=ALU.mult, op1=ALU.add))
            ph.emit("vector", lambda e: e.memset(mark[:], 0.0), inc=[(vt, 1)])
            cmb_vt = vt.n
            for h in range(4):
                ph.emit("scalar", lambda e, h=h: e.activation(
                    out=sjnk[:], in_=cmb[:, 2 * h:2 * h + 2, :].rearrange("p a b -> p (a b)"),
                    func=AF.Square, accum_out=rms4[:, h:h + 1]), wait=[(vt, cmb_vt)])
            ph.emit("scalar", lambda e: e.activation(
                out=sjnk[:, 0:4], in_=rms4[:], func=AF.Identity))
            ph.emit("scalar", lambda e: e.activation(
                out=rr4[:], in_=rms4[:], func=AF.Ln,
                scale=m[:, MI[1.0 / DK]:MI[1.0 / DK] + 1],
                bias=m[:, MI[1e-5]:MI[1e-5] + 1]))
            ph.emit("scalar", lambda e: e.activation(
                out=rr4[:], in_=rr4[:], func=AF.Exp,
                scale=m[:, MI[-0.5]:MI[-0.5] + 1]), inc=[(st, 1)])
            for h in range(4):
                ph.emit("vector", lambda e, h=h: e.scalar_tensor_tensor(
                    out=cmb2[:, 2 * h:2 * h + 2, :].rearrange("p a b -> p (a b)"),
                    in0=cmb[:, 2 * h:2 * h + 2, :].rearrange("p a b -> p (a b)"),
                    scalar=rr4[:, h:h + 1],
                    in1=c["onb"][:], op0=ALU.mult, op1=ALU.mult),
                    wait=[(st, st.n)],
                    inc=([(vt, 1)] if h == 3 else None))
            cmb2_vt = vt.n
            ctv = mmg(ptp, [(lambda e, kc=kc: e.transpose(
                ptp[:, kc * 128:(kc + 1) * 128], cmb2[:, kc, :], c["id128b"][:]))
                for kc in range(8)], extra_wait=[(vt, cmb2_vt)])
            ev("vector", lambda e, ca=ca: e.tensor_copy(
                out=cfm[:, :, ca, :],
                in_=ptp[:].rearrange("p (a b) -> p a b", a=8)),
                bank=ptp, wait=[(tv, ctv)])
        cfm_vt = vt.n
        for ot in range(8):
            bank = pA if ot % 2 == 0 else pB
            gv = mmg(bank, [(lambda e, kc=kc, ot=ot, bank=bank: e.matmul(
                bank[:, 0:TILE], wosb[:, kc, ot * 128:(ot + 1) * 128],
                cfm[:, kc, :, :].rearrange("p a b -> p (a b)"),
                start=(kc == 0), stop=(kc == 7))) for kc in range(8)],
                extra_wait=[(vt, cfm_vt)])
            ev("vector", lambda e, ot=ot, bank=bank, sfb=sfb: e.tensor_copy(
                out=sfb[:, ot, :], in_=bank[:, 0:TILE]), bank=bank,
                wait=[(tv, gv), (swb[par], swb[par].n)])
        ph.emit("sync", lambda e, t0=t0, sfb=sfb: e.dma_start(
            out=io["s_d"][:, :, t0:t0 + TILE].rearrange("a p b -> p a b"), in_=sfb[:]),
            wait=[(vt, vt.n)], inc=[(swb[par], 16)])
    for s in swb:
        ph.wait("sync", s, s.n)
    return ph




def build_p3(co, io, c, es):
    L = co.L
    ph = Phase("p3")
    if DEBUG_SYNC:
        ph.auto = {e_: co.sem("p3_a" + e_) for e_ in Phase.ENGINES}
    m = c["misc"]
    TM = min(1024, L)
    nmega = L // TM
    NT = TM // 512
    NSUB = TM // 128

    tv = co.sem("p3tv")
    vt = co.sem("p3vt")
    st = co.sem("p3st")
    wl = [co.sem("p3wl0"), co.sem("p3wl1")]
    nrl = co.sem("p3nrl")
    ssl = co.sem("p3ssl")
    xl = [co.sem("p3xl0"), co.sem("p3xl1")]
    owb = [co.sem("p3ow0"), co.sem("p3ow1")]

    nrm = co.sb(es, "p3_nrm", [128, 8, TM], BF16)
    w1b = [co.sb(es, f"p3_w1{i}", [128, 8, 1024], BF16) for i in range(2)]
    w2b = [co.sb(es, f"p3_w2{i}", [128, 8, 1024], BF16) for i in range(2)]
    hq = co.sb(es, "p3_hq", [128, 8, TM], BF16)
    hr = co.sb(es, "p3_hr", [128, 2, 512], BF16)
    racc = co.sb(es, "p3_racc", [128, 8, TM], F32)
    cbc = co.sb(es, "p3_cbc", [128, TM], BF16)
    comb8 = co.sb(es, "p3_c8", [8, NSUB, 128], BF16)
    probs = co.sb(es, "p3_probs", [128, 8], F32)
    ge88 = co.sb(es, "p3_ge", [128, 8, 8], F32)
    cnt8 = co.sb(es, "p3_cnt", [128, 8], F32)
    den = co.sb(es, "p3_den", [128, 1], F32)
    rden = co.sb(es, "p3_rden", [128, 1], F32)
    cmbb = co.sb(es, "p3_cmbb", [128, 8], BF16)
    dbg3p = co.sb(es, "p3_d3p", [128, 4, 8, 32], F32) if DEBUG_P3 else None
    gfm = co.sb(es, "p3_gfm", [128, 8, TM], BF16)
    sslab = co.sb(es, "p3_ss", [128, 8, TM], BF16)
    xt3 = [co.sb(es, f"p3_x{i}", [128, 1024], F32) for i in range(2)]
    zt = co.sb(es, "p3_zt", [128, 2, 128], BF16)
    ot3 = [co.sb(es, f"p3_o{i}", [128, 1024], F32) for i in range(2)]

    pA = co.ps(es, "p3_pA", [128, 512], F32)
    pB = co.ps(es, "p3_pB", [128, 512], F32)
    pC = co.ps(es, "p3_pC", [128, 512], F32)
    pD = co.ps(es, "p3_pD", [128, 512], F32)
    pR = co.ps(es, "p3_pR", [128, 512], F32)
    ptp = co.ps(es, "p3_pt", [128, 1024], BF16)

    guard = {}

    def mmg(bank, mms, extra_wait=None, inc=True):
        w = [guard[id(bank)]] if id(bank) in guard else []
        if extra_wait:
            w += extra_wait
        if PARANOID:
            w = w + [(vt, vt.n), (st, st.n)]
        for j, f in enumerate(mms):
            ph.emit("tensor", f, wait=(w if j == 0 else None),
                    inc=([(tv, 1)] if (inc and j == len(mms) - 1) else None))
        return tv.n

    def ev(eng, f, bank=None, wait=None):
        sem = vt if eng == "vector" else st
        if PARANOID:
            wait = (wait or []) + [(tv, tv.n), (vt if eng != "vector" else st,
                                                (vt if eng != "vector" else st).n)]
        ph.emit(eng, f, wait=wait, inc=[(sem, 1)])
        if bank is not None:
            guard[id(bank)] = (sem, sem.n)
        return sem.n

    wfree = [0, 0]
    def wload(e, q, par):
        if e == 8:
            src1 = io["sw1"][:, q * 1024:(q + 1) * 1024]
            src2 = io["sw2"][q * 1024:(q + 1) * 1024, :]
        else:
            src1 = io["ew1"][e, :, q * 1024:(q + 1) * 1024]
            src2 = io["ew2"][e, q * 1024:(q + 1) * 1024, :]
        w = [(tv, wfree[par])] if wfree[par] else None
        if PARANOID:
            w = [(tv, tv.n), (vt, vt.n), (st, st.n)]
        ph.emit("gpsimd", lambda e_, s=src1, par=par: e_.dma_start(
            out=w1b[par][:], in_=s.rearrange("(kc p) mm -> p kc mm", p=128)),
            wait=w, inc=[(wl[par], 16)])
        ph.emit("gpsimd", lambda e_, s=src2, par=par: e_.dma_start(
            out=w2b[par][:], in_=s.rearrange("(kc p) mm -> p kc mm", p=128)),
            inc=[(wl[par], 16)])

    first_racc = True
    for mi in range(nmega):
        t0 = mi * TM
        ph.emit("sync", lambda e, t0=t0: e.dma_start(
            out=nrm[:], in_=io["nr_d"][:, :, t0:t0 + TM].rearrange("a p b -> p a b")),
            wait=[(vt, vt.n), (st, st.n), (tv, tv.n)], inc=[(nrl, 16)])
        ph.emit("sync", lambda e, t0=t0: e.dma_start(
            out=sslab[:], in_=io["s_d"][:, :, t0:t0 + TM].rearrange("a p b -> p a b")),
            inc=[(ssl, 16)])
        wn = [(nrl, nrl.n)]
        NEXP = len(P3_EXPERTS)
        if mi == 0:
            wload(P3_EXPERTS[0], 0, 0)
        for sub in range(NSUB):
            rtv = mmg(pR, [(lambda e, kc=kc, sub=sub: e.matmul(
                pR[:, 0:8], nrm[:, kc, sub * 128:(sub + 1) * 128], c["rwT"][:, kc, :],
                start=(kc == 0), stop=(kc == 7))) for kc in range(8)], extra_wait=wn)
            ev("scalar", lambda e: e.activation(out=probs[:], in_=pR[:, 0:8],
                                                func=AF.Sigmoid), bank=pR,
               wait=[(tv, rtv)])
            sp = st.n
            for e8 in range(8):
                ph.emit("vector", lambda e, e8=e8: e.tensor_scalar(
                    out=ge88[:, e8, :], in0=probs[:], scalar1=probs[:, e8:e8 + 1],
                    scalar2=None, op0=ALU.is_ge), wait=([(st, sp)] if e8 == 0 else None))
            co.vpad(ph, 1)
            ph.emit("vector", lambda e: e.reduce_sum(
                out=cnt8[:].rearrange("p (a b) -> p a b", b=1), in_=ge88[:],
                axis=mybir.AxisListType.X))
            co.vpad(ph, 4)
            ph.emit("vector", lambda e: e.scalar_tensor_tensor(
                out=cnt8[:], in0=cnt8[:], scalar=m[:, MI[2.0]:MI[2.0] + 1],
                in1=probs[:], op0=ALU.is_le, op1=ALU.mult))
            co.vpad(ph, 4)
            ph.emit("vector", lambda e: e.reduce_sum(
                out=den[:], in_=cnt8[:], axis=mybir.AxisListType.X), inc=[(vt, 1)])
            ph.emit("scalar", lambda e: e.activation(
                out=rden[:], in_=den[:], func=AF.Ln,
                bias=m[:, MI[1e-6]:MI[1e-6] + 1]), wait=[(vt, vt.n)])
            ph.emit("scalar", lambda e: e.activation(
                out=rden[:], in_=rden[:], func=AF.Exp,
                scale=m[:, MI[-1.0]:MI[-1.0] + 1]), inc=[(st, 1)])
            ph.emit("vector", lambda e: e.tensor_scalar(
                out=cmbb[:], in0=cnt8[:], scalar1=rden[:, 0:1], scalar2=None,
                op0=ALU.mult), wait=[(st, st.n)], inc=[(vt, 1)])
            if DEBUG_P3:
                d3w = co.sem("p3d3w")
                ph.emit("vector", lambda e, mi=mi, sub=sub: e.tensor_copy(
                    out=dbg3p[:, mi, sub, 0:8], in_=probs[:]))
                ph.emit("vector", lambda e, mi=mi, sub=sub: e.tensor_copy(
                    out=dbg3p[:, mi, sub, 8:16], in_=cnt8[:]))
                ph.emit("vector", lambda e, mi=mi, sub=sub: e.tensor_copy(
                    out=dbg3p[:, mi, sub, 16:24], in_=cmbb[:]))
                ph.emit("vector", lambda e, mi=mi, sub=sub: e.tensor_copy(
                    out=dbg3p[:, mi, sub, 24:25], in_=den[:]))
                ph.emit("vector", lambda e, mi=mi, sub=sub: e.tensor_copy(
                    out=dbg3p[:, mi, sub, 25:26], in_=rden[:]), inc=[(vt, 1)])
            ctv = mmg(ptp, [lambda e: e.transpose(ptp[0:8, 0:128], cmbb[:],
                                                  c["id128b"][:])],
                      extra_wait=[(vt, vt.n)])
            ev("vector", lambda e, sub=sub: e.tensor_copy(
                out=comb8[:, sub, :], in_=ptp[0:8, 0:128]), bank=ptp, wait=[(tv, ctv)])
        comb_vt = vt.n
        if DEBUG_P3:
            ph.emit("sync", lambda e, mi=mi: e.dma_start(
                out=io["dbg3_d"][:, mi, :, :], in_=comb8[:]),
                wait=[(vt, comb_vt)], inc=[(d3w, 16)])
            ph.emit("sync", lambda e, mi=mi: e.dma_start(
                out=io["dbg3p_d"][:, mi, :, :], in_=dbg3p[:, mi, :, :]),
                inc=[(d3w, 16)])
            ph.wait("sync", d3w, d3w.n)

        for ei, e8 in enumerate(P3_EXPERTS):
            for q in range(P3_QUARTERS):
                gq = (mi * NEXP + ei) * 4 + q
                par = gq % 2
                nxt = gq + 1
                nei, nq = divmod(nxt % (NEXP * 4), 4) if nmega > 1 else divmod(nxt, 4)
                if nxt < (mi + 1) * NEXP * 4:
                    ni = nxt - (mi * NEXP) * 4
                    wload(P3_EXPERTS[ni // 4], ni % 4, nxt % 2)
                elif mi + 1 < nmega:
                    wload(P3_EXPERTS[0], 0, nxt % 2)
                if e8 < 8 and q == 0:
                    for nt in range(NT):
                        btv = mmg(pR, [lambda e, e8=e8, nt=nt: e.matmul(
                            pR[:, 0:512], c["sel8"][:, e8, :],
                            comb8[:, 4 * nt:4 * nt + 4, :].rearrange("p a b -> p (a b)"),
                            start=True, stop=True)], extra_wait=[(vt, comb_vt)])
                        ev("vector", lambda e, nt=nt: e.tensor_copy(
                            out=cbc[:, nt * 512:(nt + 1) * 512], in_=pR[:, 0:512]),
                            bank=pR, wait=[(tv, btv)])
                cb_vt = vt.n
                wlw = [(wl[par], wl[par].n)]
                skip = P3_SKIP if (isinstance(P3_SKIP, tuple) and P3_SKIP and isinstance(P3_SKIP[0], str) and q == 1) else ()
                for mt in range(8):
                    for nt in range(NT):
                        bank = pA if (mt * NT + nt) % 2 == 0 else pB
                        gv = mmg(bank, [(lambda e, kc=kc, mt=mt, nt=nt, bank=bank,
                                         par=par: e.matmul(
                            bank[:, 0:512], w1b[par][:, kc, mt * 128:(mt + 1) * 128],
                            nrm[:, kc, nt * 512:(nt + 1) * 512],
                            start=(kc == 0), stop=(kc == 7))) for kc in range(8)],
                            extra_wait=wlw + wn)
                        if "relu" in skip:
                            guard[id(bank)] = (tv, gv)
                        elif e8 == 8:
                            ev("scalar", lambda e, mt=mt, nt=nt, bank=bank, q=q: e.activation(
                                out=hq[:, mt, nt * 512:(nt + 1) * 512], in_=bank[:, 0:512],
                                func=AF.Relu, bias=c["sb1c"][:, q * 8 + mt:q * 8 + mt + 1]),
                                bank=bank, wait=[(tv, gv)])
                        else:
                            hpar = (mt * NT + nt) % 2
                            ev("scalar", lambda e, mt=mt, nt=nt, bank=bank, hpar=hpar,
                               e8=e8, q=q: e.activation(
                                out=hr[:, hpar, :], in_=bank[:, 0:512], func=AF.Relu,
                                bias=c["eb1c"][:, e8, q * 8 + mt:q * 8 + mt + 1]),
                                bank=bank, wait=[(tv, gv), (vt, vt.n)])
                            ph.emit("vector", lambda e, mt=mt, nt=nt, hpar=hpar:
                                    e.tensor_tensor(
                                out=hq[:, mt, nt * 512:(nt + 1) * 512], in0=hr[:, hpar, :],
                                in1=cbc[:, nt * 512:(nt + 1) * 512], op=ALU.mult),
                                wait=[(st, st.n)], inc=[(vt, 1)])
                if e8 == 8:
                    h_ready = [(st, st.n)]
                else:
                    h_ready = [(vt, vt.n)]
                if P3_TRAP is not None and (e8, q) == P3_TRAP:
                    from concourse.bass_interp import add_trap
                    ph.emit("tensor", lambda e: add_trap(e, key="w2start"),
                            wait=h_ready)
                    ph.emit("vector", lambda e: add_trap(e, key="vq0end"))
                for nt in range(NT):
                    if "w2" in skip:
                        break
                    for ft in range(8):
                        if P3_W2MAX is not None and q == 1 and (nt * 8 + ft) >= P3_W2MAX:
                            continue
                        bank = pC if (nt * 8 + ft) % 2 == 0 else pD
                        mms = [(lambda e, kc=kc, ft=ft, nt=nt, bank=bank, par=par:
                                e.matmul(
                            bank[:, 0:512], w2b[par][:, kc, ft * 128:(ft + 1) * 128],
                            hq[:, kc, nt * 512:(nt + 1) * 512],
                            start=(kc == 0), stop=(kc == 7)))
                            for kc in range(8)]
                        gv = mmg(bank, mms, extra_wait=h_ready)
                        if P3_TRAP is not None and (e8, q, nt, ft) == P3_TRAP + (0, 0):
                            from concourse.bass_interp import add_trap
                            ph.emit("tensor", lambda e: add_trap(e, key="w2g0"))
                            trap_after_add = [True]
                        else:
                            trap_after_add = [False]
                        if nt == NT - 1 and ft == 7:
                            wfree[par] = gv
                        if first_racc and e8 == P3_EXPERTS[0] and q == 0:
                            ev("vector", lambda e, ft=ft, nt=nt, bank=bank: e.tensor_copy(
                                out=racc[:, ft, nt * 512:(nt + 1) * 512],
                                in_=bank[:, 0:512]), bank=bank, wait=[(tv, gv)])
                        else:
                            ev("vector", lambda e, ft=ft, nt=nt, bank=bank: e.tensor_tensor(
                                out=racc[:, ft, nt * 512:(nt + 1) * 512],
                                in0=racc[:, ft, nt * 512:(nt + 1) * 512],
                                in1=bank[:, 0:512], op=ALU.add), bank=bank,
                                wait=[(tv, gv)])
                        if trap_after_add[0]:
                            from concourse.bass_interp import add_trap
                            ph.emit("vector", lambda e: add_trap(e, key="postadd"))
                        if e8 == 8 and q == 0 and P3_EB2:
                            btv2 = mmg(pR, [lambda e, ft=ft, nt=nt: e.matmul(
                                pR[:, 0:512], c["eb2"][:, ft * 128:(ft + 1) * 128],
                                comb8[:, 4 * nt:4 * nt + 4, :].rearrange("p a b -> p (a b)"),
                                start=True, stop=True)], extra_wait=[(vt, comb_vt)])
                            ev("vector", lambda e, ft=ft, nt=nt: e.tensor_tensor(
                                out=racc[:, ft, nt * 512:(nt + 1) * 512],
                                in0=racc[:, ft, nt * 512:(nt + 1) * 512],
                                in1=pR[:, 0:512], op=ALU.add), bank=pR,
                                wait=[(tv, btv2)])
                if e8 == P3_EXPERTS[0] and q == 3:
                    first_racc = False
        racc_vt = vt.n
        first_racc = True

        for ft in range(8):
            for nt in range(NT):
                ph.emit("scalar", lambda e, ft=ft, nt=nt: e.activation(
                    out=gfm[:, ft, nt * 512:(nt + 1) * 512],
                    in_=racc[:, ft, nt * 512:(nt + 1) * 512], func=AF.Silu,
                    bias=c["sb2c"][:, ft:ft + 1]), wait=[(vt, racc_vt)],
                    inc=[(st, 1)])
        g_st = st.n
        ztr_done = [0, 0]
        for tt in range(NSUB):
            xpar = tt % 2
            ph.emit("sync", lambda e, t0=t0, tt=tt, xpar=xpar: e.dma_start(
                out=xt3[xpar][:], in_=io["x"][t0 + tt * 128:t0 + (tt + 1) * 128, :]),
                wait=[(owb[xpar], owb[xpar].n)], inc=[(xl[xpar], 16)])
            for ft in range(8):
                zpar = ft % 2
                w_z = [(st, g_st), (ssl, ssl.n), (tv, ztr_done[zpar])]
                ph.emit("vector", lambda e, ft=ft, tt=tt, zpar=zpar: e.tensor_tensor(
                    out=zt[:, zpar, :], in0=gfm[:, ft, tt * 128:(tt + 1) * 128],
                    in1=sslab[:, ft, tt * 128:(tt + 1) * 128], op=ALU.mult),
                    wait=w_z, inc=[(vt, 1)])
                ztv = mmg(ptp, [lambda e, ft=ft, zpar=zpar: e.transpose(
                    ptp[:, ft * 128:(ft + 1) * 128], zt[:, zpar, :], c["id128b"][:])],
                    extra_wait=[(vt, vt.n)], inc=True)
                ztr_done[zpar] = tv.n
            ztv = tv.n
            opar = tt % 2
            for ft in range(8):
                ev("vector", lambda e, ft=ft, opar=opar, xpar=xpar: e.tensor_tensor(
                    out=ot3[opar][:, ft * 128:(ft + 1) * 128],
                    in0=ptp[:, ft * 128:(ft + 1) * 128],
                    in1=xt3[xpar][:, ft * 128:(ft + 1) * 128], op=ALU.add),
                    bank=(ptp if ft == 7 else None),
                    wait=[(tv, ztv), (xl[xpar], xl[xpar].n),
                          (owb[opar], owb[opar].n)])
            ph.emit("sync", lambda e, t0=t0, tt=tt, opar=opar: e.dma_start(
                out=io["out"][t0 + tt * 128:t0 + (tt + 1) * 128, :], in_=ot3[opar][:]),
                wait=[(vt, vt.n)], inc=[(owb[opar], 16)])
    for s_ in owb:
        ph.wait("sync", s_, s_.n)
    ph.wait("vector", owb[0], owb[0].n)
    ph.wait("vector", owb[1], owb[1].n)
    return ph



def build_core_program(L, host, phases=("p1",)):
    nc = bass.Bass("TRN2", target_bir_lowering=False)
    io = declare_io(nc, L, host)
    with contextlib.ExitStack() as ges:
        co = Core(nc, ges, L)
        with contextlib.ExitStack() as es0:
            c, ph0 = build_consts(co, io, es0)
            ph0.run(nc)
        if "p1" in phases:
            with contextlib.ExitStack() as es1:
                build_p1(co, io, c, es1).run(nc)
        if "p2" in phases:
            with contextlib.ExitStack() as es2:
                build_p2(co, io, c, es2).run(nc)
        if "p3" in phases:
            with contextlib.ExitStack() as es3:
                build_p3(co, io, c, es3).run(nc)
        with nc.Block("sclr") as blk0:
            def _clr(g):
                for s_ in co.sems.values():
                    g.sem_clear(s_.h)
            blk0.gpsimd(_clr)
    return nc, io


def kernel(**inputs):
    L = 4096
    host = host_layout(inputs)
    x = np.ascontiguousarray(np.asarray(inputs["L0_attn_hidden_states"], dtype=np.float32))
    nc, io = build_core_program(L, host, phases=("p1", "p2", "p3"))
    in_maps = []
    for cid in range(NCORES):
        im = {"x": np.ascontiguousarray(x[cid])}
        im.update(host)
        in_maps.append(im)
    res = run_bass_kernel_spmd(nc, in_maps, core_ids=list(range(NCORES)))
    return np.stack([r["out"] for r in res.results], axis=0)


# version 04
# speedup vs baseline: 1.0269x; 1.0269x over previous
import contextlib
import numpy as np
import concourse.bass as bass
import concourse.mybir as mybir
from concourse.bass_utils import run_bass_kernel_spmd

F32 = mybir.dt.float32
BF16 = mybir.dt.bfloat16
AF = mybir.ActivationFunctionType
ALU = mybir.AluOpType

D = 1024
H = 4
DK = 256
CHUNK = 32
CDEV = 128
NCORES = 8

P3_EXPERTS = list(range(8)) + [8]
P3_QUARTERS = 4
P3_TRAP = None
P3_EB2 = True
P3_SKIP = ()
P3_W2MAX = None
DEBUG_SCRATCH = False
DEBUG_FEED = False
DEBUG_P3 = False
PARANOID = False
MISC_VALS = [1e-6, 1e-5, 1e-8, -0.5, 1.0 / D, 1.0 / DK, 1e-4, 0.05, 2.0, 1.0,
             0.5, -1.0, 0.0]
MI = {v: i for i, v in enumerate(MISC_VALS)}


class Sem:
    def __init__(self, nc, es, name):
        self.h = es.enter_context(nc.semaphore(name))
        self.n = 0


DEBUG_SYNC = False


class Phase:
    ENGINES = ("sync", "gpsimd", "vector", "scalar", "tensor")

    def __init__(self, name):
        self.name = name
        self.q = {k: [] for k in self.ENGINES}
        self.hwm = {k: {} for k in self.ENGINES}
        self.auto = None
        self.last_done = {}

    def emit(self, eng, f, wait=None, inc=None):
        if DEBUG_SYNC and self.auto and f is not None:
            a = self.auto[eng]
            d = 16 if eng in ("sync", "gpsimd") else 1
            if self.last_done.get(eng):
                wait = list(wait or []) + [self.last_done[eng]]
            if inc:
                s0, d0 = inc[0]
                self.last_done[eng] = (s0, s0.n + d0)
            else:
                inc = [(a, d)]
                self.last_done[eng] = (a, a.n + d)
        q = self.q[eng]
        if wait:
            hw = self.hwm[eng]
            for s, v in wait:
                if v > 0 and v > hw.get(id(s), -1):
                    hw[id(s)] = v
                    q.append(lambda e, s=s, v=v: e.wait_ge(s.h, v))
        if inc:
            for s, d in inc:
                s.n += d
            def g(e, f=f, incs=tuple(inc)):
                ins = f(e)
                for s, d in incs:
                    ins = ins.then_inc(s.h, d)
            q.append(g)
        elif f is not None:
            q.append(f)

    def wait(self, eng, sem, val):
        self.emit(eng, None, wait=[(sem, val)])

    def run(self, nc):
        with nc.Block(self.name) as block:
            for eng in self.ENGINES:
                cmds = self.q[eng]
                if not cmds:
                    continue
                def body(e, cmds=cmds):
                    for cf in cmds:
                        cf(e)
                getattr(block, eng)(body)


def host_layout(inputs):
    g = lambda k: np.ascontiguousarray(np.asarray(inputs[k], dtype=np.float32))
    d = {}
    d["wqT"] = np.ascontiguousarray(g("L0_attn_Wq").T)
    d["wkT"] = np.ascontiguousarray(g("L0_attn_Wk").T)
    d["wvT"] = np.ascontiguousarray(g("L0_attn_Wv").T)
    d["woT"] = np.ascontiguousarray(g("L0_attn_Wo").T)
    g1T = np.ascontiguousarray(g("L0_attn_gate_w1").T)
    d["g1T"] = np.ascontiguousarray(g1T[:D])
    d["g1Ts"] = np.ascontiguousarray(g1T[D:])
    d["g2T"] = np.ascontiguousarray(g("L0_attn_gate_w2").T)
    d["wbT"] = np.ascontiguousarray(g("L0_attn_Wb").T)
    d["rwT"] = np.ascontiguousarray(g("L0_ffn_router_w").T)
    d["sw1"] = g("L0_ffn_sw1")
    d["sw2"] = g("L0_ffn_sw2")
    d["ew1"] = g("L0_ffn_ew1")
    d["ew2"] = g("L0_ffn_ew2")
    d["eb2"] = g("L0_ffn_eb2")
    d["eb1c"] = np.ascontiguousarray(
        g("L0_ffn_eb1").reshape(8, 32, 128).transpose(2, 0, 1))
    d["sb1c"] = np.ascontiguousarray(g("L0_ffn_sb1").reshape(32, 128).T)
    d["sb2c"] = np.ascontiguousarray(g("L0_ffn_sb2").reshape(8, 128).T)
    d["gb1c"] = np.ascontiguousarray(g("L0_attn_gate_b1").reshape(16, 128).T)
    for k, nm in (("qcw", "L0_attn_qconv_w"), ("kcw", "L0_attn_kconv_w"),
                  ("vcw", "L0_attn_vconv_w")):
        d[k] = np.ascontiguousarray(g(nm).reshape(8, 128, 4).transpose(1, 0, 2))
    d["fsw"] = np.ascontiguousarray(
        g("L0_attn_fir_short_w").reshape(8, 128, 3).transpose(1, 0, 2))
    d["flw"] = np.ascontiguousarray(
        g("L0_attn_fir_long_w").reshape(8, 128, 63).transpose(1, 0, 2))
    d["nwb"] = np.ascontiguousarray(
        g("L0_ffn_norm_weight").reshape(8, 128).T)
    d["onb"] = np.ascontiguousarray(
        np.broadcast_to(g("L0_attn_onorm_w")[None, :], (128, DK)))
    d["bbb"] = np.ascontiguousarray(
        np.broadcast_to(g("L0_attn_base_bias").reshape(-1)[None, :], (128, 16)))
    d["lt"] = g("L0_attn_log_temp").reshape(1, 4)
    d["fr"] = g("L0_attn_floor_raw").reshape(1, 16)
    d["misc"] = np.ascontiguousarray(
        np.broadcast_to(np.array(MISC_VALS, dtype=np.float32)[None, :],
                        (128, len(MISC_VALS))))
    d["id128"] = np.eye(128, dtype=np.float32)
    tri = np.triu(np.ones((CDEV, CDEV), dtype=np.float32))
    d["mLSn"] = np.ascontiguousarray(-(1.0 - tri))
    d["mUSn"] = np.ascontiguousarray(-np.triu(tri, 1))
    d["mUI"] = tri.copy()
    d["ones1"] = np.ones((1, 128), dtype=np.float32)
    d["id4"] = np.eye(4, dtype=np.float32)
    d["id16"] = np.eye(16, dtype=np.float32)
    sel8 = np.zeros((8, 8, 128), dtype=np.float32)
    for e in range(8):
        sel8[e, e, :] = 1.0
    d["sel8"] = sel8
    sel4 = np.zeros((4, 4, CDEV), dtype=np.float32)
    for h in range(4):
        sel4[h, h, :] = 1.0
    d["sel4"] = sel4
    nsel = np.zeros((128, 4, 4), dtype=np.float32)
    for h in range(4):
        nsel[:, h, h] = 1.0
    d["nsel"] = nsel
    return d


class Core:
    def __init__(self, nc, es, L):
        self.nc = nc
        self.es = es
        self.L = L
        self.sems = {}
        self.gjk = es.enter_context(nc.sbuf_tensor("gjk", [128, 128], F32))

    def vpad(self, ph, n=4):
        for k in range(n):
            ph.emit("vector", lambda e: e.memset(self.gjk[:, 0:128], 0.0))

    def sem(self, name):
        if name not in self.sems:
            self.sems[name] = Sem(self.nc, self.es, name)
        return self.sems[name]

    def sb(self, es, name, shape, dtype):
        return es.enter_context(self.nc.sbuf_tensor(name, list(shape), dtype))

    def ps(self, es, name, shape, dtype=F32):
        return es.enter_context(self.nc.psum_tensor(name, list(shape), dtype))


def declare_io(nc, L, host):
    t = {}
    t["x"] = nc.dram_tensor("x", [L, D], F32, kind="ExternalInput").ap()
    for k, v in host.items():
        t[k] = nc.dram_tensor(k, list(v.shape), F32, kind="ExternalInput").ap()
    t["out"] = nc.dram_tensor("out", [L, D], F32, kind="ExternalOutput").ap()
    kind = "ExternalOutput" if DEBUG_SCRATCH else "Internal"
    if DEBUG_FEED:
        kind = "ExternalInput"
    if DEBUG_SCRATCH:
        t["dbg_d"] = nc.dram_tensor("dbg_d", [128, 32, 8], F32, kind="ExternalOutput").ap()
    if DEBUG_P3:
        t["dbg3_d"] = nc.dram_tensor("dbg3_d", [8, 4, 8, 128], BF16,
                                     kind="ExternalOutput").ap()
        t["dbg3p_d"] = nc.dram_tensor("dbg3p_d", [128, 4, 8, 32], F32,
                                      kind="ExternalOutput").ap()
    t["xn_d"] = nc.dram_tensor("xn_d", [8, 128, L], BF16, kind=kind).ap()
    t["nr_d"] = nc.dram_tensor("nr_d", [8, 128, L], BF16, kind=kind).ap()
    t["s_d"] = nc.dram_tensor("s_d", [8, 128, L], BF16, kind=kind).ap()
    return t



def build_consts(co, io, es):
    c = {}
    mk = lambda n, shp, dt=F32: c.__setitem__(n, co.sb(co.es, "c_" + n, shp, dt))
    mk("misc", [128, len(MISC_VALS)])
    mk("id128b", [128, 128], BF16)
    mk("id128f", [128, 128])
    mk("mLSn", [CDEV, CDEV]); mk("mUSn", [CDEV, CDEV]); mk("mUI", [CDEV, CDEV])
    mk("id4f", [4, 4]); mk("id16f", [16, 16])
    mk("ones1b", [1, 128], BF16); mk("ones1f", [1, 128])
    mk("sel8", [8, 8, 128], BF16); mk("sel4", [4, 4, CDEV]); mk("nsel", [128, 4, 4], BF16)
    mk("onb", [128, DK]); mk("bbb", [128, 16])
    mk("gb1c", [128, 16]); mk("sb1c", [128, 32]); mk("sb2c", [128, 8])
    mk("eb1c", [128, 8, 32]); mk("eb2", [8, 1024], BF16)
    mk("qcw", [128, 8, 4]); mk("kcw", [128, 8, 4]); mk("vcw", [128, 8, 4])
    mk("fsw", [128, 8, 3]); mk("flw", [128, 8, 63])
    mk("wbT", [128, 8, 4], BF16); mk("rwT", [128, 8, 8], BF16)
    mk("g2T", [128, 16, 16], BF16)
    mk("lt", [1, 4]); mk("fr", [1, 16])
    mk("itb", [128, 16]); mk("flb", [128, 16])
    mk("it16", [1, 16]); mk("fl16", [1, 16]); mk("sp_t", [1, 4])

    ph = Phase("consts")
    if DEBUG_SYNC:
        ph.auto = {e_: co.sem("consts_a" + e_) for e_ in Phase.ENGINES}
    ld = co.sem("c_ld")
    ldg = co.sem("c_ldg")
    cv = co.sem("c_cv")
    cs = co.sem("c_cs")
    mmc = co.sem("c_mmc")
    m = c["misc"]

    plain = ["misc", "id128f", "id4f", "id16f", "mLSn", "mUSn", "mUI", "onb", "bbb",
             "gb1c", "sb1c", "sb2c", "eb1c", "qcw", "kcw", "vcw", "fsw", "flw",
             "sel4", "lt", "fr"]
    for nm in plain:
        srcmap = {"ones1f": "ones1", "id128f": "id128", "id4f": "id4", "id16f": "id16"}
        src = io[srcmap.get(nm, nm)]
        ph.emit("sync", lambda e, o=c[nm], i=src: e.dma_start(out=o[:], in_=i),
                inc=[(ld, 16)])
    ph.emit("sync", lambda e: e.dma_start(out=c["ones1f"][:], in_=io["ones1"]), inc=[(ld, 16)])
    casting = [("id128b", io["id128"]), ("ones1b", io["ones1"]),
               ("sel8", io["sel8"]), ("nsel", io["nsel"]), ("eb2", io["eb2"]),
               ("wbT", io["wbT"].rearrange("(kc p) h -> p kc h", p=128)),
               ("rwT", io["rwT"].rearrange("(kc p) h -> p kc h", p=128)),
               ("g2T", io["g2T"].rearrange("(kc p) h -> p kc h", p=128))]
    for nm, src in casting:
        ph.emit("gpsimd", lambda e, o=c[nm], i=src: e.dma_start(out=o[:], in_=i),
                inc=[(ldg, 16)])
    for fn in (AF.Identity, AF.Relu, AF.Square, AF.Abs, AF.Exp, AF.Ln,
               AF.Sigmoid, AF.Silu, AF.Gelu):
        ph.emit("scalar", lambda e, fn=fn: e.activation(
            out=c["sp_t"][:], in_=c["misc"][0:1, MI[1.0]:MI[1.0] + 4], func=fn),
            wait=[(ld, ld.n)])
    sp = c["sp_t"]
    ph.emit("scalar", lambda e: e.activation(out=sp[:], in_=c["lt"][:], func=AF.Abs),
            wait=[(ld, ld.n), (ldg, ldg.n)])
    ph.emit("scalar", lambda e: e.activation(out=sp[:], in_=sp[:], func=AF.Exp,
                                             scale=m[0:1, MI[-1.0]:MI[-1.0] + 1]))
    ph.emit("scalar", lambda e: e.activation(out=sp[:], in_=sp[:], func=AF.Ln,
                                             bias=m[0:1, MI[1.0]:MI[1.0] + 1]),
            inc=[(cs, 1)])
    ph.emit("vector", lambda e: e.tensor_scalar(
        out=c["it16"][:, 0:4], in0=c["lt"][:], scalar1=m[0:1, MI[0.0]:MI[0.0] + 1],
        scalar2=None, op0=ALU.max), wait=[(cs, cs.n)])
    co.vpad(ph)
    ph.emit("vector", lambda e: e.tensor_tensor(out=c["it16"][:, 0:4],
                                                in0=c["it16"][:, 0:4], in1=sp[:], op=ALU.add))
    co.vpad(ph)
    ph.emit("vector", lambda e: e.tensor_scalar(
        out=c["it16"][:, 0:4], in0=c["it16"][:, 0:4],
        scalar1=m[0:1, MI[1e-4]:MI[1e-4] + 1], scalar2=None, op0=ALU.add))
    co.vpad(ph)
    ph.emit("vector", lambda e: e.reciprocal(out=c["it16"][:, 0:4], in_=c["it16"][:, 0:4]))
    co.vpad(ph)
    for h in range(3, -1, -1):
        for p in range(4):
            if h == 0 and p == 0:
                continue
            last = (h == 0 and p == 3)
            ph.emit("vector", lambda e, h=h, p=p: e.tensor_copy(
                out=c["it16"][:, h * 4 + p:h * 4 + p + 1], in_=c["it16"][:, h:h + 1]),
                inc=([(cv, 1)] if last else None))
    co.vpad(ph)
    ph.emit("scalar", lambda e: e.activation(out=c["fl16"][:], in_=c["fr"][:],
                                             func=AF.Sigmoid), wait=[(ld, ld.n)])
    ph.emit("scalar", lambda e: e.mul(c["fl16"][:], c["fl16"][:],
                                      m[0:1, MI[0.05]:MI[0.05] + 1]), inc=[(cs, 1)])
    pb = co.ps(es, "c_pb", [128, 32], F32)
    ph.emit("tensor", lambda e: e.matmul(pb[:, 0:16], c["ones1f"][:], c["it16"][:],
                                         start=True, stop=True),
            wait=[(cv, cv.n), (cs, cs.n)])
    ph.emit("tensor", lambda e: e.matmul(pb[:, 16:32], c["ones1f"][:], c["fl16"][:],
                                         start=True, stop=True), inc=[(mmc, 1)])
    ph.emit("vector", lambda e: e.tensor_copy(out=c["itb"][:], in_=pb[:, 0:16]),
            wait=[(mmc, mmc.n)])
    ph.emit("vector", lambda e: e.tensor_copy(out=c["flb"][:], in_=pb[:, 16:32]),
            inc=[(cv, 1)])
    ph.wait("sync", cv, cv.n)
    ph.wait("sync", ld, ld.n)
    ph.wait("gpsimd", ldg, ldg.n)
    return c, ph



def build_p1(co, io, c, es):
    L = co.L
    ph = Phase("p1")
    if DEBUG_SYNC:
        ph.auto = {e_: co.sem("p1_a" + e_) for e_ in Phase.ENGINES}
    m = c["misc"]
    nt = L // 128
    nwb = co.sb(es, "p1_nwb", [128, 8], F32)
    nwl = co.sem("p1nwl")
    ph.emit("sync", lambda e: e.dma_start(out=nwb[:], in_=io["nwb"]), inc=[(nwl, 16)])
    ph.wait("vector", nwl, 16)

    xld = [co.sem("p1x0"), co.sem("p1x1")]
    vv = co.sem("p1v")
    ss = co.sem("p1s")
    tt = co.sem("p1t")
    wbx = [co.sem("p1wx0"), co.sem("p1wx1")]
    wbn = [co.sem("p1wn0"), co.sem("p1wn1")]

    xt = [co.sb(es, f"p1_x{i}", [128, D], F32) for i in range(2)]
    jnk = co.sb(es, "p1_jnk", [128, D], F32)
    sumall = co.sb(es, "p1_suma", [128, nt], F32)
    sqall = co.sb(es, "p1_sqa", [128, nt], F32)
    meanall = co.sb(es, "p1_mna", [128, nt], F32)
    varall = co.sb(es, "p1_vra", [128, nt], F32)
    rstdall = co.sb(es, "p1_rsa", [128, nt], F32)
    rrall = co.sb(es, "p1_rra", [128, nt], F32)
    scr = co.sb(es, "p1_scr", [128, nt], F32)
    dbg = co.sb(es, "p1_dbg", [128, nt, 8], F32) if DEBUG_SCRATCH else None
    xn = co.sb(es, "p1_xn", [128, D], F32)
    xnb = co.sb(es, "p1_xnb", [128, D], BF16)
    nrb = co.sb(es, "p1_nrb", [128, D], BF16)
    fmx = [co.sb(es, f"p1_fmx{i}", [128, 8, 128], BF16) for i in range(2)]
    fmn = [co.sb(es, f"p1_fmn{i}", [128, 8, 128], BF16) for i in range(2)]
    pt_ = [co.ps(es, f"p1_pt{i}", [128, 1024], BF16) for i in range(2)]

    acc_done = [0, 0]
    sq_done = [0, 0]
    for i in range(nt):
        par = i % 2
        ph.emit("sync", lambda e, i=i, par=par: e.dma_start(
            out=xt[par][:], in_=io["x"][i * 128:(i + 1) * 128, :]),
            wait=([(vv, acc_done[par])] if acc_done[par] else None),
            inc=[(xld[par], 16)])
        ph.emit("vector", lambda e, i=i, par=par: e.reduce_sum(
            out=sumall[:, i:i + 1], in_=xt[par][:], axis=mybir.AxisListType.X),
            wait=[(xld[par], xld[par].n)])
        ph.emit("scalar", lambda e, i=i, par=par: e.activation(
            out=(jnk if par == 0 else xn)[:], in_=xt[par][:], func=AF.Square),
            wait=[(xld[par], xld[par].n), (vv, sq_done[par])], inc=[(ss, 1)])
        ph.emit("vector", lambda e, i=i, par=par: e.reduce_sum(
            out=sqall[:, i:i + 1], in_=(jnk if par == 0 else xn)[:],
            axis=mybir.AxisListType.X),
            wait=[(ss, ss.n)], inc=[(vv, 1)])
        acc_done[par] = vv.n
        sq_done[par] = vv.n
    if DEBUG_SCRATCH:
        ph.emit("vector", lambda e: e.tensor_copy(out=dbg[:, :, 4], in_=sumall[:]),
                wait=[(ss, ss.n)])
        ph.emit("vector", lambda e: e.tensor_copy(out=dbg[:, :, 5], in_=sqall[:]))
    co.vpad(ph)
    ph.emit("vector", lambda e: e.tensor_scalar(
        out=meanall[:], in0=sumall[:], scalar1=m[:, MI[1.0 / D]:MI[1.0 / D] + 1],
        scalar2=None, op0=ALU.mult), wait=[(ss, ss.n)])
    co.vpad(ph)
    ph.emit("vector", lambda e: e.tensor_tensor(
        out=scr[:], in0=meanall[:], in1=meanall[:], op=ALU.mult))
    co.vpad(ph)
    ph.emit("vector", lambda e: e.scalar_tensor_tensor(
        out=varall[:], in0=sqall[:], scalar=m[:, MI[1.0 / D]:MI[1.0 / D] + 1],
        in1=scr[:], op0=ALU.mult, op1=ALU.subtract), inc=[(vv, 1)])
    ph.emit("scalar", lambda e: e.activation(out=rstdall[:], in_=varall[:], func=AF.Ln,
                                             bias=m[:, MI[1e-5]:MI[1e-5] + 1]),
            wait=[(vv, vv.n)])
    ph.emit("scalar", lambda e: e.activation(out=rstdall[:], in_=rstdall[:], func=AF.Exp,
                                             scale=m[:, MI[-0.5]:MI[-0.5] + 1]),
            inc=[(ss, 1)])
    co.vpad(ph)
    ph.emit("vector", lambda e: e.tensor_tensor(
        out=scr[:], in0=rstdall[:], in1=rstdall[:], op=ALU.mult), wait=[(ss, ss.n)])
    co.vpad(ph)
    ph.emit("vector", lambda e: e.tensor_tensor(
        out=scr[:], in0=scr[:], in1=varall[:], op=ALU.mult), inc=[(vv, 1)])
    co.vpad(ph)
    ph.emit("scalar", lambda e: e.activation(out=rrall[:], in_=scr[:], func=AF.Ln,
                                             bias=m[:, MI[1e-8]:MI[1e-8] + 1]),
            wait=[(vv, vv.n)])
    ph.emit("scalar", lambda e: e.activation(out=rrall[:], in_=rrall[:], func=AF.Exp,
                                             scale=m[:, MI[-0.5]:MI[-0.5] + 1]),
            inc=[(ss, 1)])
    batch_ss = ss.n
    ph.wait("vector", ss, batch_ss)
    if DEBUG_SCRATCH:
        dwb = co.sem("p1dwb")
        ph.emit("vector", lambda e: e.tensor_copy(out=dbg[:, :, 0], in_=meanall[:]))
        ph.emit("vector", lambda e: e.tensor_copy(out=dbg[:, :, 1], in_=varall[:]))
        ph.emit("vector", lambda e: e.tensor_copy(out=dbg[:, :, 2], in_=rstdall[:]))
        ph.emit("vector", lambda e: e.tensor_copy(out=dbg[:, :, 3], in_=sqall[:]))
        ph.emit("vector", lambda e: e.tensor_copy(out=dbg[:, :, 6], in_=rrall[:]),
                inc=[(vv, 1)])
        ph.emit("sync", lambda e: e.dma_start(out=io["dbg_d"], in_=dbg[:]),
                wait=[(vv, vv.n)], inc=[(dwb, 16)])
        ph.wait("sync", dwb, 16)

    for i in range(nt):
        par = i % 2
        x_t = xt[par]
        pt = pt_[par]
        ph.emit("sync", lambda e, i=i, par=par: e.dma_start(
            out=xt[par][:], in_=io["x"][i * 128:(i + 1) * 128, :]),
            wait=([(ss, batch_ss)] if i < 2 else None), inc=[(xld[par], 16)])
        ph.emit("vector", lambda e, x_t=x_t, i=i: e.tensor_scalar(
            out=xn[:], in0=x_t[:], scalar1=meanall[:, i:i + 1],
            scalar2=rstdall[:, i:i + 1], op0=ALU.subtract, op1=ALU.mult),
            wait=[(xld[par], xld[par].n)])
        ph.emit("vector", lambda e, i=i: e.tensor_scalar(
            out=nrb[:], in0=xn[:], scalar1=rrall[:, i:i + 1], scalar2=None,
            op0=ALU.mult))
        ph.emit("vector", lambda e: e.tensor_copy(out=xnb[:], in_=xn[:]), inc=[(vv, 1)])
        for kc in range(8):
            ph.emit("tensor", lambda e, kc=kc, pt=pt: e.transpose(
                pt[:, kc * 128:(kc + 1) * 128], xnb[:, kc * 128:(kc + 1) * 128],
                c["id128b"][:]),
                wait=([(vv, vv.n)] if kc == 0 else None),
                inc=([(tt, 1)] if kc == 7 else None))
        ph.emit("vector", lambda e, pt=pt, par=par: e.tensor_copy(
            out=fmx[par][:].rearrange("p a b -> p (a b)"), in_=pt[:]),
            wait=[(tt, tt.n), (wbx[par], wbx[par].n)], inc=[(vv, 1)])
        for kc in range(8):
            ph.emit("tensor", lambda e, kc=kc, pt=pt: e.transpose(
                pt[:, kc * 128:(kc + 1) * 128], nrb[:, kc * 128:(kc + 1) * 128],
                c["id128b"][:]),
                wait=([(vv, vv.n)] if kc == 0 else None),
                inc=([(tt, 1)] if kc == 7 else None))
        for kc in range(8):
            ph.emit("vector", lambda e, pt=pt, par=par, kc=kc: e.tensor_scalar(
                out=fmn[par][:, kc, :], in0=pt[:, kc * 128:(kc + 1) * 128],
                scalar1=nwb[:, kc:kc + 1], scalar2=None, op0=ALU.mult),
                wait=([(tt, tt.n), (wbn[par], wbn[par].n)] if kc == 0 else None),
                inc=([(vv, 1)] if kc == 7 else None))
        ph.emit("sync", lambda e, i=i, par=par: e.dma_start(
            out=io["xn_d"][:, :, i * 128:(i + 1) * 128].rearrange("a p b -> p a b"),
            in_=fmx[par][:]), wait=[(vv, vv.n - 1)], inc=[(wbx[par], 16)])
        ph.emit("sync", lambda e, i=i, par=par: e.dma_start(
            out=io["nr_d"][:, :, i * 128:(i + 1) * 128].rearrange("a p b -> p a b"),
            in_=fmn[par][:]), wait=[(vv, vv.n)], inc=[(wbn[par], 16)])
    for s in wbx + wbn:
        ph.wait("sync", s, s.n)
    return ph



def build_p2(co, io, c, es):
    L = co.L
    ph = Phase("p2")
    if DEBUG_SYNC:
        ph.auto = {e_: co.sem("p2_a" + e_) for e_ in Phase.ENGINES}
    m = c["misc"]
    TILE = 256
    ntile = L // TILE

    wld = co.sem("p2wld")
    xnl = [co.sem("p2xn0"), co.sem("p2xn1")]
    swb = [co.sem("p2swb0"), co.sem("p2swb1")]
    tv = co.sem("p2tv")
    vt = co.sem("p2vt")
    st = co.sem("p2st")

    wq = co.sb(es, "p2_wq", [128, 8, 1024], BF16)
    wk = co.sb(es, "p2_wk", [128, 8, 1024], BF16)
    wv = co.sb(es, "p2_wv", [128, 8, 1024], BF16)
    wosb = co.sb(es, "p2_wo", [128, 8, 1024], BF16)
    g1 = co.sb(es, "p2_g1", [128, 8, 2048], BF16)
    g1s = co.sb(es, "p2_g1s", [32, 2048], BF16)
    for sb_t, nm in ((wq, "wqT"), (wk, "wkT"), (wv, "wvT"), (wosb, "woT"), (g1, "g1T")):
        ph.emit("gpsimd", lambda e, o=sb_t, i=io[nm]: e.dma_start(
            out=o[:], in_=i.rearrange("(kc p) mm -> p kc mm", p=128)), inc=[(wld, 16)])
    ph.emit("gpsimd", lambda e: e.dma_start(out=g1s[:], in_=io["g1Ts"]), inc=[(wld, 16)])

    xnt = [co.sb(es, f"p2_xn{i}", [128, 8, TILE], BF16) for i in range(2)]
    qf = co.sb(es, "p2_qf", [128, 8, TILE], BF16)
    kf = co.sb(es, "p2_kf", [128, 8, TILE], BF16)
    vf = co.sb(es, "p2_vf", [128, 8, 64 + TILE], BF16)
    pre = co.sb(es, "p2_pre", [128, 2, TILE + 3], F32)
    cvx = co.sb(es, "p2_cvx", [128, 2, TILE], F32)
    usb = cvx[:, 0, :]
    lgF = cvx[0:16, 1, :]
    halos = {nm: co.sb(es, f"p2_h{nm}", [128, 8, 3], F32) for nm in "qkv"}

    bP = co.sb(es, "p2_bP", [4, TILE], F32)
    qnP = co.sb(es, "p2_qnP", [4, TILE], F32)
    knP = co.sb(es, "p2_knP", [4, TILE], F32)
    brnP = co.sb(es, "p2_brnP", [4, TILE], F32)
    qncol = co.sb(es, "p2_qnc", [128, 2, 4], F32)
    kncol = co.sb(es, "p2_knc", [128, 2, 4], F32)
    bcol = co.sb(es, "p2_bc", [128, 2, 4], F32)
    brncol = co.sb(es, "p2_brc", [128, 2, 4], F32)
    kntm = co.sb(es, "p2_kntm", [128, 2, 4, 256], BF16)
    kbtm = co.sb(es, "p2_kbtm", [128, 2, 4, 256], BF16)
    vbtm = co.sb(es, "p2_vbtm", [128, 2, 4, 256], BF16)
    vtmT = co.sb(es, "p2_vtm", [128, 2, 8, 128], BF16)
    dtm = co.sb(es, "p2_dtm", [128, 2, 8, 128], BF16)
    stmT = co.sb(es, "p2_stm", [128, 2, 8, 128], BF16)
    ltmT = co.sb(es, "p2_ltm", [128, 2, 8, 128], BF16)
    Sf = co.sb(es, "p2_Sf", [128, 2, 4, 256], F32)
    Sb = co.sb(es, "p2_Sb", [128, 2, 4, 256], BF16)
    firfm = co.sb(es, "p2_firfm", [128, 2, 8, TILE], BF16)
    SF_ = firfm[:, 0, :, :]
    LF_ = firfm[:, 1, :, :]
    qsq = firfm[:, 0, :, :]
    Nb = co.sb(es, "p2_Nb", [CDEV, CDEV], F32)
    Npb = co.sb(es, "p2_Npb", [CDEV, CDEV], F32)
    nt1 = co.sb(es, "p2_nt1", [CDEV, CDEV], F32)
    Msl = co.sb(es, "p2_Msl", [CDEV, CDEV], F32)
    Mpsl = co.sb(es, "p2_Mpsl", [CDEV, CDEV], F32)
    TPf = co.sb(es, "p2_TPf", [CDEV, CDEV], F32)
    TPb = co.sb(es, "p2_TPb", [CDEV, CDEV], BF16)
    attnb = co.sb(es, "p2_attnb", [CDEV, CDEV], BF16)

    ub = co.sb(es, "p2_ub", [CDEV, 256], BF16)
    Wfm = co.sb(es, "p2_Wfm", [128, 2, CDEV], BF16)
    stats = co.sb(es, "p2_stats", [128, 2, 32], F32)
    statsb = co.sb(es, "p2_statsb", [128, 32], BF16)
    statsF = co.sb(es, "p2_stF", [32, 2, 128], BF16)
    sjnk = co.sb(es, "p2_sjnk", [128, 256], F32)
    ssum = co.sb(es, "p2_ssum", [128, 2, 16], F32)
    ssq2 = co.sb(es, "p2_ssq2", [128, 2, 16], F32)
    msq = co.sb(es, "p2_msq", [128, 2, 16], F32)


    lgt = co.sb(es, "p2_lgt", [128, 16], F32)
    mx4 = co.sb(es, "p2_mx4", [128, 4], F32)
    sm4 = co.sb(es, "p2_sm4", [128, 4], F32)
    rsm4 = co.sb(es, "p2_rsm4", [128, 4], F32)
    exv = co.sb(es, "p2_exv", [128, 16], F32)
    prb = co.sb(es, "p2_prb", [128, 2, 16], F32)

    rms4 = co.sb(es, "p2_rms4", [128, 4], F32)
    rr4 = co.sb(es, "p2_rr4", [128, 4], F32)

    cmb2 = co.sb(es, "p2_cmb2", [128, 8, 128], BF16)
    cmb = kntm[:].rearrange("p a b c -> p (a b c)").bitcast(F32).rearrange(
        "p (a b) -> p a b", a=8)
    cfm = kbtm[:].rearrange("p a b c -> p (a b c)").rearrange(
        "p (a b c) -> p a b c", a=8, b=2)
    sf = [co.sb(es, "p2_sf0", [128, 8, TILE], BF16)]
    sf = [sf[0], sf[0]]

    mark = co.sb(es, "p2_mark", [1, 4], F32)
    pA = co.ps(es, "p2_pA", [128, 512], F32)
    pB = co.ps(es, "p2_pB", [128, 512], F32)
    pN = co.ps(es, "p2_pN", [128, 512], F32)
    pT = co.ps(es, "p2_pT", [128, 512], F32)
    pU = co.ps(es, "p2_pU", [128, 512], F32)
    pW = co.ps(es, "p2_pW", [128, 512], F32)
    pS = co.ps(es, "p2_pS", [128, 512], F32)
    ptp = co.ps(es, "p2_pt", [128, 1024], BF16)

    guard = {}

    def mmg(bank, mms, extra_wait=None, inc=True):
        w = [guard[id(bank)]] if id(bank) in guard else []
        if extra_wait:
            w += extra_wait
        for j, f in enumerate(mms):
            ph.emit("tensor", f, wait=(w if j == 0 else None),
                    inc=([(tv, 1)] if (inc and j == len(mms) - 1) else None))
        return tv.n

    def ev(eng, f, bank=None, wait=None, sem=None):
        sem = sem or (vt if eng == "vector" else st)
        ph.emit(eng, f, wait=wait, inc=[(sem, 1)])
        if bank is not None:
            guard[id(bank)] = (sem, sem.n)
        return sem.n

    ph.emit("vector", lambda e: e.memset(Sf[:].rearrange("p a b c -> p (a b c)"), 0.0))
    ph.emit("vector", lambda e: e.memset(Sb[:].rearrange("p a b c -> p (a b c)"), 0.0))
    ph.emit("vector", lambda e: e.memset(vf[:, :, 0:64], 0.0))
    for nm in "qkv":
        ph.emit("vector", lambda e, t=halos[nm]: e.memset(
            t[:].rearrange("p a b -> p (a b)"), 0.0))
    ph.emit("vector", lambda e: e.memset(mark[:], 0.0), inc=[(vt, 1)])
    halo_vt = vt.n

    for ti in range(ntile):
        par = ti % 2
        t0 = ti * TILE
        xb = xnt[par]
        sfb = sf[par]
        ph.emit("sync", lambda e, t0=t0, xb=xb: e.dma_start(
            out=xb[:], in_=io["xn_d"][:, :, t0:t0 + TILE].rearrange("a p b -> p a b")),
            inc=[(xnl[par], 16)])
        wxn = [(xnl[par], xnl[par].n), (wld, wld.n)]

        bv = mmg(pA, [(lambda e, kc=kc, xb=xb: e.matmul(
            pA[0:4, 0:TILE], c["wbT"][:, kc, :], xb[:, kc, :],
            start=(kc == 0), stop=(kc == 7))) for kc in range(8)], extra_wait=wxn)
        ev("scalar", lambda e: e.activation(out=bP[:], in_=pA[0:4, 0:TILE],
                                            func=AF.Sigmoid), bank=pA, wait=[(tv, bv)])

        silu_st = [0, 0]
        conv_done = [0, 0]
        for (wT, cwn, hk, wrdst) in ((wq, "qcw", "q", ("qf", 0)),
                                     (wk, "kcw", "k", ("kf", 0)),
                                     (wv, "vcw", "v", ("vf", 64))):
            halo = halos[hk]
            cw = c[cwn]
            dstn, doff = wrdst
            dst = {"qf": qf, "kf": kf, "vf": vf}[dstn]
            for ot in range(8):
                bank = pA if ot % 2 == 0 else pB
                cpar = ot % 2
                gv = mmg(bank, [(lambda e, kc=kc, ot=ot, wT=wT, bank=bank, xb=xb: e.matmul(
                    bank[:, 0:TILE], wT[:, kc, ot * 128:(ot + 1) * 128], xb[:, kc, :],
                    start=(kc == 0), stop=(kc == 7))) for kc in range(8)],
                    extra_wait=wxn)
                w_pre = [(vt, conv_done[cpar] or halo_vt)]
                ev("scalar", lambda e, ot=ot, bank=bank, cpar=cpar: e.copy(
                    pre[:, cpar, 3:3 + TILE], bank[:, 0:TILE]), bank=bank,
                    wait=[(tv, gv)] + w_pre)
                ph.emit("scalar", lambda e, ot=ot, halo=halo, cpar=cpar: e.copy(
                    pre[:, cpar, 0:3], halo[:, ot, :]), inc=[(st, 1)])
                w_slot = [(st, st.n)] + ([(st, silu_st[cpar])] if silu_st[cpar] else [])
                ph.emit("vector", lambda e, ot=ot, cw=cw, cpar=cpar: e.tensor_scalar(
                    out=cvx[:, cpar, :], in0=pre[:, cpar, 0:TILE], scalar1=cw[:, ot, 0:1],
                    scalar2=None, op0=ALU.mult), wait=w_slot)
                for j in (1, 2, 3):
                    ph.emit("vector", lambda e, ot=ot, j=j, cw=cw, cpar=cpar:
                            e.scalar_tensor_tensor(
                        out=cvx[:, cpar, :], in0=pre[:, cpar, j:j + TILE],
                        scalar=cw[:, ot, j:j + 1],
                        in1=cvx[:, cpar, :], op0=ALU.mult, op1=ALU.add))
                ph.emit("vector", lambda e, ot=ot, halo=halo, cpar=cpar: e.tensor_copy(
                    out=halo[:, ot, :], in_=pre[:, cpar, TILE:TILE + 3]), inc=[(vt, 1)])
                conv_done[cpar] = vt.n
                ph.emit("scalar", lambda e, ot=ot, dst=dst, doff=doff, cpar=cpar:
                        e.activation(
                    out=dst[:, ot, doff:doff + TILE], in_=cvx[:, cpar, :], func=AF.Silu),
                    wait=[(vt, vt.n)], inc=[(st, 1)])
                silu_st[cpar] = st.n
        qkv_st = st.n
        halo_vt = vt.n

        ph.emit("scalar", lambda e: e.activation(out=qsq.rearrange("p a b -> p (a b)"),
                                                 in_=qf[:].rearrange("p a b -> p (a b)"),
                                                 func=AF.Square), inc=[(st, 1)])
        sq_q = st.n
        qmm = []
        for h in range(4):
            for j in range(2):
                idx = len(qmm)
                qmm.append(lambda e, h=h, j=j, first=(idx == 0), last=(idx == 7): e.matmul(
                    pN[0:4, 0:TILE], c["nsel"][:, h, :], qsq[:, 2 * h + j, :],
                    start=first, stop=last))
        nv = mmg(pN, qmm, extra_wait=[(st, sq_q)])
        ev("scalar", lambda e: e.activation(out=qnP[:], in_=pN[0:4, 0:TILE], func=AF.Ln,
                                            bias=m[0:4, MI[1e-6]:MI[1e-6] + 1]),
           wait=[(tv, nv)])
        ph.emit("scalar", lambda e: e.activation(out=qnP[:], in_=qnP[:], func=AF.Exp,
                                                 scale=m[0:4, MI[-0.5]:MI[-0.5] + 1]))
        ph.emit("scalar", lambda e: e.activation(out=qsq.rearrange("p a b -> p (a b)"),
                                                 in_=kf[:].rearrange("p a b -> p (a b)"),
                                                 func=AF.Square), inc=[(st, 1)])
        sq_k = st.n
        kmm = []
        for h in range(4):
            for j in range(2):
                idx = len(kmm)
                kmm.append(lambda e, h=h, j=j, first=(idx == 0), last=(idx == 7): e.matmul(
                    pN[0:4, TILE:2 * TILE], c["nsel"][:, h, :], qsq[:, 2 * h + j, :],
                    start=first, stop=last))
        nv2 = mmg(pN, kmm, extra_wait=[(st, sq_k)])
        ev("scalar", lambda e: e.activation(out=knP[:], in_=pN[0:4, TILE:2 * TILE],
                                            func=AF.Ln,
                                            bias=m[0:4, MI[1e-6]:MI[1e-6] + 1]),
           bank=pN, wait=[(tv, nv2)])
        ph.emit("scalar", lambda e: e.activation(out=knP[:], in_=knP[:], func=AF.Exp,
                                                 scale=m[0:4, MI[-0.5]:MI[-0.5] + 1]),
                inc=[(st, 1)])
        rows_st = st.n
        ph.emit("vector", lambda e: e.tensor_tensor(out=brnP[:], in0=bP[:], in1=knP[:],
                                                    op=ALU.mult),
                wait=[(st, rows_st)], inc=[(vt, 1)])
        rows_vt = vt.n
        colmap = ((qnP, qncol), (knP, kncol), (bP, bcol), (brnP, brncol))
        for ca in range(2):
            for qi, (rowt, colt) in enumerate(colmap):
                cvv = mmg(pN, [lambda e, rowt=rowt, ca=ca, qi=qi: e.transpose(
                    pN[:, 256 + 16 * (4 * ca + qi):256 + 16 * (4 * ca + qi) + 4],
                    rowt[0:4, ca * 128:(ca + 1) * 128], c["id4f"][:])],
                    extra_wait=[(vt, rows_vt), (st, rows_st)])
        cols_tv = tv.n
        for ca in range(2):
            for qi, (rowt, colt) in enumerate(colmap):
                ev("scalar", lambda e, ca=ca, qi=qi, colt=colt: e.copy(
                    colt[:, ca, :],
                    pN[:, 256 + 16 * (4 * ca + qi):256 + 16 * (4 * ca + qi) + 4]),
                    bank=(pN if (ca == 1 and qi == 3) else None),
                    wait=[(tv, cols_tv)])
        cols_st = st.n

        for ca in range(2):
            ktv = mmg(ptp, [(lambda e, kc=kc, ca=ca: e.transpose(
                ptp[:, kc * 128:(kc + 1) * 128], kf[:, kc, ca * 128:(ca + 1) * 128],
                c["id128b"][:])) for kc in range(8)],
                extra_wait=[(st, max(qkv_st, cols_st))])
            for h in range(4):
                ph.emit("vector", lambda e, ca=ca, h=h: e.tensor_scalar(
                    out=kntm[:, ca, h, :],
                    in0=ptp[:, 2 * h * 128:(2 * h + 2) * 128], scalar1=kncol[:, ca, h:h + 1],
                    scalar2=None, op0=ALU.mult), wait=[(tv, ktv)])
                ph.emit("vector", lambda e, ca=ca, h=h: e.tensor_scalar(
                    out=kbtm[:, ca, h, :], in0=ptp[:, 2 * h * 128:(2 * h + 2) * 128],
                    scalar1=kncol[:, ca, h:h + 1], scalar2=bcol[:, ca, h:h + 1],
                    op0=ALU.mult, op1=ALU.mult))
            ph.emit("vector", lambda e: e.memset(mark[:], 0.0), inc=[(vt, 1)])
            guard[id(ptp)] = (vt, vt.n)
            vtv = mmg(ptp, [(lambda e, kc=kc, ca=ca: e.transpose(
                ptp[:, kc * 128:(kc + 1) * 128], vf[:, kc, 64 + ca * 128:64 + (ca + 1) * 128],
                c["id128b"][:])) for kc in range(8)])
            for h in range(4):
                ph.emit("vector", lambda e, ca=ca, h=h: e.tensor_scalar(
                    out=vbtm[:, ca, h, :], in0=ptp[:, 2 * h * 128:(2 * h + 2) * 128],
                    scalar1=bcol[:, ca, h:h + 1], scalar2=None, op0=ALU.mult),
                    wait=[(tv, vtv)])
            ph.emit("vector", lambda e, ca=ca: e.tensor_copy(
                out=vtmT[:, ca, :, :].rearrange("p a b -> p (a b)"), in_=ptp[:]),
                inc=[(vt, 1)])
            guard[id(ptp)] = (vt, vt.n)

            for h in range(4):
                gv1 = mmg(pN, [(lambda e, j=j, ca=ca, h=h: e.matmul(
                    pN[:, 0:128], kf[:, 2 * h + j, ca * 128:(ca + 1) * 128],
                    kf[:, 2 * h + j, ca * 128:(ca + 1) * 128],
                    start=(j == 0), stop=(j == 1))) for j in range(2)], inc=False)
                gv2 = mmg(pN, [(lambda e, j=j, ca=ca, h=h: e.matmul(
                    pN[:, 128:256], kf[:, 2 * h + j, ca * 128:(ca + 1) * 128],
                    qf[:, 2 * h + j, ca * 128:(ca + 1) * 128],
                    start=(j == 0), stop=(j == 1))) for j in range(2)], inc=False)
                gv3 = mmg(pN, [lambda e, ca=ca, h=h: e.matmul(
                    pN[:, 256:384], c["sel4"][:, h, :], knP[0:4, ca * 128:(ca + 1) * 128],
                    start=True, stop=True)], inc=False)
                gv4 = mmg(pN, [lambda e, ca=ca, h=h: e.matmul(
                    pN[:, 384:512], c["sel4"][:, h, :], brnP[0:4, ca * 128:(ca + 1) * 128],
                    start=True, stop=True)])
                gv = tv.n
                ph.emit("vector", lambda e, ca=ca, h=h: e.scalar_tensor_tensor(
                    out=nt1[:], in0=pN[:, 0:128], scalar=brncol[:, ca, h:h + 1],
                    in1=c["mLSn"][:], op0=ALU.mult, op1=ALU.mult), wait=[(tv, gv)])
                ph.emit("vector", lambda e, ca=ca, h=h: e.scalar_tensor_tensor(
                    out=attnb[:], in0=pN[:, 128:256], scalar=kncol[:, ca, h:h + 1],
                    in1=c["mUI"][:], op0=ALU.mult, op1=ALU.mult))
                ph.emit("vector", lambda e: e.tensor_tensor(
                    out=Nb[:], in0=nt1[:], in1=pN[:, 256:384], op=ALU.mult))
                ph.emit("vector", lambda e, ca=ca, h=h: e.scalar_tensor_tensor(
                    out=nt1[:], in0=pN[:, 0:128], scalar=kncol[:, ca, h:h + 1],
                    in1=c["mUSn"][:], op0=ALU.mult, op1=ALU.mult))
                co.vpad(ph, 1)
                ph.emit("vector", lambda e: e.tensor_tensor(
                    out=Npb[:], in0=nt1[:], in1=pN[:, 384:512], op=ALU.mult))
                co.vpad(ph, 1)
                ph.emit("vector", lambda e: e.tensor_tensor(
                    out=TPf[:], in0=Npb[:], in1=c["id128f"][:], op=ALU.add),
                        inc=[(vt, 1)])
                guard[id(pN)] = (vt, vt.n)
                nb_vt = vt.n
                curM = lambda: Nb[:]
                curMp = lambda: Npb[:]
                for lev in range(6):
                    sq_tv = mmg(pT, [
                        lambda e, curM=curM, curMp=curMp: e.matmul(
                            pT[:, 0:128], curMp(), curM(), start=True, stop=True),
                        lambda e, curM=curM, curMp=curMp: e.matmul(
                            pT[:, 128:256], curM(), curMp(), start=True, stop=True)],
                        extra_wait=[(vt, nb_vt)])
                    ph.emit("vector", lambda e: e.tensor_copy(
                        out=Msl[:], in_=pT[:, 0:128]), wait=[(tv, sq_tv)])
                    ph.emit("vector", lambda e: e.tensor_copy(
                        out=Mpsl[:], in_=pT[:, 128:256]), inc=[(vt, 1)])
                    mev = vt.n
                    acc_tv = mmg(pT, [lambda e: e.matmul(
                        pT[:, 256:384], Msl[:], TPf[:], start=True, stop=True)],
                        extra_wait=[(vt, mev)])
                    ph.emit("vector", lambda e: e.tensor_tensor(
                        out=TPf[:], in0=TPf[:], in1=pT[:, 256:384], op=ALU.add),
                        inc=[(vt, 1)], wait=[(tv, acc_tv)])
                    guard[id(pT)] = (vt, vt.n)
                    nb_vt = vt.n
                    curM = lambda: Msl[:]
                    curMp = lambda: Mpsl[:]
                co.vpad(ph, 1)
                ph.emit("vector", lambda e: e.tensor_copy(out=TPb[:], in_=TPf[:]),
                        inc=[(vt, 1)])
                tb_vt = vt.n
                u_tv = mmg(pU, [lambda e, ca=ca, h=h: e.matmul(
                    pU[:, 0:256], TPb[:], vbtm[:, ca, h, :], start=True, stop=True)],
                    extra_wait=[(vt, tb_vt)], inc=False)
                w_tv = mmg(pW, [
                    lambda e, ca=ca, h=h: e.matmul(
                        pW[:, 0:128], kbtm[:, ca, h, 0:128], TPb[:], start=True, stop=True),
                    lambda e, ca=ca, h=h: e.matmul(
                        pW[:, 128:256], kbtm[:, ca, h, 128:256], TPb[:],
                        start=True, stop=True)])
                ph.emit("vector", lambda e: e.tensor_copy(out=usb[:], in_=pU[:, 0:256]),
                        wait=[(tv, w_tv)])
                ph.emit("vector", lambda e: e.tensor_copy(
                    out=Wfm[:].rearrange("p a b -> p (a b)"), in_=pW[:, 0:256]),
                    inc=[(vt, 1)])
                guard[id(pU)] = (vt, vt.n)
                guard[id(pW)] = (vt, vt.n)
                wev = vt.n
                ws_tv = mmg(pW, [(lambda e, kc=kc, h=h: e.matmul(
                    pW[:, 256:512], Wfm[:, kc, :], Sb[:, kc, h, :],
                    start=(kc == 0), stop=(kc == 1))) for kc in range(2)],
                    extra_wait=[(vt, wev)])
                ph.emit("vector", lambda e: e.tensor_tensor(
                    out=ub[:], in0=usb[:], in1=pW[:, 256:512], op=ALU.subtract),
                    wait=[(tv, ws_tv)], inc=[(vt, 1)])
                guard[id(pW)] = (vt, vt.n)
                ub_vt = vt.n
                o_tv = mmg(pU, [
                    lambda e, ca=ca, h=h: e.matmul(
                        pU[:, 256:512], qf[:, 2 * h, ca * 128:(ca + 1) * 128],
                        Sb[:, 0, h, :], start=True, stop=False),
                    lambda e, ca=ca, h=h: e.matmul(
                        pU[:, 256:512], qf[:, 2 * h + 1, ca * 128:(ca + 1) * 128],
                        Sb[:, 1, h, :], start=False, stop=False),
                    lambda e: e.matmul(
                        pU[:, 256:512], attnb[:], ub[:], start=False, stop=True)],
                    extra_wait=[(vt, ub_vt)])
                ph.emit("vector", lambda e, ca=ca, h=h: e.tensor_scalar(
                    out=dtm[:, ca, 2 * h:2 * h + 2, :].rearrange("p a b -> p (a b)"),
                    in0=pU[:, 256:512], scalar1=qncol[:, ca, h:h + 1], scalar2=None,
                    op0=ALU.mult), wait=[(tv, o_tv)], inc=[(vt, 1)])
                guard[id(pU)] = (vt, vt.n)
                s_tv = mmg(pS, [
                    lambda e, ca=ca, h=h: e.matmul(
                        pS[:, 0:256], kntm[:, ca, h, 0:128], ub[:], start=True, stop=True),
                    lambda e, ca=ca, h=h: e.matmul(
                        pS[:, 256:512], kntm[:, ca, h, 128:256], ub[:],
                        start=True, stop=True)], extra_wait=[(vt, ub_vt)])
                for kc in range(2):
                    ph.emit("vector", lambda e, kc=kc, h=h: e.tensor_tensor(
                        out=Sf[:, kc, h, :], in0=Sf[:, kc, h, :],
                        in1=pS[:, kc * 256:(kc + 1) * 256], op=ALU.add),
                        wait=[(tv, s_tv)])
                for kc in range(2):
                    ph.emit("vector", lambda e, kc=kc, h=h: e.tensor_copy(
                        out=Sb[:, kc, h, :], in_=Sf[:, kc, h, :]),
                        inc=([(vt, 1)] if kc == 1 else None))
                guard[id(pS)] = (vt, vt.n)

        fir_wait = [(st, qkv_st), (tv, tv.n)]
        for kc in range(8):
            ph.emit("vector", lambda e, kc=kc: e.tensor_scalar(
                out=SF_[:, kc, :], in0=vf[:, kc, 62:62 + TILE],
                scalar1=c["fsw"][:, kc, 0:1], scalar2=None, op0=ALU.mult),
                wait=(fir_wait if kc == 0 else None))
        for j in (1, 2):
            for kc in range(8):
                ph.emit("vector", lambda e, kc=kc, j=j: e.scalar_tensor_tensor(
                    out=SF_[:, kc, :], in0=vf[:, kc, 62 + j:62 + j + TILE],
                    scalar=c["fsw"][:, kc, j:j + 1], in1=SF_[:, kc, :],
                    op0=ALU.mult, op1=ALU.add))
        for kc in range(8):
            ph.emit("vector", lambda e, kc=kc: e.tensor_scalar(
                out=LF_[:, kc, :], in0=vf[:, kc, 2:2 + TILE],
                scalar1=c["flw"][:, kc, 0:1], scalar2=None, op0=ALU.mult))
        for j in range(1, 63):
            for kc in range(8):
                ph.emit("vector", lambda e, kc=kc, j=j: e.scalar_tensor_tensor(
                    out=LF_[:, kc, :], in0=vf[:, kc, 2 + j:2 + j + TILE],
                    scalar=c["flw"][:, kc, j:j + 1], in1=LF_[:, kc, :],
                    op0=ALU.mult, op1=ALU.add))
        ph.emit("vector", lambda e: e.tensor_copy(
            out=vf[:, :, 0:64], in_=vf[:, :, TILE:TILE + 64]), inc=[(vt, 1)])
        fir_vt = vt.n

        for ca in range(2):
            for (src, dstt) in ((SF_, stmT), (LF_, ltmT)):
                ptv = mmg(ptp, [(lambda e, kc=kc, ca=ca, src=src: e.transpose(
                    ptp[:, kc * 128:(kc + 1) * 128], src[:, kc, ca * 128:(ca + 1) * 128],
                    c["id128b"][:])) for kc in range(8)],
                    extra_wait=[(vt, fir_vt)])
                ev("vector", lambda e, ca=ca, dstt=dstt: e.tensor_copy(
                    out=dstt[:, ca, :, :].rearrange("p a b -> p (a b)"), in_=ptp[:]),
                    bank=ptp, wait=[(tv, ptv)])
        paths_vt = vt.n

        for ca in range(2):
            for pi, patht in enumerate((stmT, ltmT, dtm, vtmT)):
                for h in range(4):
                    col = pi * 4 + h
                    ph.emit("scalar", lambda e, ca=ca, h=h, patht=patht, col=col:
                            e.activation(
                        out=sjnk[:], in_=patht[:, ca, 2 * h:2 * h + 2, :]
                        .rearrange("p a b -> p (a b)"), func=AF.Identity,
                        accum_out=ssum[:, ca, col:col + 1]), wait=[(vt, paths_vt)])
                    ph.emit("scalar", lambda e, ca=ca, h=h, patht=patht, col=col:
                            e.activation(
                        out=sjnk[:], in_=patht[:, ca, 2 * h:2 * h + 2, :]
                        .rearrange("p a b -> p (a b)"), func=AF.Square,
                        accum_out=ssq2[:, ca, col:col + 1]))
            ph.emit("scalar", lambda e, ca=ca: e.activation(
                out=sjnk[:, 0:16], in_=ssum[:, ca, :], func=AF.Identity))
            ph.emit("scalar", lambda e, ca=ca: e.activation(
                out=sjnk[:, 0:16], in_=ssq2[:, ca, :], func=AF.Identity),
                inc=[(st, 1)])
            ph.emit("vector", lambda e, ca=ca: e.tensor_scalar(
                out=stats[:, ca, 0:32:2], in0=ssum[:, ca, :],
                scalar1=m[:, MI[1.0 / DK]:MI[1.0 / DK] + 1], scalar2=None,
                op0=ALU.mult), wait=[(st, st.n)], inc=[(vt, 1)])
            ph.emit("scalar", lambda e, ca=ca: e.activation(
                out=msq[:, ca, :], in_=stats[:, ca, 0:32:2], func=AF.Square),
                wait=[(vt, vt.n)], inc=[(st, 1)])
            ph.emit("vector", lambda e, ca=ca: e.scalar_tensor_tensor(
                out=stats[:, ca, 1:32:2], in0=ssq2[:, ca, :],
                scalar=m[:, MI[1.0 / DK]:MI[1.0 / DK] + 1], in1=msq[:, ca, :],
                op0=ALU.mult, op1=ALU.subtract), wait=[(st, st.n)])
        co.vpad(ph, 1)
        ph.emit("vector", lambda e: e.memset(mark[:], 0.0), inc=[(vt, 1)])
        stats_vt = vt.n
        statsb2 = statsb
        for ca in range(2):
            prev_tv = tv.n
            ph.emit("vector", lambda e, ca=ca: e.tensor_copy(
                out=statsb2[:], in_=stats[:, ca, :]),
                wait=[(vt, stats_vt), (tv, prev_tv)], inc=[(vt, 1)])
            stv = mmg(ptp, [lambda e: e.transpose(ptp[0:32, 0:128], statsb2[:],
                                                  c["id128b"][:])],
                      extra_wait=[(vt, vt.n)])
            ev("vector", lambda e, ca=ca: e.tensor_copy(
                out=statsF[0:32, ca, :], in_=ptp[0:32, 0:128]), bank=ptp,
                wait=[(tv, stv)])
        stats_f_vt = vt.n

        hmid = firfm[:].rearrange("p f a b -> p (f a) b")
        for mt in range(16):
            bank = pA if mt % 2 == 0 else pB
            mms = [(lambda e, kc=kc, mt=mt, bank=bank, xb=xb: e.matmul(
                bank[:, 0:TILE], g1[:, kc, mt * 128:(mt + 1) * 128], xb[:, kc, :],
                start=(kc == 0), stop=False)) for kc in range(8)]
            mms.append(lambda e, mt=mt, bank=bank: e.matmul(
                bank[:, 0:TILE], g1s[:, mt * 128:(mt + 1) * 128],
                statsF[:].rearrange("p a b -> p (a b)"), start=False, stop=True))
            gv = mmg(bank, mms, extra_wait=[(vt, stats_f_vt)])
            ev("scalar", lambda e, mt=mt, bank=bank: e.activation(
                out=hmid[:, mt, :], in_=bank[:, 0:TILE], func=AF.Gelu,
                bias=c["gb1c"][:, mt:mt + 1]), bank=bank, wait=[(tv, gv)])
        hmid_st = st.n
        lmm = [(lambda e, kc=kc: e.matmul(
            pN[0:16, 0:TILE], c["g2T"][:, kc, :], hmid[:, kc, :],
            start=(kc == 0), stop=(kc == 15))) for kc in range(16)]
        lg_tv = mmg(pN, lmm, extra_wait=[(st, hmid_st)])
        ev("vector", lambda e: e.tensor_copy(out=lgF[:], in_=pN[0:16, 0:TILE]),
           bank=pN, wait=[(tv, lg_tv)])
        lg_vt = vt.n
        for ca in range(2):
            lt_tv = mmg(pU, [lambda e, ca=ca: e.transpose(
                pU[:, 0:16], lgF[0:16, ca * 128:(ca + 1) * 128], c["id16f"][:])],
                extra_wait=[(vt, lg_vt)])
            ph.emit("vector", lambda e: e.tensor_tensor(
                out=lgt[:], in0=pU[:, 0:16], in1=c["bbb"][:], op=ALU.add),
                wait=[(tv, lt_tv)], inc=[(vt, 1)])
            guard[id(pU)] = (vt, vt.n)
            co.vpad(ph, 1)
            ph.emit("vector", lambda e: e.tensor_tensor(
                out=lgt[:], in0=lgt[:], in1=c["itb"][:], op=ALU.mult))
            co.vpad(ph, 1)
            for h in range(4):
                ph.emit("vector", lambda e, h=h: e.reduce_max(
                    out=mx4[:, h:h + 1], in_=lgt[:, 4 * h:4 * h + 4],
                    axis=mybir.AxisListType.X))
            co.vpad(ph, 4)
            for h in range(4):
                ph.emit("vector", lambda e, h=h: e.tensor_scalar(
                    out=exv[:, 4 * h:4 * h + 4], in0=lgt[:, 4 * h:4 * h + 4],
                    scalar1=mx4[:, h:h + 1], scalar2=None, op0=ALU.subtract),
                    inc=([(vt, 1)] if h == 3 else None))
            ph.emit("scalar", lambda e: e.activation(out=exv[:], in_=exv[:], func=AF.Exp),
                    wait=[(vt, vt.n)], inc=[(st, 1)])
            ph.wait("vector", st, st.n)
            for h in range(4):
                ph.emit("vector", lambda e, h=h: e.reduce_sum(
                    out=sm4[:, h:h + 1], in_=exv[:, 4 * h:4 * h + 4],
                    axis=mybir.AxisListType.X), inc=([(vt, 1)] if h == 3 else None))
            ph.emit("scalar", lambda e: e.activation(out=rsm4[:], in_=sm4[:],
                                                     func=AF.Ln), wait=[(vt, vt.n)])
            ph.emit("scalar", lambda e: e.activation(
                out=rsm4[:], in_=rsm4[:], func=AF.Exp,
                scale=m[:, MI[-1.0]:MI[-1.0] + 1]), inc=[(st, 1)])
            ph.wait("vector", st, st.n)
            for h in range(4):
                ph.emit("vector", lambda e, h=h, ca=ca: e.scalar_tensor_tensor(
                    out=prb[:, ca, 4 * h:4 * h + 4], in0=exv[:, 4 * h:4 * h + 4],
                    scalar=rsm4[:, h:h + 1], in1=c["flb"][:, 4 * h:4 * h + 4],
                    op0=ALU.mult, op1=ALU.max))
            co.vpad(ph, 1)
            for h in range(4):
                ph.emit("vector", lambda e, h=h, ca=ca: e.reduce_sum(
                    out=sm4[:, h:h + 1], in_=prb[:, ca, 4 * h:4 * h + 4],
                    axis=mybir.AxisListType.X), inc=([(vt, 1)] if h == 3 else None))
            ph.emit("scalar", lambda e: e.activation(out=rsm4[:], in_=sm4[:],
                                                     func=AF.Ln), wait=[(vt, vt.n)])
            ph.emit("scalar", lambda e: e.activation(
                out=rsm4[:], in_=rsm4[:], func=AF.Exp,
                scale=m[:, MI[-1.0]:MI[-1.0] + 1]), inc=[(st, 1)])
            ph.wait("vector", st, st.n)
            co.vpad(ph, 1)
            for h in range(4):
                ph.emit("vector", lambda e, h=h, ca=ca: e.tensor_scalar(
                    out=prb[:, ca, 4 * h:4 * h + 4], in0=prb[:, ca, 4 * h:4 * h + 4],
                    scalar1=rsm4[:, h:h + 1], scalar2=None, op0=ALU.mult))
            co.vpad(ph, 1)
            for h in range(4):
                for pi, patht in enumerate((stmT, ltmT, dtm, vtmT)):
                    if pi == 0:
                        ph.emit("vector", lambda e, h=h, ca=ca, patht=patht: e.tensor_scalar(
                            out=cmb[:, 2 * h:2 * h + 2, :].rearrange("p a b -> p (a b)"),
                            in0=patht[:, ca, 2 * h:2 * h + 2, :].rearrange("p a b -> p (a b)"),
                            scalar1=prb[:, ca, 4 * h:4 * h + 1], scalar2=None,
                            op0=ALU.mult))
                    else:
                        ph.emit("vector", lambda e, h=h, ca=ca, pi=pi, patht=patht:
                                e.scalar_tensor_tensor(
                            out=cmb[:, 2 * h:2 * h + 2, :].rearrange("p a b -> p (a b)"),
                            in0=patht[:, ca, 2 * h:2 * h + 2, :].rearrange("p a b -> p (a b)"),
                            scalar=prb[:, ca, 4 * h + pi:4 * h + pi + 1],
                            in1=cmb[:, 2 * h:2 * h + 2, :].rearrange("p a b -> p (a b)"),
                            op0=ALU.mult, op1=ALU.add))
            ph.emit("vector", lambda e: e.memset(mark[:], 0.0), inc=[(vt, 1)])
            cmb_vt = vt.n
            for h in range(4):
                ph.emit("scalar", lambda e, h=h: e.activation(
                    out=sjnk[:], in_=cmb[:, 2 * h:2 * h + 2, :].rearrange("p a b -> p (a b)"),
                    func=AF.Square, accum_out=rms4[:, h:h + 1]), wait=[(vt, cmb_vt)])
            ph.emit("scalar", lambda e: e.activation(
                out=sjnk[:, 0:4], in_=rms4[:], func=AF.Identity))
            ph.emit("scalar", lambda e: e.activation(
                out=rr4[:], in_=rms4[:], func=AF.Ln,
                scale=m[:, MI[1.0 / DK]:MI[1.0 / DK] + 1],
                bias=m[:, MI[1e-5]:MI[1e-5] + 1]))
            ph.emit("scalar", lambda e: e.activation(
                out=rr4[:], in_=rr4[:], func=AF.Exp,
                scale=m[:, MI[-0.5]:MI[-0.5] + 1]), inc=[(st, 1)])
            for h in range(4):
                ph.emit("vector", lambda e, h=h: e.scalar_tensor_tensor(
                    out=cmb2[:, 2 * h:2 * h + 2, :].rearrange("p a b -> p (a b)"),
                    in0=cmb[:, 2 * h:2 * h + 2, :].rearrange("p a b -> p (a b)"),
                    scalar=rr4[:, h:h + 1],
                    in1=c["onb"][:], op0=ALU.mult, op1=ALU.mult),
                    wait=[(st, st.n)],
                    inc=([(vt, 1)] if h == 3 else None))
            cmb2_vt = vt.n
            ctv = mmg(ptp, [(lambda e, kc=kc: e.transpose(
                ptp[:, kc * 128:(kc + 1) * 128], cmb2[:, kc, :], c["id128b"][:]))
                for kc in range(8)], extra_wait=[(vt, cmb2_vt)])
            ev("vector", lambda e, ca=ca: e.tensor_copy(
                out=cfm[:, :, ca, :],
                in_=ptp[:].rearrange("p (a b) -> p a b", a=8)),
                bank=ptp, wait=[(tv, ctv)])
        cfm_vt = vt.n
        for ot in range(8):
            bank = pA if ot % 2 == 0 else pB
            gv = mmg(bank, [(lambda e, kc=kc, ot=ot, bank=bank: e.matmul(
                bank[:, 0:TILE], wosb[:, kc, ot * 128:(ot + 1) * 128],
                cfm[:, kc, :, :].rearrange("p a b -> p (a b)"),
                start=(kc == 0), stop=(kc == 7))) for kc in range(8)],
                extra_wait=[(vt, cfm_vt)])
            ev("vector", lambda e, ot=ot, bank=bank, sfb=sfb: e.tensor_copy(
                out=sfb[:, ot, :], in_=bank[:, 0:TILE]), bank=bank,
                wait=[(tv, gv), (swb[par], swb[par].n)])
        ph.emit("sync", lambda e, t0=t0, sfb=sfb: e.dma_start(
            out=io["s_d"][:, :, t0:t0 + TILE].rearrange("a p b -> p a b"), in_=sfb[:]),
            wait=[(vt, vt.n)], inc=[(swb[par], 16)])
    for s in swb:
        ph.wait("sync", s, s.n)
    return ph




def build_p3(co, io, c, es):
    L = co.L
    ph = Phase("p3")
    if DEBUG_SYNC:
        ph.auto = {e_: co.sem("p3_a" + e_) for e_ in Phase.ENGINES}
    m = c["misc"]
    TM = min(1024, L)
    nmega = L // TM
    NT = TM // 512
    NSUB = TM // 128

    tv = co.sem("p3tv")
    vt = co.sem("p3vt")
    st = co.sem("p3st")
    wl = [co.sem("p3wl0"), co.sem("p3wl1")]
    nrl = co.sem("p3nrl")
    ssl = co.sem("p3ssl")
    xl = [co.sem("p3xl0"), co.sem("p3xl1")]
    owb = [co.sem("p3ow0"), co.sem("p3ow1")]

    nrm = co.sb(es, "p3_nrm", [128, 8, TM], BF16)
    w1b = [co.sb(es, f"p3_w1{i}", [128, 8, 1024], BF16) for i in range(2)]
    w2b = [co.sb(es, f"p3_w2{i}", [128, 8, 1024], BF16) for i in range(2)]
    hq = co.sb(es, "p3_hq", [128, 8, TM], BF16)
    hr = co.sb(es, "p3_hr", [128, 2, 512], BF16)
    racc = co.sb(es, "p3_racc", [128, 8, TM], F32)
    cbc = co.sb(es, "p3_cbc", [128, TM], BF16)
    comb8 = co.sb(es, "p3_c8", [8, NSUB, 128], BF16)
    probs = co.sb(es, "p3_probs", [128, 8], F32)
    ge88 = co.sb(es, "p3_ge", [128, 8, 8], F32)
    cnt8 = co.sb(es, "p3_cnt", [128, 8], F32)
    den = co.sb(es, "p3_den", [128, 1], F32)
    rden = co.sb(es, "p3_rden", [128, 1], F32)
    cmbb = co.sb(es, "p3_cmbb", [128, 8], BF16)
    dbg3p = co.sb(es, "p3_d3p", [128, 4, 8, 32], F32) if DEBUG_P3 else None
    gfm = co.sb(es, "p3_gfm", [128, 8, TM], BF16)
    sslab = co.sb(es, "p3_ss", [128, 8, TM], BF16)
    xt3 = [co.sb(es, f"p3_x{i}", [128, 1024], F32) for i in range(2)]
    zt = co.sb(es, "p3_zt", [128, 2, 128], BF16)
    ot3 = [co.sb(es, f"p3_o{i}", [128, 1024], F32) for i in range(2)]

    pA = co.ps(es, "p3_pA", [128, 512], F32)
    pB = co.ps(es, "p3_pB", [128, 512], F32)
    pC = co.ps(es, "p3_pC", [128, 512], F32)
    pD = co.ps(es, "p3_pD", [128, 512], F32)
    pR = co.ps(es, "p3_pR", [128, 512], F32)
    ptp = co.ps(es, "p3_pt", [128, 1024], BF16)

    guard = {}

    def mmg(bank, mms, extra_wait=None, inc=True):
        w = [guard[id(bank)]] if id(bank) in guard else []
        if extra_wait:
            w += extra_wait
        if PARANOID:
            w = w + [(vt, vt.n), (st, st.n)]
        for j, f in enumerate(mms):
            ph.emit("tensor", f, wait=(w if j == 0 else None),
                    inc=([(tv, 1)] if (inc and j == len(mms) - 1) else None))
        return tv.n

    def ev(eng, f, bank=None, wait=None):
        sem = vt if eng == "vector" else st
        if PARANOID:
            wait = (wait or []) + [(tv, tv.n), (vt if eng != "vector" else st,
                                                (vt if eng != "vector" else st).n)]
        ph.emit(eng, f, wait=wait, inc=[(sem, 1)])
        if bank is not None:
            guard[id(bank)] = (sem, sem.n)
        return sem.n

    wfree = [0, 0]
    def wload(e, q, par):
        if e == 8:
            src1 = io["sw1"][:, q * 1024:(q + 1) * 1024]
            src2 = io["sw2"][q * 1024:(q + 1) * 1024, :]
        else:
            src1 = io["ew1"][e, :, q * 1024:(q + 1) * 1024]
            src2 = io["ew2"][e, q * 1024:(q + 1) * 1024, :]
        w = [(tv, wfree[par])] if wfree[par] else None
        if PARANOID:
            w = [(tv, tv.n), (vt, vt.n), (st, st.n)]
        ph.emit("gpsimd", lambda e_, s=src1, par=par: e_.dma_start(
            out=w1b[par][:], in_=s.rearrange("(kc p) mm -> p kc mm", p=128)),
            wait=w, inc=[(wl[par], 16)])
        ph.emit("gpsimd", lambda e_, s=src2, par=par: e_.dma_start(
            out=w2b[par][:], in_=s.rearrange("(kc p) mm -> p kc mm", p=128)),
            inc=[(wl[par], 16)])

    first_racc = True
    for mi in range(nmega):
        t0 = mi * TM
        ph.emit("sync", lambda e, t0=t0: e.dma_start(
            out=nrm[:], in_=io["nr_d"][:, :, t0:t0 + TM].rearrange("a p b -> p a b")),
            wait=[(vt, vt.n), (st, st.n), (tv, tv.n)], inc=[(nrl, 16)])
        ph.emit("sync", lambda e, t0=t0: e.dma_start(
            out=sslab[:], in_=io["s_d"][:, :, t0:t0 + TM].rearrange("a p b -> p a b")),
            inc=[(ssl, 16)])
        wn = [(nrl, nrl.n)]
        NEXP = len(P3_EXPERTS)
        if mi == 0:
            wload(P3_EXPERTS[0], 0, 0)
        for sub in range(NSUB):
            rtv = mmg(pR, [(lambda e, kc=kc, sub=sub: e.matmul(
                pR[:, 0:8], nrm[:, kc, sub * 128:(sub + 1) * 128], c["rwT"][:, kc, :],
                start=(kc == 0), stop=(kc == 7))) for kc in range(8)], extra_wait=wn)
            ev("scalar", lambda e: e.activation(out=probs[:], in_=pR[:, 0:8],
                                                func=AF.Sigmoid), bank=pR,
               wait=[(tv, rtv)])
            sp = st.n
            for e8 in range(8):
                ph.emit("vector", lambda e, e8=e8: e.tensor_scalar(
                    out=ge88[:, e8, :], in0=probs[:], scalar1=probs[:, e8:e8 + 1],
                    scalar2=None, op0=ALU.is_ge), wait=([(st, sp)] if e8 == 0 else None))
            co.vpad(ph, 1)
            ph.emit("vector", lambda e: e.reduce_sum(
                out=cnt8[:].rearrange("p (a b) -> p a b", b=1), in_=ge88[:],
                axis=mybir.AxisListType.X))
            co.vpad(ph, 4)
            ph.emit("vector", lambda e: e.scalar_tensor_tensor(
                out=cnt8[:], in0=cnt8[:], scalar=m[:, MI[2.0]:MI[2.0] + 1],
                in1=probs[:], op0=ALU.is_le, op1=ALU.mult))
            co.vpad(ph, 4)
            ph.emit("vector", lambda e: e.reduce_sum(
                out=den[:], in_=cnt8[:], axis=mybir.AxisListType.X), inc=[(vt, 1)])
            ph.emit("scalar", lambda e: e.activation(
                out=rden[:], in_=den[:], func=AF.Ln,
                bias=m[:, MI[1e-6]:MI[1e-6] + 1]), wait=[(vt, vt.n)])
            ph.emit("scalar", lambda e: e.activation(
                out=rden[:], in_=rden[:], func=AF.Exp,
                scale=m[:, MI[-1.0]:MI[-1.0] + 1]), inc=[(st, 1)])
            ph.emit("vector", lambda e: e.tensor_scalar(
                out=cmbb[:], in0=cnt8[:], scalar1=rden[:, 0:1], scalar2=None,
                op0=ALU.mult), wait=[(st, st.n)], inc=[(vt, 1)])
            if DEBUG_P3:
                d3w = co.sem("p3d3w")
                ph.emit("vector", lambda e, mi=mi, sub=sub: e.tensor_copy(
                    out=dbg3p[:, mi, sub, 0:8], in_=probs[:]))
                ph.emit("vector", lambda e, mi=mi, sub=sub: e.tensor_copy(
                    out=dbg3p[:, mi, sub, 8:16], in_=cnt8[:]))
                ph.emit("vector", lambda e, mi=mi, sub=sub: e.tensor_copy(
                    out=dbg3p[:, mi, sub, 16:24], in_=cmbb[:]))
                ph.emit("vector", lambda e, mi=mi, sub=sub: e.tensor_copy(
                    out=dbg3p[:, mi, sub, 24:25], in_=den[:]))
                ph.emit("vector", lambda e, mi=mi, sub=sub: e.tensor_copy(
                    out=dbg3p[:, mi, sub, 25:26], in_=rden[:]), inc=[(vt, 1)])
            ctv = mmg(ptp, [lambda e: e.transpose(ptp[0:8, 0:128], cmbb[:],
                                                  c["id128b"][:])],
                      extra_wait=[(vt, vt.n)])
            ev("vector", lambda e, sub=sub: e.tensor_copy(
                out=comb8[:, sub, :], in_=ptp[0:8, 0:128]), bank=ptp, wait=[(tv, ctv)])
        comb_vt = vt.n
        if DEBUG_P3:
            ph.emit("sync", lambda e, mi=mi: e.dma_start(
                out=io["dbg3_d"][:, mi, :, :], in_=comb8[:]),
                wait=[(vt, comb_vt)], inc=[(d3w, 16)])
            ph.emit("sync", lambda e, mi=mi: e.dma_start(
                out=io["dbg3p_d"][:, mi, :, :], in_=dbg3p[:, mi, :, :]),
                inc=[(d3w, 16)])
            ph.wait("sync", d3w, d3w.n)

        for ei, e8 in enumerate(P3_EXPERTS):
            for q in range(P3_QUARTERS):
                gq = (mi * NEXP + ei) * 4 + q
                par = gq % 2
                nxt = gq + 1
                nei, nq = divmod(nxt % (NEXP * 4), 4) if nmega > 1 else divmod(nxt, 4)
                if nxt < (mi + 1) * NEXP * 4:
                    ni = nxt - (mi * NEXP) * 4
                    wload(P3_EXPERTS[ni // 4], ni % 4, nxt % 2)
                elif mi + 1 < nmega:
                    wload(P3_EXPERTS[0], 0, nxt % 2)
                if e8 < 8 and q == 0:
                    for nt in range(NT):
                        btv = mmg(pR, [lambda e, e8=e8, nt=nt: e.matmul(
                            pR[:, 0:512], c["sel8"][:, e8, :],
                            comb8[:, 4 * nt:4 * nt + 4, :].rearrange("p a b -> p (a b)"),
                            start=True, stop=True)], extra_wait=[(vt, comb_vt)])
                        ev("vector", lambda e, nt=nt: e.tensor_copy(
                            out=cbc[:, nt * 512:(nt + 1) * 512], in_=pR[:, 0:512]),
                            bank=pR, wait=[(tv, btv)])
                cb_vt = vt.n
                wlw = [(wl[par], wl[par].n)]
                skip = P3_SKIP if (isinstance(P3_SKIP, tuple) and P3_SKIP and isinstance(P3_SKIP[0], str) and q == 1) else ()
                for mt in range(8):
                    for nt in range(NT):
                        bank = pA if (mt * NT + nt) % 2 == 0 else pB
                        gv = mmg(bank, [(lambda e, kc=kc, mt=mt, nt=nt, bank=bank,
                                         par=par: e.matmul(
                            bank[:, 0:512], w1b[par][:, kc, mt * 128:(mt + 1) * 128],
                            nrm[:, kc, nt * 512:(nt + 1) * 512],
                            start=(kc == 0), stop=(kc == 7))) for kc in range(8)],
                            extra_wait=wlw + wn)
                        if "relu" in skip:
                            guard[id(bank)] = (tv, gv)
                        elif e8 == 8:
                            ev("scalar", lambda e, mt=mt, nt=nt, bank=bank, q=q: e.activation(
                                out=hq[:, mt, nt * 512:(nt + 1) * 512], in_=bank[:, 0:512],
                                func=AF.Relu, bias=c["sb1c"][:, q * 8 + mt:q * 8 + mt + 1]),
                                bank=bank, wait=[(tv, gv)])
                        else:
                            hpar = (mt * NT + nt) % 2
                            ev("scalar", lambda e, mt=mt, nt=nt, bank=bank, hpar=hpar,
                               e8=e8, q=q: e.activation(
                                out=hr[:, hpar, :], in_=bank[:, 0:512], func=AF.Relu,
                                bias=c["eb1c"][:, e8, q * 8 + mt:q * 8 + mt + 1]),
                                bank=bank, wait=[(tv, gv), (vt, vt.n)])
                            ph.emit("vector", lambda e, mt=mt, nt=nt, hpar=hpar:
                                    e.tensor_tensor(
                                out=hq[:, mt, nt * 512:(nt + 1) * 512], in0=hr[:, hpar, :],
                                in1=cbc[:, nt * 512:(nt + 1) * 512], op=ALU.mult),
                                wait=[(st, st.n)], inc=[(vt, 1)])
                if e8 == 8:
                    h_ready = [(st, st.n)]
                else:
                    h_ready = [(vt, vt.n)]
                if P3_TRAP is not None and (e8, q) == P3_TRAP:
                    from concourse.bass_interp import add_trap
                    ph.emit("tensor", lambda e: add_trap(e, key="w2start"),
                            wait=h_ready)
                    ph.emit("vector", lambda e: add_trap(e, key="vq0end"))
                for nt in range(NT):
                    if "w2" in skip:
                        break
                    for ft in range(8):
                        if P3_W2MAX is not None and q == 1 and (nt * 8 + ft) >= P3_W2MAX:
                            continue
                        bank = pC if (nt * 8 + ft) % 2 == 0 else pD
                        mms = [(lambda e, kc=kc, ft=ft, nt=nt, bank=bank, par=par:
                                e.matmul(
                            bank[:, 0:512], w2b[par][:, kc, ft * 128:(ft + 1) * 128],
                            hq[:, kc, nt * 512:(nt + 1) * 512],
                            start=(kc == 0), stop=(kc == 7)))
                            for kc in range(8)]
                        gv = mmg(bank, mms, extra_wait=h_ready)
                        if P3_TRAP is not None and (e8, q, nt, ft) == P3_TRAP + (0, 0):
                            from concourse.bass_interp import add_trap
                            ph.emit("tensor", lambda e: add_trap(e, key="w2g0"))
                            trap_after_add = [True]
                        else:
                            trap_after_add = [False]
                        if nt == NT - 1 and ft == 7:
                            wfree[par] = gv
                        if first_racc and e8 == P3_EXPERTS[0] and q == 0:
                            ev("vector", lambda e, ft=ft, nt=nt, bank=bank: e.tensor_copy(
                                out=racc[:, ft, nt * 512:(nt + 1) * 512],
                                in_=bank[:, 0:512]), bank=bank, wait=[(tv, gv)])
                        else:
                            ev("vector", lambda e, ft=ft, nt=nt, bank=bank: e.tensor_tensor(
                                out=racc[:, ft, nt * 512:(nt + 1) * 512],
                                in0=racc[:, ft, nt * 512:(nt + 1) * 512],
                                in1=bank[:, 0:512], op=ALU.add), bank=bank,
                                wait=[(tv, gv)])
                        if trap_after_add[0]:
                            from concourse.bass_interp import add_trap
                            ph.emit("vector", lambda e: add_trap(e, key="postadd"))
                        if e8 == 8 and q == 0 and P3_EB2:
                            btv2 = mmg(pR, [lambda e, ft=ft, nt=nt: e.matmul(
                                pR[:, 0:512], c["eb2"][:, ft * 128:(ft + 1) * 128],
                                comb8[:, 4 * nt:4 * nt + 4, :].rearrange("p a b -> p (a b)"),
                                start=True, stop=True)], extra_wait=[(vt, comb_vt)])
                            ev("vector", lambda e, ft=ft, nt=nt: e.tensor_tensor(
                                out=racc[:, ft, nt * 512:(nt + 1) * 512],
                                in0=racc[:, ft, nt * 512:(nt + 1) * 512],
                                in1=pR[:, 0:512], op=ALU.add), bank=pR,
                                wait=[(tv, btv2)])
                if e8 == P3_EXPERTS[0] and q == 3:
                    first_racc = False
        racc_vt = vt.n
        first_racc = True

        for ft in range(8):
            for nt in range(NT):
                ph.emit("scalar", lambda e, ft=ft, nt=nt: e.activation(
                    out=gfm[:, ft, nt * 512:(nt + 1) * 512],
                    in_=racc[:, ft, nt * 512:(nt + 1) * 512], func=AF.Silu,
                    bias=c["sb2c"][:, ft:ft + 1]), wait=[(vt, racc_vt)],
                    inc=[(st, 1)])
        g_st = st.n
        ztr_done = [0, 0]
        for tt in range(NSUB):
            xpar = tt % 2
            ph.emit("sync", lambda e, t0=t0, tt=tt, xpar=xpar: e.dma_start(
                out=xt3[xpar][:], in_=io["x"][t0 + tt * 128:t0 + (tt + 1) * 128, :]),
                wait=[(owb[xpar], owb[xpar].n)], inc=[(xl[xpar], 16)])
            for ft in range(8):
                zpar = ft % 2
                w_z = [(st, g_st), (ssl, ssl.n), (tv, ztr_done[zpar])]
                ph.emit("vector", lambda e, ft=ft, tt=tt, zpar=zpar: e.tensor_tensor(
                    out=zt[:, zpar, :], in0=gfm[:, ft, tt * 128:(tt + 1) * 128],
                    in1=sslab[:, ft, tt * 128:(tt + 1) * 128], op=ALU.mult),
                    wait=w_z, inc=[(vt, 1)])
                ztv = mmg(ptp, [lambda e, ft=ft, zpar=zpar: e.transpose(
                    ptp[:, ft * 128:(ft + 1) * 128], zt[:, zpar, :], c["id128b"][:])],
                    extra_wait=[(vt, vt.n)], inc=True)
                ztr_done[zpar] = tv.n
            ztv = tv.n
            opar = tt % 2
            for ft in range(8):
                ev("vector", lambda e, ft=ft, opar=opar, xpar=xpar: e.tensor_tensor(
                    out=ot3[opar][:, ft * 128:(ft + 1) * 128],
                    in0=ptp[:, ft * 128:(ft + 1) * 128],
                    in1=xt3[xpar][:, ft * 128:(ft + 1) * 128], op=ALU.add),
                    bank=(ptp if ft == 7 else None),
                    wait=[(tv, ztv), (xl[xpar], xl[xpar].n),
                          (owb[opar], owb[opar].n)])
            ph.emit("sync", lambda e, t0=t0, tt=tt, opar=opar: e.dma_start(
                out=io["out"][t0 + tt * 128:t0 + (tt + 1) * 128, :], in_=ot3[opar][:]),
                wait=[(vt, vt.n)], inc=[(owb[opar], 16)])
    for s_ in owb:
        ph.wait("sync", s_, s_.n)
    ph.wait("vector", owb[0], owb[0].n)
    ph.wait("vector", owb[1], owb[1].n)
    return ph



def build_core_program(L, host, phases=("p1",)):
    nc = bass.Bass("TRN2", target_bir_lowering=False)
    io = declare_io(nc, L, host)
    with contextlib.ExitStack() as ges:
        co = Core(nc, ges, L)
        with contextlib.ExitStack() as es0:
            c, ph0 = build_consts(co, io, es0)
            ph0.run(nc)
        if "p1" in phases:
            with contextlib.ExitStack() as es1:
                build_p1(co, io, c, es1).run(nc)
        if "p2" in phases:
            with contextlib.ExitStack() as es2:
                build_p2(co, io, c, es2).run(nc)
        if "p3" in phases:
            with contextlib.ExitStack() as es3:
                build_p3(co, io, c, es3).run(nc)
        with nc.Block("sclr") as blk0:
            def _clr(g):
                for s_ in co.sems.values():
                    g.sem_clear(s_.h)
            blk0.gpsimd(_clr)
    return nc, io


def kernel(**inputs):
    L = 4096
    host = host_layout(inputs)
    x = np.ascontiguousarray(np.asarray(inputs["L0_attn_hidden_states"], dtype=np.float32))
    nc, io = build_core_program(L, host, phases=("p1", "p2", "p3"))
    in_maps = []
    for cid in range(NCORES):
        im = {"x": np.ascontiguousarray(x[cid])}
        im.update(host)
        in_maps.append(im)
    res = run_bass_kernel_spmd(nc, in_maps, core_ids=list(range(NCORES)))
    return np.stack([r["out"] for r in res.results], axis=0)


# version 05
# speedup vs baseline: 1.0334x; 1.0063x over previous
import contextlib
import numpy as np
import concourse.bass as bass
import concourse.mybir as mybir
from concourse.bass_utils import run_bass_kernel_spmd

F32 = mybir.dt.float32
BF16 = mybir.dt.bfloat16
AF = mybir.ActivationFunctionType
ALU = mybir.AluOpType

D = 1024
H = 4
DK = 256
CHUNK = 32
CDEV = 128
NCORES = 8

P3_EXPERTS = list(range(8)) + [8]
P3_QUARTERS = 4
P3_TRAP = None
P3_EB2 = True
P3_SKIP = ()
P3_W2MAX = None
DEBUG_SCRATCH = False
DEBUG_FEED = False
DEBUG_P3 = False
PARANOID = False
MISC_VALS = [1e-6, 1e-5, 1e-8, -0.5, 1.0 / D, 1.0 / DK, 1e-4, 0.05, 2.0, 1.0,
             0.5, -1.0, 0.0]
MI = {v: i for i, v in enumerate(MISC_VALS)}


class Sem:
    def __init__(self, nc, es, name):
        self.h = es.enter_context(nc.semaphore(name))
        self.n = 0


DEBUG_SYNC = False


class Phase:
    ENGINES = ("sync", "gpsimd", "vector", "scalar", "tensor")

    def __init__(self, name):
        self.name = name
        self.q = {k: [] for k in self.ENGINES}
        self.hwm = {k: {} for k in self.ENGINES}
        self.auto = None
        self.last_done = {}

    def emit(self, eng, f, wait=None, inc=None):
        if DEBUG_SYNC and self.auto and f is not None:
            a = self.auto[eng]
            d = 16 if eng in ("sync", "gpsimd") else 1
            if self.last_done.get(eng):
                wait = list(wait or []) + [self.last_done[eng]]
            if inc:
                s0, d0 = inc[0]
                self.last_done[eng] = (s0, s0.n + d0)
            else:
                inc = [(a, d)]
                self.last_done[eng] = (a, a.n + d)
        q = self.q[eng]
        if wait:
            hw = self.hwm[eng]
            for s, v in wait:
                if v > 0 and v > hw.get(id(s), -1):
                    hw[id(s)] = v
                    q.append(lambda e, s=s, v=v: e.wait_ge(s.h, v))
        if inc:
            for s, d in inc:
                s.n += d
            def g(e, f=f, incs=tuple(inc)):
                ins = f(e)
                for s, d in incs:
                    ins = ins.then_inc(s.h, d)
            q.append(g)
        elif f is not None:
            q.append(f)

    def wait(self, eng, sem, val):
        self.emit(eng, None, wait=[(sem, val)])

    def run(self, nc):
        with nc.Block(self.name) as block:
            for eng in self.ENGINES:
                cmds = self.q[eng]
                if not cmds:
                    continue
                def body(e, cmds=cmds):
                    for cf in cmds:
                        cf(e)
                getattr(block, eng)(body)


def host_layout(inputs):
    g = lambda k: np.ascontiguousarray(np.asarray(inputs[k], dtype=np.float32))
    d = {}
    d["wqT"] = np.ascontiguousarray(g("L0_attn_Wq").T)
    d["wkT"] = np.ascontiguousarray(g("L0_attn_Wk").T)
    d["wvT"] = np.ascontiguousarray(g("L0_attn_Wv").T)
    d["woT"] = np.ascontiguousarray(g("L0_attn_Wo").T)
    g1T = np.ascontiguousarray(g("L0_attn_gate_w1").T)
    d["g1T"] = np.ascontiguousarray(g1T[:D])
    d["g1Ts"] = np.ascontiguousarray(g1T[D:])
    d["g2T"] = np.ascontiguousarray(g("L0_attn_gate_w2").T)
    d["wbT"] = np.ascontiguousarray(g("L0_attn_Wb").T)
    d["rwT"] = np.ascontiguousarray(g("L0_ffn_router_w").T)
    d["sw1"] = g("L0_ffn_sw1")
    d["sw2"] = g("L0_ffn_sw2")
    d["ew1"] = g("L0_ffn_ew1")
    d["ew2"] = g("L0_ffn_ew2")
    d["eb2"] = g("L0_ffn_eb2")
    d["eb1c"] = np.ascontiguousarray(
        g("L0_ffn_eb1").reshape(8, 32, 128).transpose(2, 0, 1))
    d["sb1c"] = np.ascontiguousarray(g("L0_ffn_sb1").reshape(32, 128).T)
    d["sb2c"] = np.ascontiguousarray(g("L0_ffn_sb2").reshape(8, 128).T)
    d["gb1c"] = np.ascontiguousarray(g("L0_attn_gate_b1").reshape(16, 128).T)
    for k, nm in (("qcw", "L0_attn_qconv_w"), ("kcw", "L0_attn_kconv_w"),
                  ("vcw", "L0_attn_vconv_w")):
        d[k] = np.ascontiguousarray(g(nm).reshape(8, 128, 4).transpose(1, 0, 2))
    d["fsw"] = np.ascontiguousarray(
        g("L0_attn_fir_short_w").reshape(8, 128, 3).transpose(1, 0, 2))
    d["flw"] = np.ascontiguousarray(
        g("L0_attn_fir_long_w").reshape(8, 128, 63).transpose(1, 0, 2))
    d["nwb"] = np.ascontiguousarray(
        g("L0_ffn_norm_weight").reshape(8, 128).T)
    d["onb"] = np.ascontiguousarray(
        np.broadcast_to(g("L0_attn_onorm_w")[None, :], (128, DK)))
    d["bbb"] = np.ascontiguousarray(
        np.broadcast_to(g("L0_attn_base_bias").reshape(-1)[None, :], (128, 16)))
    d["lt"] = g("L0_attn_log_temp").reshape(1, 4)
    d["fr"] = g("L0_attn_floor_raw").reshape(1, 16)
    d["misc"] = np.ascontiguousarray(
        np.broadcast_to(np.array(MISC_VALS, dtype=np.float32)[None, :],
                        (128, len(MISC_VALS))))
    d["id128"] = np.eye(128, dtype=np.float32)
    tri = np.triu(np.ones((CDEV, CDEV), dtype=np.float32))
    d["mLSn"] = np.ascontiguousarray(-(1.0 - tri))
    d["mUSn"] = np.ascontiguousarray(-np.triu(tri, 1))
    d["mUI"] = tri.copy()
    d["ones1"] = np.ones((1, 128), dtype=np.float32)
    d["id4"] = np.eye(4, dtype=np.float32)
    d["id16"] = np.eye(16, dtype=np.float32)
    sel8 = np.zeros((8, 8, 128), dtype=np.float32)
    for e in range(8):
        sel8[e, e, :] = 1.0
    d["sel8"] = sel8
    sel4 = np.zeros((4, 4, CDEV), dtype=np.float32)
    for h in range(4):
        sel4[h, h, :] = 1.0
    d["sel4"] = sel4
    nsel = np.zeros((128, 4, 4), dtype=np.float32)
    for h in range(4):
        nsel[:, h, h] = 1.0
    d["nsel"] = nsel
    return d


class Core:
    def __init__(self, nc, es, L):
        self.nc = nc
        self.es = es
        self.L = L
        self.sems = {}
        self.gjk = es.enter_context(nc.sbuf_tensor("gjk", [128, 128], F32))

    def vpad(self, ph, n=4):
        for k in range(n):
            ph.emit("vector", lambda e: e.memset(self.gjk[:, 0:128], 0.0))

    def sem(self, name):
        if name not in self.sems:
            self.sems[name] = Sem(self.nc, self.es, name)
        return self.sems[name]

    def sb(self, es, name, shape, dtype):
        return es.enter_context(self.nc.sbuf_tensor(name, list(shape), dtype))

    def ps(self, es, name, shape, dtype=F32):
        return es.enter_context(self.nc.psum_tensor(name, list(shape), dtype))


def declare_io(nc, L, host):
    t = {}
    t["x"] = nc.dram_tensor("x", [L, D], F32, kind="ExternalInput").ap()
    for k, v in host.items():
        t[k] = nc.dram_tensor(k, list(v.shape), F32, kind="ExternalInput").ap()
    t["out"] = nc.dram_tensor("out", [L, D], F32, kind="ExternalOutput").ap()
    kind = "ExternalOutput" if DEBUG_SCRATCH else "Internal"
    if DEBUG_FEED:
        kind = "ExternalInput"
    if DEBUG_SCRATCH:
        t["dbg_d"] = nc.dram_tensor("dbg_d", [128, 32, 8], F32, kind="ExternalOutput").ap()
    if DEBUG_P3:
        t["dbg3_d"] = nc.dram_tensor("dbg3_d", [8, 4, 8, 128], BF16,
                                     kind="ExternalOutput").ap()
        t["dbg3p_d"] = nc.dram_tensor("dbg3p_d", [128, 4, 8, 32], F32,
                                      kind="ExternalOutput").ap()
    t["xn_d"] = nc.dram_tensor("xn_d", [8, 128, L], BF16, kind=kind).ap()
    t["nr_d"] = nc.dram_tensor("nr_d", [8, 128, L], BF16, kind=kind).ap()
    t["s_d"] = nc.dram_tensor("s_d", [8, 128, L], BF16, kind=kind).ap()
    return t



def build_consts(co, io, es):
    c = {}
    mk = lambda n, shp, dt=F32: c.__setitem__(n, co.sb(co.es, "c_" + n, shp, dt))
    mk("misc", [128, len(MISC_VALS)])
    mk("id128b", [128, 128], BF16)
    mk("id128f", [128, 128])
    mk("mLSn", [CDEV, CDEV]); mk("mUSn", [CDEV, CDEV]); mk("mUI", [CDEV, CDEV])
    mk("id4f", [4, 4]); mk("id16f", [16, 16])
    mk("ones1b", [1, 128], BF16); mk("ones1f", [1, 128])
    mk("sel8", [8, 8, 128], BF16); mk("sel4", [4, 4, CDEV]); mk("nsel", [128, 4, 4], BF16)
    mk("onb", [128, DK]); mk("bbb", [128, 16])
    mk("gb1c", [128, 16]); mk("sb1c", [128, 32]); mk("sb2c", [128, 8])
    mk("eb1c", [128, 8, 32]); mk("eb2", [8, 1024], BF16)
    mk("qcw", [128, 8, 4]); mk("kcw", [128, 8, 4]); mk("vcw", [128, 8, 4])
    mk("fsw", [128, 8, 3]); mk("flw", [128, 8, 63])
    mk("wbT", [128, 8, 4], BF16); mk("rwT", [128, 8, 8], BF16)
    mk("g2T", [128, 16, 16], BF16)
    mk("lt", [1, 4]); mk("fr", [1, 16])
    mk("itb", [128, 16]); mk("flb", [128, 16])
    mk("it16", [1, 16]); mk("fl16", [1, 16]); mk("sp_t", [1, 4])

    ph = Phase("consts")
    if DEBUG_SYNC:
        ph.auto = {e_: co.sem("consts_a" + e_) for e_ in Phase.ENGINES}
    ld = co.sem("c_ld")
    ldg = co.sem("c_ldg")
    cv = co.sem("c_cv")
    cs = co.sem("c_cs")
    mmc = co.sem("c_mmc")
    m = c["misc"]

    plain = ["misc", "id128f", "id4f", "id16f", "mLSn", "mUSn", "mUI", "onb", "bbb",
             "gb1c", "sb1c", "sb2c", "eb1c", "qcw", "kcw", "vcw", "fsw", "flw",
             "sel4", "lt", "fr"]
    for nm in plain:
        srcmap = {"ones1f": "ones1", "id128f": "id128", "id4f": "id4", "id16f": "id16"}
        src = io[srcmap.get(nm, nm)]
        ph.emit("sync", lambda e, o=c[nm], i=src: e.dma_start(out=o[:], in_=i),
                inc=[(ld, 16)])
    ph.emit("sync", lambda e: e.dma_start(out=c["ones1f"][:], in_=io["ones1"]), inc=[(ld, 16)])
    casting = [("id128b", io["id128"]), ("ones1b", io["ones1"]),
               ("sel8", io["sel8"]), ("nsel", io["nsel"]), ("eb2", io["eb2"]),
               ("wbT", io["wbT"].rearrange("(kc p) h -> p kc h", p=128)),
               ("rwT", io["rwT"].rearrange("(kc p) h -> p kc h", p=128)),
               ("g2T", io["g2T"].rearrange("(kc p) h -> p kc h", p=128))]
    for nm, src in casting:
        ph.emit("gpsimd", lambda e, o=c[nm], i=src: e.dma_start(out=o[:], in_=i),
                inc=[(ldg, 16)])
    for fn in (AF.Identity, AF.Relu, AF.Square, AF.Abs, AF.Exp, AF.Ln,
               AF.Sigmoid, AF.Silu, AF.Gelu):
        ph.emit("scalar", lambda e, fn=fn: e.activation(
            out=c["sp_t"][:], in_=c["misc"][0:1, MI[1.0]:MI[1.0] + 4], func=fn),
            wait=[(ld, ld.n)])
    sp = c["sp_t"]
    ph.emit("scalar", lambda e: e.activation(out=sp[:], in_=c["lt"][:], func=AF.Abs),
            wait=[(ld, ld.n), (ldg, ldg.n)])
    ph.emit("scalar", lambda e: e.activation(out=sp[:], in_=sp[:], func=AF.Exp,
                                             scale=m[0:1, MI[-1.0]:MI[-1.0] + 1]))
    ph.emit("scalar", lambda e: e.activation(out=sp[:], in_=sp[:], func=AF.Ln,
                                             bias=m[0:1, MI[1.0]:MI[1.0] + 1]),
            inc=[(cs, 1)])
    ph.emit("vector", lambda e: e.tensor_scalar(
        out=c["it16"][:, 0:4], in0=c["lt"][:], scalar1=m[0:1, MI[0.0]:MI[0.0] + 1],
        scalar2=None, op0=ALU.max), wait=[(cs, cs.n)])
    co.vpad(ph)
    ph.emit("vector", lambda e: e.tensor_tensor(out=c["it16"][:, 0:4],
                                                in0=c["it16"][:, 0:4], in1=sp[:], op=ALU.add))
    co.vpad(ph)
    ph.emit("vector", lambda e: e.tensor_scalar(
        out=c["it16"][:, 0:4], in0=c["it16"][:, 0:4],
        scalar1=m[0:1, MI[1e-4]:MI[1e-4] + 1], scalar2=None, op0=ALU.add))
    co.vpad(ph)
    ph.emit("vector", lambda e: e.reciprocal(out=c["it16"][:, 0:4], in_=c["it16"][:, 0:4]))
    co.vpad(ph)
    for h in range(3, -1, -1):
        for p in range(4):
            if h == 0 and p == 0:
                continue
            last = (h == 0 and p == 3)
            ph.emit("vector", lambda e, h=h, p=p: e.tensor_copy(
                out=c["it16"][:, h * 4 + p:h * 4 + p + 1], in_=c["it16"][:, h:h + 1]),
                inc=([(cv, 1)] if last else None))
    co.vpad(ph)
    ph.emit("scalar", lambda e: e.activation(out=c["fl16"][:], in_=c["fr"][:],
                                             func=AF.Sigmoid), wait=[(ld, ld.n)])
    ph.emit("scalar", lambda e: e.mul(c["fl16"][:], c["fl16"][:],
                                      m[0:1, MI[0.05]:MI[0.05] + 1]), inc=[(cs, 1)])
    pb = co.ps(es, "c_pb", [128, 32], F32)
    ph.emit("tensor", lambda e: e.matmul(pb[:, 0:16], c["ones1f"][:], c["it16"][:],
                                         start=True, stop=True),
            wait=[(cv, cv.n), (cs, cs.n)])
    ph.emit("tensor", lambda e: e.matmul(pb[:, 16:32], c["ones1f"][:], c["fl16"][:],
                                         start=True, stop=True), inc=[(mmc, 1)])
    ph.emit("vector", lambda e: e.tensor_copy(out=c["itb"][:], in_=pb[:, 0:16]),
            wait=[(mmc, mmc.n)])
    ph.emit("vector", lambda e: e.tensor_copy(out=c["flb"][:], in_=pb[:, 16:32]),
            inc=[(cv, 1)])
    ph.wait("sync", cv, cv.n)
    ph.wait("sync", ld, ld.n)
    ph.wait("gpsimd", ldg, ldg.n)
    return c, ph



def build_p1(co, io, c, es):
    L = co.L
    ph = Phase("p1")
    if DEBUG_SYNC:
        ph.auto = {e_: co.sem("p1_a" + e_) for e_ in Phase.ENGINES}
    m = c["misc"]
    nt = L // 128
    nwb = co.sb(es, "p1_nwb", [128, 8], F32)
    nwl = co.sem("p1nwl")
    ph.emit("sync", lambda e: e.dma_start(out=nwb[:], in_=io["nwb"]), inc=[(nwl, 16)])
    ph.wait("vector", nwl, 16)

    xld = [co.sem("p1x0"), co.sem("p1x1")]
    vv = co.sem("p1v")
    ss = co.sem("p1s")
    tt = co.sem("p1t")
    wbx = [co.sem("p1wx0"), co.sem("p1wx1")]
    wbn = [co.sem("p1wn0"), co.sem("p1wn1")]

    xt = [co.sb(es, f"p1_x{i}", [128, D], F32) for i in range(2)]
    jnk = co.sb(es, "p1_jnk", [128, D], F32)
    sumall = co.sb(es, "p1_suma", [128, nt], F32)
    sqall = co.sb(es, "p1_sqa", [128, nt], F32)
    meanall = co.sb(es, "p1_mna", [128, nt], F32)
    varall = co.sb(es, "p1_vra", [128, nt], F32)
    rstdall = co.sb(es, "p1_rsa", [128, nt], F32)
    rrall = co.sb(es, "p1_rra", [128, nt], F32)
    scr = co.sb(es, "p1_scr", [128, nt], F32)
    dbg = co.sb(es, "p1_dbg", [128, nt, 8], F32) if DEBUG_SCRATCH else None
    xn = co.sb(es, "p1_xn", [128, D], F32)
    xnb = co.sb(es, "p1_xnb", [128, D], BF16)
    nrb = co.sb(es, "p1_nrb", [128, D], BF16)
    fmx = [co.sb(es, f"p1_fmx{i}", [128, 8, 128], BF16) for i in range(2)]
    fmn = [co.sb(es, f"p1_fmn{i}", [128, 8, 128], BF16) for i in range(2)]
    pt_ = [co.ps(es, f"p1_pt{i}", [128, 1024], BF16) for i in range(2)]

    acc_done = [0, 0]
    sq_done = [0, 0]
    for i in range(nt):
        par = i % 2
        ph.emit("sync", lambda e, i=i, par=par: e.dma_start(
            out=xt[par][:], in_=io["x"][i * 128:(i + 1) * 128, :]),
            wait=([(vv, acc_done[par])] if acc_done[par] else None),
            inc=[(xld[par], 16)])
        ph.emit("vector", lambda e, i=i, par=par: e.reduce_sum(
            out=sumall[:, i:i + 1], in_=xt[par][:], axis=mybir.AxisListType.X),
            wait=[(xld[par], xld[par].n)])
        ph.emit("scalar", lambda e, i=i, par=par: e.activation(
            out=(jnk if par == 0 else xn)[:], in_=xt[par][:], func=AF.Square),
            wait=[(xld[par], xld[par].n), (vv, sq_done[par])], inc=[(ss, 1)])
        ph.emit("vector", lambda e, i=i, par=par: e.reduce_sum(
            out=sqall[:, i:i + 1], in_=(jnk if par == 0 else xn)[:],
            axis=mybir.AxisListType.X),
            wait=[(ss, ss.n)], inc=[(vv, 1)])
        acc_done[par] = vv.n
        sq_done[par] = vv.n
    if DEBUG_SCRATCH:
        ph.emit("vector", lambda e: e.tensor_copy(out=dbg[:, :, 4], in_=sumall[:]),
                wait=[(ss, ss.n)])
        ph.emit("vector", lambda e: e.tensor_copy(out=dbg[:, :, 5], in_=sqall[:]))
    co.vpad(ph)
    ph.emit("vector", lambda e: e.tensor_scalar(
        out=meanall[:], in0=sumall[:], scalar1=m[:, MI[1.0 / D]:MI[1.0 / D] + 1],
        scalar2=None, op0=ALU.mult), wait=[(ss, ss.n)])
    co.vpad(ph)
    ph.emit("vector", lambda e: e.tensor_tensor(
        out=scr[:], in0=meanall[:], in1=meanall[:], op=ALU.mult))
    co.vpad(ph)
    ph.emit("vector", lambda e: e.scalar_tensor_tensor(
        out=varall[:], in0=sqall[:], scalar=m[:, MI[1.0 / D]:MI[1.0 / D] + 1],
        in1=scr[:], op0=ALU.mult, op1=ALU.subtract), inc=[(vv, 1)])
    ph.emit("scalar", lambda e: e.activation(out=rstdall[:], in_=varall[:], func=AF.Ln,
                                             bias=m[:, MI[1e-5]:MI[1e-5] + 1]),
            wait=[(vv, vv.n)])
    ph.emit("scalar", lambda e: e.activation(out=rstdall[:], in_=rstdall[:], func=AF.Exp,
                                             scale=m[:, MI[-0.5]:MI[-0.5] + 1]),
            inc=[(ss, 1)])
    co.vpad(ph)
    ph.emit("vector", lambda e: e.tensor_tensor(
        out=scr[:], in0=rstdall[:], in1=rstdall[:], op=ALU.mult), wait=[(ss, ss.n)])
    co.vpad(ph)
    ph.emit("vector", lambda e: e.tensor_tensor(
        out=scr[:], in0=scr[:], in1=varall[:], op=ALU.mult), inc=[(vv, 1)])
    co.vpad(ph)
    ph.emit("scalar", lambda e: e.activation(out=rrall[:], in_=scr[:], func=AF.Ln,
                                             bias=m[:, MI[1e-8]:MI[1e-8] + 1]),
            wait=[(vv, vv.n)])
    ph.emit("scalar", lambda e: e.activation(out=rrall[:], in_=rrall[:], func=AF.Exp,
                                             scale=m[:, MI[-0.5]:MI[-0.5] + 1]),
            inc=[(ss, 1)])
    batch_ss = ss.n
    ph.wait("vector", ss, batch_ss)
    if DEBUG_SCRATCH:
        dwb = co.sem("p1dwb")
        ph.emit("vector", lambda e: e.tensor_copy(out=dbg[:, :, 0], in_=meanall[:]))
        ph.emit("vector", lambda e: e.tensor_copy(out=dbg[:, :, 1], in_=varall[:]))
        ph.emit("vector", lambda e: e.tensor_copy(out=dbg[:, :, 2], in_=rstdall[:]))
        ph.emit("vector", lambda e: e.tensor_copy(out=dbg[:, :, 3], in_=sqall[:]))
        ph.emit("vector", lambda e: e.tensor_copy(out=dbg[:, :, 6], in_=rrall[:]),
                inc=[(vv, 1)])
        ph.emit("sync", lambda e: e.dma_start(out=io["dbg_d"], in_=dbg[:]),
                wait=[(vv, vv.n)], inc=[(dwb, 16)])
        ph.wait("sync", dwb, 16)

    for i in range(nt):
        par = i % 2
        x_t = xt[par]
        pt = pt_[par]
        ph.emit("sync", lambda e, i=i, par=par: e.dma_start(
            out=xt[par][:], in_=io["x"][i * 128:(i + 1) * 128, :]),
            wait=([(ss, batch_ss)] if i < 2 else None), inc=[(xld[par], 16)])
        ph.emit("vector", lambda e, x_t=x_t, i=i: e.tensor_scalar(
            out=xn[:], in0=x_t[:], scalar1=meanall[:, i:i + 1],
            scalar2=rstdall[:, i:i + 1], op0=ALU.subtract, op1=ALU.mult),
            wait=[(xld[par], xld[par].n)])
        ph.emit("vector", lambda e, i=i: e.tensor_scalar(
            out=nrb[:], in0=xn[:], scalar1=rrall[:, i:i + 1], scalar2=None,
            op0=ALU.mult))
        ph.emit("vector", lambda e: e.tensor_copy(out=xnb[:], in_=xn[:]), inc=[(vv, 1)])
        for kc in range(8):
            ph.emit("tensor", lambda e, kc=kc, pt=pt: e.transpose(
                pt[:, kc * 128:(kc + 1) * 128], xnb[:, kc * 128:(kc + 1) * 128],
                c["id128b"][:]),
                wait=([(vv, vv.n)] if kc == 0 else None),
                inc=([(tt, 1)] if kc == 7 else None))
        ph.emit("vector", lambda e, pt=pt, par=par: e.tensor_copy(
            out=fmx[par][:].rearrange("p a b -> p (a b)"), in_=pt[:]),
            wait=[(tt, tt.n), (wbx[par], wbx[par].n)], inc=[(vv, 1)])
        for kc in range(8):
            ph.emit("tensor", lambda e, kc=kc, pt=pt: e.transpose(
                pt[:, kc * 128:(kc + 1) * 128], nrb[:, kc * 128:(kc + 1) * 128],
                c["id128b"][:]),
                wait=([(vv, vv.n)] if kc == 0 else None),
                inc=([(tt, 1)] if kc == 7 else None))
        for kc in range(8):
            ph.emit("vector", lambda e, pt=pt, par=par, kc=kc: e.tensor_scalar(
                out=fmn[par][:, kc, :], in0=pt[:, kc * 128:(kc + 1) * 128],
                scalar1=nwb[:, kc:kc + 1], scalar2=None, op0=ALU.mult),
                wait=([(tt, tt.n), (wbn[par], wbn[par].n)] if kc == 0 else None),
                inc=([(vv, 1)] if kc == 7 else None))
        ph.emit("sync", lambda e, i=i, par=par: e.dma_start(
            out=io["xn_d"][:, :, i * 128:(i + 1) * 128].rearrange("a p b -> p a b"),
            in_=fmx[par][:]), wait=[(vv, vv.n - 1)], inc=[(wbx[par], 16)])
        ph.emit("sync", lambda e, i=i, par=par: e.dma_start(
            out=io["nr_d"][:, :, i * 128:(i + 1) * 128].rearrange("a p b -> p a b"),
            in_=fmn[par][:]), wait=[(vv, vv.n)], inc=[(wbn[par], 16)])
    for s in wbx + wbn:
        ph.wait("sync", s, s.n)
    return ph



def build_p2(co, io, c, es):
    L = co.L
    ph = Phase("p2")
    if DEBUG_SYNC:
        ph.auto = {e_: co.sem("p2_a" + e_) for e_ in Phase.ENGINES}
    m = c["misc"]
    TILE = 256
    ntile = L // TILE

    wld = co.sem("p2wld")
    xnl = [co.sem("p2xn0"), co.sem("p2xn1")]
    swb = [co.sem("p2swb0"), co.sem("p2swb1")]
    tv = co.sem("p2tv")
    vt = co.sem("p2vt")
    st = co.sem("p2st")

    wq = co.sb(es, "p2_wq", [128, 8, 1024], BF16)
    wk = co.sb(es, "p2_wk", [128, 8, 1024], BF16)
    wv = co.sb(es, "p2_wv", [128, 8, 1024], BF16)
    wosb = co.sb(es, "p2_wo", [128, 8, 1024], BF16)
    g1 = co.sb(es, "p2_g1", [128, 8, 2048], BF16)
    g1s = co.sb(es, "p2_g1s", [32, 2048], BF16)
    for sb_t, nm in ((wq, "wqT"), (wk, "wkT"), (wv, "wvT"), (wosb, "woT"), (g1, "g1T")):
        ph.emit("gpsimd", lambda e, o=sb_t, i=io[nm]: e.dma_start(
            out=o[:], in_=i.rearrange("(kc p) mm -> p kc mm", p=128)), inc=[(wld, 16)])
    ph.emit("gpsimd", lambda e: e.dma_start(out=g1s[:], in_=io["g1Ts"]), inc=[(wld, 16)])

    xnt = [co.sb(es, f"p2_xn{i}", [128, 8, TILE], BF16) for i in range(2)]
    qf = co.sb(es, "p2_qf", [128, 8, TILE], BF16)
    kf = co.sb(es, "p2_kf", [128, 8, TILE], BF16)
    vf = co.sb(es, "p2_vf", [128, 8, 64 + TILE], BF16)
    pre = co.sb(es, "p2_pre", [128, 2, TILE + 3], F32)
    cvx = co.sb(es, "p2_cvx", [128, 2, TILE], F32)
    usb = cvx[:, 0, :]
    lgF = cvx[0:16, 1, :]
    halos = {nm: co.sb(es, f"p2_h{nm}", [128, 8, 3], F32) for nm in "qkv"}

    bP = co.sb(es, "p2_bP", [4, TILE], F32)
    qnP = co.sb(es, "p2_qnP", [4, TILE], F32)
    knP = co.sb(es, "p2_knP", [4, TILE], F32)
    brnP = co.sb(es, "p2_brnP", [4, TILE], F32)
    qncol = co.sb(es, "p2_qnc", [128, 2, 4], F32)
    kncol = co.sb(es, "p2_knc", [128, 2, 4], F32)
    bcol = co.sb(es, "p2_bc", [128, 2, 4], F32)
    brncol = co.sb(es, "p2_brc", [128, 2, 4], F32)
    kntm = co.sb(es, "p2_kntm", [128, 2, 4, 256], BF16)
    kbtm = co.sb(es, "p2_kbtm", [128, 2, 4, 256], BF16)
    vbtm = co.sb(es, "p2_vbtm", [128, 2, 4, 256], BF16)
    vtmT = co.sb(es, "p2_vtm", [128, 2, 8, 128], BF16)
    dtm = co.sb(es, "p2_dtm", [128, 2, 8, 128], BF16)
    stmT = co.sb(es, "p2_stm", [128, 2, 8, 128], BF16)
    ltmT = co.sb(es, "p2_ltm", [128, 2, 8, 128], BF16)
    Sf = co.sb(es, "p2_Sf", [128, 2, 4, 256], F32)
    Sb = co.sb(es, "p2_Sb", [128, 2, 4, 256], BF16)
    firfm = co.sb(es, "p2_firfm", [128, 2, 8, TILE], BF16)
    SF_ = firfm[:, 0, :, :]
    LF_ = firfm[:, 1, :, :]
    qsq = firfm[:, 0, :, :]
    Nb = co.sb(es, "p2_Nb", [CDEV, CDEV], F32)
    Npb = co.sb(es, "p2_Npb", [CDEV, CDEV], F32)
    nt1 = co.sb(es, "p2_nt1", [CDEV, CDEV], F32)
    Msl = co.sb(es, "p2_Msl", [CDEV, CDEV], F32)
    Mpsl = co.sb(es, "p2_Mpsl", [CDEV, CDEV], F32)
    TPf = co.sb(es, "p2_TPf", [CDEV, CDEV], F32)
    TPb = co.sb(es, "p2_TPb", [CDEV, CDEV], BF16)
    attnb = co.sb(es, "p2_attnb", [CDEV, CDEV], BF16)

    ub = co.sb(es, "p2_ub", [CDEV, 256], BF16)
    Wfm = co.sb(es, "p2_Wfm", [128, 2, CDEV], BF16)
    stats = co.sb(es, "p2_stats", [128, 2, 32], F32)
    statsb = co.sb(es, "p2_statsb", [128, 32], BF16)
    statsF = co.sb(es, "p2_stF", [32, 2, 128], BF16)
    sjnk = co.sb(es, "p2_sjnk", [128, 256], F32)
    ssum = co.sb(es, "p2_ssum", [128, 2, 16], F32)
    ssq2 = co.sb(es, "p2_ssq2", [128, 2, 16], F32)
    msq = co.sb(es, "p2_msq", [128, 2, 16], F32)


    lgt = co.sb(es, "p2_lgt", [128, 16], F32)
    mx4 = co.sb(es, "p2_mx4", [128, 4], F32)
    sm4 = co.sb(es, "p2_sm4", [128, 4], F32)
    rsm4 = co.sb(es, "p2_rsm4", [128, 4], F32)
    exv = co.sb(es, "p2_exv", [128, 16], F32)
    prb = co.sb(es, "p2_prb", [128, 2, 16], F32)

    rms4 = co.sb(es, "p2_rms4", [128, 4], F32)
    rr4 = co.sb(es, "p2_rr4", [128, 4], F32)

    cmb2 = co.sb(es, "p2_cmb2", [128, 8, 128], BF16)
    cmb = kntm[:].rearrange("p a b c -> p (a b c)").bitcast(F32).rearrange(
        "p (a b) -> p a b", a=8)
    cfm = kbtm[:].rearrange("p a b c -> p (a b c)").rearrange(
        "p (a b c) -> p a b c", a=8, b=2)
    sf = [co.sb(es, "p2_sf0", [128, 8, TILE], BF16)]
    sf = [sf[0], sf[0]]

    mark = co.sb(es, "p2_mark", [1, 4], F32)
    pA = co.ps(es, "p2_pA", [128, 512], F32)
    pB = co.ps(es, "p2_pB", [128, 512], F32)
    pN = co.ps(es, "p2_pN", [128, 512], F32)
    pT = co.ps(es, "p2_pT", [128, 512], F32)
    pU = co.ps(es, "p2_pU", [128, 512], F32)
    pW = co.ps(es, "p2_pW", [128, 512], F32)
    pS = co.ps(es, "p2_pS", [128, 512], F32)
    ptp = co.ps(es, "p2_pt", [128, 1024], BF16)

    guard = {}

    def mmg(bank, mms, extra_wait=None, inc=True):
        w = [guard[id(bank)]] if id(bank) in guard else []
        if extra_wait:
            w += extra_wait
        for j, f in enumerate(mms):
            ph.emit("tensor", f, wait=(w if j == 0 else None),
                    inc=([(tv, 1)] if (inc and j == len(mms) - 1) else None))
        return tv.n

    def ev(eng, f, bank=None, wait=None, sem=None):
        sem = sem or (vt if eng == "vector" else st)
        ph.emit(eng, f, wait=wait, inc=[(sem, 1)])
        if bank is not None:
            guard[id(bank)] = (sem, sem.n)
        return sem.n

    ph.emit("vector", lambda e: e.memset(Sf[:].rearrange("p a b c -> p (a b c)"), 0.0))
    ph.emit("vector", lambda e: e.memset(Sb[:].rearrange("p a b c -> p (a b c)"), 0.0))
    ph.emit("vector", lambda e: e.memset(vf[:, :, 0:64], 0.0))
    for nm in "qkv":
        ph.emit("vector", lambda e, t=halos[nm]: e.memset(
            t[:].rearrange("p a b -> p (a b)"), 0.0))
    ph.emit("vector", lambda e: e.memset(mark[:], 0.0), inc=[(vt, 1)])
    halo_vt = vt.n

    for ti in range(ntile):
        par = ti % 2
        t0 = ti * TILE
        xb = xnt[par]
        sfb = sf[par]
        ph.emit("sync", lambda e, t0=t0, xb=xb: e.dma_start(
            out=xb[:], in_=io["xn_d"][:, :, t0:t0 + TILE].rearrange("a p b -> p a b")),
            inc=[(xnl[par], 16)])
        wxn = [(xnl[par], xnl[par].n), (wld, wld.n)]

        bv = mmg(pA, [(lambda e, kc=kc, xb=xb: e.matmul(
            pA[0:4, 0:TILE], c["wbT"][:, kc, :], xb[:, kc, :],
            start=(kc == 0), stop=(kc == 7))) for kc in range(8)], extra_wait=wxn)
        ev("scalar", lambda e: e.activation(out=bP[:], in_=pA[0:4, 0:TILE],
                                            func=AF.Sigmoid), bank=pA, wait=[(tv, bv)])

        silu_st = [0, 0]
        conv_done = [0, 0]
        for (wT, cwn, hk, wrdst) in ((wq, "qcw", "q", ("qf", 0)),
                                     (wk, "kcw", "k", ("kf", 0)),
                                     (wv, "vcw", "v", ("vf", 64))):
            halo = halos[hk]
            cw = c[cwn]
            dstn, doff = wrdst
            dst = {"qf": qf, "kf": kf, "vf": vf}[dstn]
            for ot in range(8):
                bank = pA if ot % 2 == 0 else pB
                cpar = ot % 2
                gv = mmg(bank, [(lambda e, kc=kc, ot=ot, wT=wT, bank=bank, xb=xb: e.matmul(
                    bank[:, 0:TILE], wT[:, kc, ot * 128:(ot + 1) * 128], xb[:, kc, :],
                    start=(kc == 0), stop=(kc == 7))) for kc in range(8)],
                    extra_wait=wxn)
                w_pre = [(vt, conv_done[cpar] or halo_vt)]
                ev("scalar", lambda e, ot=ot, bank=bank, cpar=cpar: e.copy(
                    pre[:, cpar, 3:3 + TILE], bank[:, 0:TILE]), bank=bank,
                    wait=[(tv, gv)] + w_pre)
                ph.emit("scalar", lambda e, ot=ot, halo=halo, cpar=cpar: e.copy(
                    pre[:, cpar, 0:3], halo[:, ot, :]), inc=[(st, 1)])
                w_slot = [(st, st.n)] + ([(st, silu_st[cpar])] if silu_st[cpar] else [])
                ph.emit("vector", lambda e, ot=ot, cw=cw, cpar=cpar: e.tensor_scalar(
                    out=cvx[:, cpar, :], in0=pre[:, cpar, 0:TILE], scalar1=cw[:, ot, 0:1],
                    scalar2=None, op0=ALU.mult), wait=w_slot)
                for j in (1, 2, 3):
                    ph.emit("vector", lambda e, ot=ot, j=j, cw=cw, cpar=cpar:
                            e.scalar_tensor_tensor(
                        out=cvx[:, cpar, :], in0=pre[:, cpar, j:j + TILE],
                        scalar=cw[:, ot, j:j + 1],
                        in1=cvx[:, cpar, :], op0=ALU.mult, op1=ALU.add))
                ph.emit("vector", lambda e, ot=ot, halo=halo, cpar=cpar: e.tensor_copy(
                    out=halo[:, ot, :], in_=pre[:, cpar, TILE:TILE + 3]), inc=[(vt, 1)])
                conv_done[cpar] = vt.n
                ph.emit("scalar", lambda e, ot=ot, dst=dst, doff=doff, cpar=cpar:
                        e.activation(
                    out=dst[:, ot, doff:doff + TILE], in_=cvx[:, cpar, :], func=AF.Silu),
                    wait=[(vt, vt.n)], inc=[(st, 1)])
                silu_st[cpar] = st.n
        qkv_st = st.n
        halo_vt = vt.n

        ph.emit("scalar", lambda e: e.activation(out=qsq.rearrange("p a b -> p (a b)"),
                                                 in_=qf[:].rearrange("p a b -> p (a b)"),
                                                 func=AF.Square), inc=[(st, 1)])
        sq_q = st.n
        qmm = []
        for h in range(4):
            for j in range(2):
                idx = len(qmm)
                qmm.append(lambda e, h=h, j=j, first=(idx == 0), last=(idx == 7): e.matmul(
                    pN[0:4, 0:TILE], c["nsel"][:, h, :], qsq[:, 2 * h + j, :],
                    start=first, stop=last))
        nv = mmg(pN, qmm, extra_wait=[(st, sq_q)])
        ev("scalar", lambda e: e.activation(out=qnP[:], in_=pN[0:4, 0:TILE], func=AF.Ln,
                                            bias=m[0:4, MI[1e-6]:MI[1e-6] + 1]),
           wait=[(tv, nv)])
        ph.emit("scalar", lambda e: e.activation(out=qnP[:], in_=qnP[:], func=AF.Exp,
                                                 scale=m[0:4, MI[-0.5]:MI[-0.5] + 1]))
        ph.emit("scalar", lambda e: e.activation(out=qsq.rearrange("p a b -> p (a b)"),
                                                 in_=kf[:].rearrange("p a b -> p (a b)"),
                                                 func=AF.Square), inc=[(st, 1)])
        sq_k = st.n
        kmm = []
        for h in range(4):
            for j in range(2):
                idx = len(kmm)
                kmm.append(lambda e, h=h, j=j, first=(idx == 0), last=(idx == 7): e.matmul(
                    pN[0:4, TILE:2 * TILE], c["nsel"][:, h, :], qsq[:, 2 * h + j, :],
                    start=first, stop=last))
        nv2 = mmg(pN, kmm, extra_wait=[(st, sq_k)])
        ev("scalar", lambda e: e.activation(out=knP[:], in_=pN[0:4, TILE:2 * TILE],
                                            func=AF.Ln,
                                            bias=m[0:4, MI[1e-6]:MI[1e-6] + 1]),
           bank=pN, wait=[(tv, nv2)])
        ph.emit("scalar", lambda e: e.activation(out=knP[:], in_=knP[:], func=AF.Exp,
                                                 scale=m[0:4, MI[-0.5]:MI[-0.5] + 1]),
                inc=[(st, 1)])
        rows_st = st.n
        ph.emit("vector", lambda e: e.tensor_tensor(out=brnP[:], in0=bP[:], in1=knP[:],
                                                    op=ALU.mult),
                wait=[(st, rows_st)], inc=[(vt, 1)])
        rows_vt = vt.n
        colmap = ((qnP, qncol), (knP, kncol), (bP, bcol), (brnP, brncol))
        for ca in range(2):
            for qi, (rowt, colt) in enumerate(colmap):
                cvv = mmg(pN, [lambda e, rowt=rowt, ca=ca, qi=qi: e.transpose(
                    pN[:, 256 + 16 * (4 * ca + qi):256 + 16 * (4 * ca + qi) + 4],
                    rowt[0:4, ca * 128:(ca + 1) * 128], c["id4f"][:])],
                    extra_wait=[(vt, rows_vt), (st, rows_st)])
        cols_tv = tv.n
        for ca in range(2):
            for qi, (rowt, colt) in enumerate(colmap):
                ev("scalar", lambda e, ca=ca, qi=qi, colt=colt: e.copy(
                    colt[:, ca, :],
                    pN[:, 256 + 16 * (4 * ca + qi):256 + 16 * (4 * ca + qi) + 4]),
                    bank=(pN if (ca == 1 and qi == 3) else None),
                    wait=[(tv, cols_tv)])
        cols_st = st.n

        for ca in range(2):
            ktv = mmg(ptp, [(lambda e, kc=kc, ca=ca: e.transpose(
                ptp[:, kc * 128:(kc + 1) * 128], kf[:, kc, ca * 128:(ca + 1) * 128],
                c["id128b"][:])) for kc in range(8)],
                extra_wait=[(st, max(qkv_st, cols_st))])
            for h in range(4):
                ph.emit("vector", lambda e, ca=ca, h=h: e.tensor_scalar(
                    out=kntm[:, ca, h, :],
                    in0=ptp[:, 2 * h * 128:(2 * h + 2) * 128], scalar1=kncol[:, ca, h:h + 1],
                    scalar2=None, op0=ALU.mult), wait=[(tv, ktv)])
                ph.emit("vector", lambda e, ca=ca, h=h: e.tensor_scalar(
                    out=kbtm[:, ca, h, :], in0=ptp[:, 2 * h * 128:(2 * h + 2) * 128],
                    scalar1=kncol[:, ca, h:h + 1], scalar2=bcol[:, ca, h:h + 1],
                    op0=ALU.mult, op1=ALU.mult))
            ph.emit("vector", lambda e: e.memset(mark[:], 0.0), inc=[(vt, 1)])
            guard[id(ptp)] = (vt, vt.n)
            vtv = mmg(ptp, [(lambda e, kc=kc, ca=ca: e.transpose(
                ptp[:, kc * 128:(kc + 1) * 128], vf[:, kc, 64 + ca * 128:64 + (ca + 1) * 128],
                c["id128b"][:])) for kc in range(8)])
            for h in range(4):
                ph.emit("vector", lambda e, ca=ca, h=h: e.tensor_scalar(
                    out=vbtm[:, ca, h, :], in0=ptp[:, 2 * h * 128:(2 * h + 2) * 128],
                    scalar1=bcol[:, ca, h:h + 1], scalar2=None, op0=ALU.mult),
                    wait=[(tv, vtv)])
            ph.emit("vector", lambda e, ca=ca: e.tensor_copy(
                out=vtmT[:, ca, :, :].rearrange("p a b -> p (a b)"), in_=ptp[:]),
                inc=[(vt, 1)])
            guard[id(ptp)] = (vt, vt.n)

            for h in range(4):
                gv1 = mmg(pN, [(lambda e, j=j, ca=ca, h=h: e.matmul(
                    pN[:, 0:128], kf[:, 2 * h + j, ca * 128:(ca + 1) * 128],
                    kf[:, 2 * h + j, ca * 128:(ca + 1) * 128],
                    start=(j == 0), stop=(j == 1))) for j in range(2)], inc=False)
                gv2 = mmg(pN, [(lambda e, j=j, ca=ca, h=h: e.matmul(
                    pN[:, 128:256], kf[:, 2 * h + j, ca * 128:(ca + 1) * 128],
                    qf[:, 2 * h + j, ca * 128:(ca + 1) * 128],
                    start=(j == 0), stop=(j == 1))) for j in range(2)], inc=False)
                gv3 = mmg(pN, [lambda e, ca=ca, h=h: e.matmul(
                    pN[:, 256:384], c["sel4"][:, h, :], knP[0:4, ca * 128:(ca + 1) * 128],
                    start=True, stop=True)], inc=False)
                gv4 = mmg(pN, [lambda e, ca=ca, h=h: e.matmul(
                    pN[:, 384:512], c["sel4"][:, h, :], brnP[0:4, ca * 128:(ca + 1) * 128],
                    start=True, stop=True)])
                gv = tv.n
                ph.emit("vector", lambda e, ca=ca, h=h: e.scalar_tensor_tensor(
                    out=nt1[:], in0=pN[:, 0:128], scalar=brncol[:, ca, h:h + 1],
                    in1=c["mLSn"][:], op0=ALU.mult, op1=ALU.mult), wait=[(tv, gv)])
                ph.emit("vector", lambda e, ca=ca, h=h: e.scalar_tensor_tensor(
                    out=attnb[:], in0=pN[:, 128:256], scalar=kncol[:, ca, h:h + 1],
                    in1=c["mUI"][:], op0=ALU.mult, op1=ALU.mult))
                ph.emit("vector", lambda e: e.tensor_tensor(
                    out=Nb[:], in0=nt1[:], in1=pN[:, 256:384], op=ALU.mult))
                ph.emit("vector", lambda e, ca=ca, h=h: e.scalar_tensor_tensor(
                    out=nt1[:], in0=pN[:, 0:128], scalar=kncol[:, ca, h:h + 1],
                    in1=c["mUSn"][:], op0=ALU.mult, op1=ALU.mult))
                co.vpad(ph, 1)
                ph.emit("vector", lambda e: e.tensor_tensor(
                    out=Npb[:], in0=nt1[:], in1=pN[:, 384:512], op=ALU.mult))
                co.vpad(ph, 1)
                ph.emit("vector", lambda e: e.tensor_tensor(
                    out=TPf[:], in0=Npb[:], in1=c["id128f"][:], op=ALU.add),
                        inc=[(vt, 1)])
                guard[id(pN)] = (vt, vt.n)
                nb_vt = vt.n
                curM = lambda: Nb[:]
                curMp = lambda: Npb[:]
                sq_wait = nb_vt
                for lev in range(6):
                    sq_tv = mmg(pU, [
                        lambda e, curM=curM, curMp=curMp: e.matmul(
                            pU[:, 0:128], curMp(), curM(), start=True, stop=True),
                        lambda e, curM=curM, curMp=curMp: e.matmul(
                            pU[:, 128:256], curM(), curMp(), start=True, stop=True)],
                        extra_wait=[(vt, sq_wait)])
                    ph.emit("vector", lambda e: e.tensor_copy(
                        out=Msl[:], in_=pU[:, 0:128]), wait=[(tv, sq_tv)])
                    ph.emit("vector", lambda e: e.tensor_copy(
                        out=Mpsl[:], in_=pU[:, 128:256]), inc=[(vt, 1)])
                    mev = vt.n
                    guard[id(pU)] = (vt, mev)
                    acc_tv = mmg(pT, [lambda e: e.matmul(
                        pT[:, 256:384], Msl[:], TPf[:], start=True, stop=True)],
                        extra_wait=[(vt, mev)])
                    ph.emit("vector", lambda e: e.tensor_tensor(
                        out=TPf[:], in0=TPf[:], in1=pT[:, 256:384], op=ALU.add),
                        inc=[(vt, 1)], wait=[(tv, acc_tv)])
                    guard[id(pT)] = (vt, vt.n)
                    sq_wait = mev
                    curM = lambda: Msl[:]
                    curMp = lambda: Mpsl[:]
                nb_vt = vt.n
                co.vpad(ph, 1)
                ph.emit("vector", lambda e: e.tensor_copy(out=TPb[:], in_=TPf[:]),
                        inc=[(vt, 1)])
                tb_vt = vt.n
                u_tv = mmg(pU, [lambda e, ca=ca, h=h: e.matmul(
                    pU[:, 0:256], TPb[:], vbtm[:, ca, h, :], start=True, stop=True)],
                    extra_wait=[(vt, tb_vt)], inc=False)
                w_tv = mmg(pW, [
                    lambda e, ca=ca, h=h: e.matmul(
                        pW[:, 0:128], kbtm[:, ca, h, 0:128], TPb[:], start=True, stop=True),
                    lambda e, ca=ca, h=h: e.matmul(
                        pW[:, 128:256], kbtm[:, ca, h, 128:256], TPb[:],
                        start=True, stop=True)])
                ph.emit("vector", lambda e: e.tensor_copy(out=usb[:], in_=pU[:, 0:256]),
                        wait=[(tv, w_tv)])
                ph.emit("vector", lambda e: e.tensor_copy(
                    out=Wfm[:].rearrange("p a b -> p (a b)"), in_=pW[:, 0:256]),
                    inc=[(vt, 1)])
                guard[id(pU)] = (vt, vt.n)
                guard[id(pW)] = (vt, vt.n)
                wev = vt.n
                ws_tv = mmg(pW, [(lambda e, kc=kc, h=h: e.matmul(
                    pW[:, 256:512], Wfm[:, kc, :], Sb[:, kc, h, :],
                    start=(kc == 0), stop=(kc == 1))) for kc in range(2)],
                    extra_wait=[(vt, wev)])
                ph.emit("vector", lambda e: e.tensor_tensor(
                    out=ub[:], in0=usb[:], in1=pW[:, 256:512], op=ALU.subtract),
                    wait=[(tv, ws_tv)], inc=[(vt, 1)])
                guard[id(pW)] = (vt, vt.n)
                ub_vt = vt.n
                o_tv = mmg(pU, [
                    lambda e, ca=ca, h=h: e.matmul(
                        pU[:, 256:512], qf[:, 2 * h, ca * 128:(ca + 1) * 128],
                        Sb[:, 0, h, :], start=True, stop=False),
                    lambda e, ca=ca, h=h: e.matmul(
                        pU[:, 256:512], qf[:, 2 * h + 1, ca * 128:(ca + 1) * 128],
                        Sb[:, 1, h, :], start=False, stop=False),
                    lambda e: e.matmul(
                        pU[:, 256:512], attnb[:], ub[:], start=False, stop=True)],
                    extra_wait=[(vt, ub_vt)])
                ph.emit("vector", lambda e, ca=ca, h=h: e.tensor_scalar(
                    out=dtm[:, ca, 2 * h:2 * h + 2, :].rearrange("p a b -> p (a b)"),
                    in0=pU[:, 256:512], scalar1=qncol[:, ca, h:h + 1], scalar2=None,
                    op0=ALU.mult), wait=[(tv, o_tv)], inc=[(vt, 1)])
                guard[id(pU)] = (vt, vt.n)
                s_tv = mmg(pS, [
                    lambda e, ca=ca, h=h: e.matmul(
                        pS[:, 0:256], kntm[:, ca, h, 0:128], ub[:], start=True, stop=True),
                    lambda e, ca=ca, h=h: e.matmul(
                        pS[:, 256:512], kntm[:, ca, h, 128:256], ub[:],
                        start=True, stop=True)], extra_wait=[(vt, ub_vt)])
                for kc in range(2):
                    ph.emit("vector", lambda e, kc=kc, h=h: e.tensor_tensor(
                        out=Sf[:, kc, h, :], in0=Sf[:, kc, h, :],
                        in1=pS[:, kc * 256:(kc + 1) * 256], op=ALU.add),
                        wait=[(tv, s_tv)])
                for kc in range(2):
                    ph.emit("vector", lambda e, kc=kc, h=h: e.tensor_copy(
                        out=Sb[:, kc, h, :], in_=Sf[:, kc, h, :]),
                        inc=([(vt, 1)] if kc == 1 else None))
                guard[id(pS)] = (vt, vt.n)

        fir_wait = [(st, qkv_st), (tv, tv.n)]
        for kc in range(8):
            ph.emit("vector", lambda e, kc=kc: e.tensor_scalar(
                out=SF_[:, kc, :], in0=vf[:, kc, 62:62 + TILE],
                scalar1=c["fsw"][:, kc, 0:1], scalar2=None, op0=ALU.mult),
                wait=(fir_wait if kc == 0 else None))
        for j in (1, 2):
            for kc in range(8):
                ph.emit("vector", lambda e, kc=kc, j=j: e.scalar_tensor_tensor(
                    out=SF_[:, kc, :], in0=vf[:, kc, 62 + j:62 + j + TILE],
                    scalar=c["fsw"][:, kc, j:j + 1], in1=SF_[:, kc, :],
                    op0=ALU.mult, op1=ALU.add))
        for kc in range(8):
            ph.emit("vector", lambda e, kc=kc: e.tensor_scalar(
                out=LF_[:, kc, :], in0=vf[:, kc, 2:2 + TILE],
                scalar1=c["flw"][:, kc, 0:1], scalar2=None, op0=ALU.mult))
        for j in range(1, 63):
            for kc in range(8):
                ph.emit("vector", lambda e, kc=kc, j=j: e.scalar_tensor_tensor(
                    out=LF_[:, kc, :], in0=vf[:, kc, 2 + j:2 + j + TILE],
                    scalar=c["flw"][:, kc, j:j + 1], in1=LF_[:, kc, :],
                    op0=ALU.mult, op1=ALU.add))
        ph.emit("vector", lambda e: e.tensor_copy(
            out=vf[:, :, 0:64], in_=vf[:, :, TILE:TILE + 64]), inc=[(vt, 1)])
        fir_vt = vt.n

        for ca in range(2):
            for (src, dstt) in ((SF_, stmT), (LF_, ltmT)):
                ptv = mmg(ptp, [(lambda e, kc=kc, ca=ca, src=src: e.transpose(
                    ptp[:, kc * 128:(kc + 1) * 128], src[:, kc, ca * 128:(ca + 1) * 128],
                    c["id128b"][:])) for kc in range(8)],
                    extra_wait=[(vt, fir_vt)])
                ev("vector", lambda e, ca=ca, dstt=dstt: e.tensor_copy(
                    out=dstt[:, ca, :, :].rearrange("p a b -> p (a b)"), in_=ptp[:]),
                    bank=ptp, wait=[(tv, ptv)])
        paths_vt = vt.n

        for ca in range(2):
            for pi, patht in enumerate((stmT, ltmT, dtm, vtmT)):
                for h in range(4):
                    col = pi * 4 + h
                    ph.emit("scalar", lambda e, ca=ca, h=h, patht=patht, col=col:
                            e.activation(
                        out=sjnk[:], in_=patht[:, ca, 2 * h:2 * h + 2, :]
                        .rearrange("p a b -> p (a b)"), func=AF.Identity,
                        accum_out=ssum[:, ca, col:col + 1]), wait=[(vt, paths_vt)])
                    ph.emit("scalar", lambda e, ca=ca, h=h, patht=patht, col=col:
                            e.activation(
                        out=sjnk[:], in_=patht[:, ca, 2 * h:2 * h + 2, :]
                        .rearrange("p a b -> p (a b)"), func=AF.Square,
                        accum_out=ssq2[:, ca, col:col + 1]))
            ph.emit("scalar", lambda e, ca=ca: e.activation(
                out=sjnk[:, 0:16], in_=ssum[:, ca, :], func=AF.Identity))
            ph.emit("scalar", lambda e, ca=ca: e.activation(
                out=sjnk[:, 0:16], in_=ssq2[:, ca, :], func=AF.Identity),
                inc=[(st, 1)])
            ph.emit("vector", lambda e, ca=ca: e.tensor_scalar(
                out=stats[:, ca, 0:32:2], in0=ssum[:, ca, :],
                scalar1=m[:, MI[1.0 / DK]:MI[1.0 / DK] + 1], scalar2=None,
                op0=ALU.mult), wait=[(st, st.n)], inc=[(vt, 1)])
            ph.emit("scalar", lambda e, ca=ca: e.activation(
                out=msq[:, ca, :], in_=stats[:, ca, 0:32:2], func=AF.Square),
                wait=[(vt, vt.n)], inc=[(st, 1)])
            ph.emit("vector", lambda e, ca=ca: e.scalar_tensor_tensor(
                out=stats[:, ca, 1:32:2], in0=ssq2[:, ca, :],
                scalar=m[:, MI[1.0 / DK]:MI[1.0 / DK] + 1], in1=msq[:, ca, :],
                op0=ALU.mult, op1=ALU.subtract), wait=[(st, st.n)])
        co.vpad(ph, 1)
        ph.emit("vector", lambda e: e.memset(mark[:], 0.0), inc=[(vt, 1)])
        stats_vt = vt.n
        statsb2 = statsb
        for ca in range(2):
            prev_tv = tv.n
            ph.emit("vector", lambda e, ca=ca: e.tensor_copy(
                out=statsb2[:], in_=stats[:, ca, :]),
                wait=[(vt, stats_vt), (tv, prev_tv)], inc=[(vt, 1)])
            stv = mmg(ptp, [lambda e: e.transpose(ptp[0:32, 0:128], statsb2[:],
                                                  c["id128b"][:])],
                      extra_wait=[(vt, vt.n)])
            ev("vector", lambda e, ca=ca: e.tensor_copy(
                out=statsF[0:32, ca, :], in_=ptp[0:32, 0:128]), bank=ptp,
                wait=[(tv, stv)])
        stats_f_vt = vt.n

        hmid = firfm[:].rearrange("p f a b -> p (f a) b")
        for mt in range(16):
            bank = pA if mt % 2 == 0 else pB
            mms = [(lambda e, kc=kc, mt=mt, bank=bank, xb=xb: e.matmul(
                bank[:, 0:TILE], g1[:, kc, mt * 128:(mt + 1) * 128], xb[:, kc, :],
                start=(kc == 0), stop=False)) for kc in range(8)]
            mms.append(lambda e, mt=mt, bank=bank: e.matmul(
                bank[:, 0:TILE], g1s[:, mt * 128:(mt + 1) * 128],
                statsF[:].rearrange("p a b -> p (a b)"), start=False, stop=True))
            gv = mmg(bank, mms, extra_wait=[(vt, stats_f_vt)])
            ev("scalar", lambda e, mt=mt, bank=bank: e.activation(
                out=hmid[:, mt, :], in_=bank[:, 0:TILE], func=AF.Gelu,
                bias=c["gb1c"][:, mt:mt + 1]), bank=bank, wait=[(tv, gv)])
        hmid_st = st.n
        lmm = [(lambda e, kc=kc: e.matmul(
            pN[0:16, 0:TILE], c["g2T"][:, kc, :], hmid[:, kc, :],
            start=(kc == 0), stop=(kc == 15))) for kc in range(16)]
        lg_tv = mmg(pN, lmm, extra_wait=[(st, hmid_st)])
        ev("vector", lambda e: e.tensor_copy(out=lgF[:], in_=pN[0:16, 0:TILE]),
           bank=pN, wait=[(tv, lg_tv)])
        lg_vt = vt.n
        for ca in range(2):
            lt_tv = mmg(pU, [lambda e, ca=ca: e.transpose(
                pU[:, 0:16], lgF[0:16, ca * 128:(ca + 1) * 128], c["id16f"][:])],
                extra_wait=[(vt, lg_vt)])
            ph.emit("vector", lambda e: e.tensor_tensor(
                out=lgt[:], in0=pU[:, 0:16], in1=c["bbb"][:], op=ALU.add),
                wait=[(tv, lt_tv)], inc=[(vt, 1)])
            guard[id(pU)] = (vt, vt.n)
            co.vpad(ph, 1)
            ph.emit("vector", lambda e: e.tensor_tensor(
                out=lgt[:], in0=lgt[:], in1=c["itb"][:], op=ALU.mult))
            co.vpad(ph, 1)
            for h in range(4):
                ph.emit("vector", lambda e, h=h: e.reduce_max(
                    out=mx4[:, h:h + 1], in_=lgt[:, 4 * h:4 * h + 4],
                    axis=mybir.AxisListType.X))
            co.vpad(ph, 4)
            for h in range(4):
                ph.emit("vector", lambda e, h=h: e.tensor_scalar(
                    out=exv[:, 4 * h:4 * h + 4], in0=lgt[:, 4 * h:4 * h + 4],
                    scalar1=mx4[:, h:h + 1], scalar2=None, op0=ALU.subtract),
                    inc=([(vt, 1)] if h == 3 else None))
            ph.emit("scalar", lambda e: e.activation(out=exv[:], in_=exv[:], func=AF.Exp),
                    wait=[(vt, vt.n)], inc=[(st, 1)])
            ph.wait("vector", st, st.n)
            for h in range(4):
                ph.emit("vector", lambda e, h=h: e.reduce_sum(
                    out=sm4[:, h:h + 1], in_=exv[:, 4 * h:4 * h + 4],
                    axis=mybir.AxisListType.X), inc=([(vt, 1)] if h == 3 else None))
            ph.emit("scalar", lambda e: e.activation(out=rsm4[:], in_=sm4[:],
                                                     func=AF.Ln), wait=[(vt, vt.n)])
            ph.emit("scalar", lambda e: e.activation(
                out=rsm4[:], in_=rsm4[:], func=AF.Exp,
                scale=m[:, MI[-1.0]:MI[-1.0] + 1]), inc=[(st, 1)])
            ph.wait("vector", st, st.n)
            for h in range(4):
                ph.emit("vector", lambda e, h=h, ca=ca: e.scalar_tensor_tensor(
                    out=prb[:, ca, 4 * h:4 * h + 4], in0=exv[:, 4 * h:4 * h + 4],
                    scalar=rsm4[:, h:h + 1], in1=c["flb"][:, 4 * h:4 * h + 4],
                    op0=ALU.mult, op1=ALU.max))
            co.vpad(ph, 1)
            for h in range(4):
                ph.emit("vector", lambda e, h=h, ca=ca: e.reduce_sum(
                    out=sm4[:, h:h + 1], in_=prb[:, ca, 4 * h:4 * h + 4],
                    axis=mybir.AxisListType.X), inc=([(vt, 1)] if h == 3 else None))
            ph.emit("scalar", lambda e: e.activation(out=rsm4[:], in_=sm4[:],
                                                     func=AF.Ln), wait=[(vt, vt.n)])
            ph.emit("scalar", lambda e: e.activation(
                out=rsm4[:], in_=rsm4[:], func=AF.Exp,
                scale=m[:, MI[-1.0]:MI[-1.0] + 1]), inc=[(st, 1)])
            ph.wait("vector", st, st.n)
            co.vpad(ph, 1)
            for h in range(4):
                ph.emit("vector", lambda e, h=h, ca=ca: e.tensor_scalar(
                    out=prb[:, ca, 4 * h:4 * h + 4], in0=prb[:, ca, 4 * h:4 * h + 4],
                    scalar1=rsm4[:, h:h + 1], scalar2=None, op0=ALU.mult))
            co.vpad(ph, 1)
            for h in range(4):
                for pi, patht in enumerate((stmT, ltmT, dtm, vtmT)):
                    if pi == 0:
                        ph.emit("vector", lambda e, h=h, ca=ca, patht=patht: e.tensor_scalar(
                            out=cmb[:, 2 * h:2 * h + 2, :].rearrange("p a b -> p (a b)"),
                            in0=patht[:, ca, 2 * h:2 * h + 2, :].rearrange("p a b -> p (a b)"),
                            scalar1=prb[:, ca, 4 * h:4 * h + 1], scalar2=None,
                            op0=ALU.mult))
                    else:
                        ph.emit("vector", lambda e, h=h, ca=ca, pi=pi, patht=patht:
                                e.scalar_tensor_tensor(
                            out=cmb[:, 2 * h:2 * h + 2, :].rearrange("p a b -> p (a b)"),
                            in0=patht[:, ca, 2 * h:2 * h + 2, :].rearrange("p a b -> p (a b)"),
                            scalar=prb[:, ca, 4 * h + pi:4 * h + pi + 1],
                            in1=cmb[:, 2 * h:2 * h + 2, :].rearrange("p a b -> p (a b)"),
                            op0=ALU.mult, op1=ALU.add))
            ph.emit("vector", lambda e: e.memset(mark[:], 0.0), inc=[(vt, 1)])
            cmb_vt = vt.n
            for h in range(4):
                ph.emit("scalar", lambda e, h=h: e.activation(
                    out=sjnk[:], in_=cmb[:, 2 * h:2 * h + 2, :].rearrange("p a b -> p (a b)"),
                    func=AF.Square, accum_out=rms4[:, h:h + 1]), wait=[(vt, cmb_vt)])
            ph.emit("scalar", lambda e: e.activation(
                out=sjnk[:, 0:4], in_=rms4[:], func=AF.Identity))
            ph.emit("scalar", lambda e: e.activation(
                out=rr4[:], in_=rms4[:], func=AF.Ln,
                scale=m[:, MI[1.0 / DK]:MI[1.0 / DK] + 1],
                bias=m[:, MI[1e-5]:MI[1e-5] + 1]))
            ph.emit("scalar", lambda e: e.activation(
                out=rr4[:], in_=rr4[:], func=AF.Exp,
                scale=m[:, MI[-0.5]:MI[-0.5] + 1]), inc=[(st, 1)])
            for h in range(4):
                ph.emit("vector", lambda e, h=h: e.scalar_tensor_tensor(
                    out=cmb2[:, 2 * h:2 * h + 2, :].rearrange("p a b -> p (a b)"),
                    in0=cmb[:, 2 * h:2 * h + 2, :].rearrange("p a b -> p (a b)"),
                    scalar=rr4[:, h:h + 1],
                    in1=c["onb"][:], op0=ALU.mult, op1=ALU.mult),
                    wait=[(st, st.n)],
                    inc=([(vt, 1)] if h == 3 else None))
            cmb2_vt = vt.n
            ctv = mmg(ptp, [(lambda e, kc=kc: e.transpose(
                ptp[:, kc * 128:(kc + 1) * 128], cmb2[:, kc, :], c["id128b"][:]))
                for kc in range(8)], extra_wait=[(vt, cmb2_vt)])
            ev("vector", lambda e, ca=ca: e.tensor_copy(
                out=cfm[:, :, ca, :],
                in_=ptp[:].rearrange("p (a b) -> p a b", a=8)),
                bank=ptp, wait=[(tv, ctv)])
        cfm_vt = vt.n
        for ot in range(8):
            bank = pA if ot % 2 == 0 else pB
            gv = mmg(bank, [(lambda e, kc=kc, ot=ot, bank=bank: e.matmul(
                bank[:, 0:TILE], wosb[:, kc, ot * 128:(ot + 1) * 128],
                cfm[:, kc, :, :].rearrange("p a b -> p (a b)"),
                start=(kc == 0), stop=(kc == 7))) for kc in range(8)],
                extra_wait=[(vt, cfm_vt)])
            ev("vector", lambda e, ot=ot, bank=bank, sfb=sfb: e.tensor_copy(
                out=sfb[:, ot, :], in_=bank[:, 0:TILE]), bank=bank,
                wait=[(tv, gv), (swb[par], swb[par].n)])
        ph.emit("sync", lambda e, t0=t0, sfb=sfb: e.dma_start(
            out=io["s_d"][:, :, t0:t0 + TILE].rearrange("a p b -> p a b"), in_=sfb[:]),
            wait=[(vt, vt.n)], inc=[(swb[par], 16)])
    for s in swb:
        ph.wait("sync", s, s.n)
    return ph




def build_p3(co, io, c, es):
    L = co.L
    ph = Phase("p3")
    if DEBUG_SYNC:
        ph.auto = {e_: co.sem("p3_a" + e_) for e_ in Phase.ENGINES}
    m = c["misc"]
    TM = min(1024, L)
    nmega = L // TM
    NT = TM // 512
    NSUB = TM // 128

    tv = co.sem("p3tv")
    vt = co.sem("p3vt")
    st = co.sem("p3st")
    wl = [co.sem("p3wl0"), co.sem("p3wl1")]
    nrl = co.sem("p3nrl")
    ssl = co.sem("p3ssl")
    xl = [co.sem("p3xl0"), co.sem("p3xl1")]
    owb = [co.sem("p3ow0"), co.sem("p3ow1")]

    nrm = co.sb(es, "p3_nrm", [128, 8, TM], BF16)
    w1b = [co.sb(es, f"p3_w1{i}", [128, 8, 1024], BF16) for i in range(2)]
    w2b = [co.sb(es, f"p3_w2{i}", [128, 8, 1024], BF16) for i in range(2)]
    hq = co.sb(es, "p3_hq", [128, 8, TM], BF16)
    hr = co.sb(es, "p3_hr", [128, 2, 512], BF16)
    racc = co.sb(es, "p3_racc", [128, 8, TM], F32)
    cbc = co.sb(es, "p3_cbc", [128, TM], BF16)
    comb8 = co.sb(es, "p3_c8", [8, NSUB, 128], BF16)
    probs = co.sb(es, "p3_probs", [128, 8], F32)
    ge88 = co.sb(es, "p3_ge", [128, 8, 8], F32)
    cnt8 = co.sb(es, "p3_cnt", [128, 8], F32)
    den = co.sb(es, "p3_den", [128, 1], F32)
    rden = co.sb(es, "p3_rden", [128, 1], F32)
    cmbb = co.sb(es, "p3_cmbb", [128, 8], BF16)
    dbg3p = co.sb(es, "p3_d3p", [128, 4, 8, 32], F32) if DEBUG_P3 else None
    gfm = co.sb(es, "p3_gfm", [128, 8, TM], BF16)
    sslab = co.sb(es, "p3_ss", [128, 8, TM], BF16)
    xt3 = [co.sb(es, f"p3_x{i}", [128, 1024], F32) for i in range(2)]
    zt = co.sb(es, "p3_zt", [128, 2, 128], BF16)
    ot3 = [co.sb(es, f"p3_o{i}", [128, 1024], F32) for i in range(2)]

    pA = co.ps(es, "p3_pA", [128, 512], F32)
    pB = co.ps(es, "p3_pB", [128, 512], F32)
    pC = co.ps(es, "p3_pC", [128, 512], F32)
    pD = co.ps(es, "p3_pD", [128, 512], F32)
    pR = co.ps(es, "p3_pR", [128, 512], F32)
    ptp = co.ps(es, "p3_pt", [128, 1024], BF16)

    guard = {}

    def mmg(bank, mms, extra_wait=None, inc=True):
        w = [guard[id(bank)]] if id(bank) in guard else []
        if extra_wait:
            w += extra_wait
        if PARANOID:
            w = w + [(vt, vt.n), (st, st.n)]
        for j, f in enumerate(mms):
            ph.emit("tensor", f, wait=(w if j == 0 else None),
                    inc=([(tv, 1)] if (inc and j == len(mms) - 1) else None))
        return tv.n

    def ev(eng, f, bank=None, wait=None):
        sem = vt if eng == "vector" else st
        if PARANOID:
            wait = (wait or []) + [(tv, tv.n), (vt if eng != "vector" else st,
                                                (vt if eng != "vector" else st).n)]
        ph.emit(eng, f, wait=wait, inc=[(sem, 1)])
        if bank is not None:
            guard[id(bank)] = (sem, sem.n)
        return sem.n

    wfree = [0, 0]
    def wload(e, q, par):
        if e == 8:
            src1 = io["sw1"][:, q * 1024:(q + 1) * 1024]
            src2 = io["sw2"][q * 1024:(q + 1) * 1024, :]
        else:
            src1 = io["ew1"][e, :, q * 1024:(q + 1) * 1024]
            src2 = io["ew2"][e, q * 1024:(q + 1) * 1024, :]
        w = [(tv, wfree[par])] if wfree[par] else None
        if PARANOID:
            w = [(tv, tv.n), (vt, vt.n), (st, st.n)]
        ph.emit("gpsimd", lambda e_, s=src1, par=par: e_.dma_start(
            out=w1b[par][:], in_=s.rearrange("(kc p) mm -> p kc mm", p=128)),
            wait=w, inc=[(wl[par], 16)])
        ph.emit("gpsimd", lambda e_, s=src2, par=par: e_.dma_start(
            out=w2b[par][:], in_=s.rearrange("(kc p) mm -> p kc mm", p=128)),
            inc=[(wl[par], 16)])

    first_racc = True
    for mi in range(nmega):
        t0 = mi * TM
        ph.emit("sync", lambda e, t0=t0: e.dma_start(
            out=nrm[:], in_=io["nr_d"][:, :, t0:t0 + TM].rearrange("a p b -> p a b")),
            wait=[(vt, vt.n), (st, st.n), (tv, tv.n)], inc=[(nrl, 16)])
        ph.emit("sync", lambda e, t0=t0: e.dma_start(
            out=sslab[:], in_=io["s_d"][:, :, t0:t0 + TM].rearrange("a p b -> p a b")),
            inc=[(ssl, 16)])
        wn = [(nrl, nrl.n)]
        NEXP = len(P3_EXPERTS)
        if mi == 0:
            wload(P3_EXPERTS[0], 0, 0)
        for sub in range(NSUB):
            rtv = mmg(pR, [(lambda e, kc=kc, sub=sub: e.matmul(
                pR[:, 0:8], nrm[:, kc, sub * 128:(sub + 1) * 128], c["rwT"][:, kc, :],
                start=(kc == 0), stop=(kc == 7))) for kc in range(8)], extra_wait=wn)
            ev("scalar", lambda e: e.activation(out=probs[:], in_=pR[:, 0:8],
                                                func=AF.Sigmoid), bank=pR,
               wait=[(tv, rtv)])
            sp = st.n
            for e8 in range(8):
                ph.emit("vector", lambda e, e8=e8: e.tensor_scalar(
                    out=ge88[:, e8, :], in0=probs[:], scalar1=probs[:, e8:e8 + 1],
                    scalar2=None, op0=ALU.is_ge), wait=([(st, sp)] if e8 == 0 else None))
            co.vpad(ph, 1)
            ph.emit("vector", lambda e: e.reduce_sum(
                out=cnt8[:].rearrange("p (a b) -> p a b", b=1), in_=ge88[:],
                axis=mybir.AxisListType.X))
            co.vpad(ph, 4)
            ph.emit("vector", lambda e: e.scalar_tensor_tensor(
                out=cnt8[:], in0=cnt8[:], scalar=m[:, MI[2.0]:MI[2.0] + 1],
                in1=probs[:], op0=ALU.is_le, op1=ALU.mult))
            co.vpad(ph, 4)
            ph.emit("vector", lambda e: e.reduce_sum(
                out=den[:], in_=cnt8[:], axis=mybir.AxisListType.X), inc=[(vt, 1)])
            ph.emit("scalar", lambda e: e.activation(
                out=rden[:], in_=den[:], func=AF.Ln,
                bias=m[:, MI[1e-6]:MI[1e-6] + 1]), wait=[(vt, vt.n)])
            ph.emit("scalar", lambda e: e.activation(
                out=rden[:], in_=rden[:], func=AF.Exp,
                scale=m[:, MI[-1.0]:MI[-1.0] + 1]), inc=[(st, 1)])
            ph.emit("vector", lambda e: e.tensor_scalar(
                out=cmbb[:], in0=cnt8[:], scalar1=rden[:, 0:1], scalar2=None,
                op0=ALU.mult), wait=[(st, st.n)], inc=[(vt, 1)])
            if DEBUG_P3:
                d3w = co.sem("p3d3w")
                ph.emit("vector", lambda e, mi=mi, sub=sub: e.tensor_copy(
                    out=dbg3p[:, mi, sub, 0:8], in_=probs[:]))
                ph.emit("vector", lambda e, mi=mi, sub=sub: e.tensor_copy(
                    out=dbg3p[:, mi, sub, 8:16], in_=cnt8[:]))
                ph.emit("vector", lambda e, mi=mi, sub=sub: e.tensor_copy(
                    out=dbg3p[:, mi, sub, 16:24], in_=cmbb[:]))
                ph.emit("vector", lambda e, mi=mi, sub=sub: e.tensor_copy(
                    out=dbg3p[:, mi, sub, 24:25], in_=den[:]))
                ph.emit("vector", lambda e, mi=mi, sub=sub: e.tensor_copy(
                    out=dbg3p[:, mi, sub, 25:26], in_=rden[:]), inc=[(vt, 1)])
            ctv = mmg(ptp, [lambda e: e.transpose(ptp[0:8, 0:128], cmbb[:],
                                                  c["id128b"][:])],
                      extra_wait=[(vt, vt.n)])
            ev("vector", lambda e, sub=sub: e.tensor_copy(
                out=comb8[:, sub, :], in_=ptp[0:8, 0:128]), bank=ptp, wait=[(tv, ctv)])
        comb_vt = vt.n
        if DEBUG_P3:
            ph.emit("sync", lambda e, mi=mi: e.dma_start(
                out=io["dbg3_d"][:, mi, :, :], in_=comb8[:]),
                wait=[(vt, comb_vt)], inc=[(d3w, 16)])
            ph.emit("sync", lambda e, mi=mi: e.dma_start(
                out=io["dbg3p_d"][:, mi, :, :], in_=dbg3p[:, mi, :, :]),
                inc=[(d3w, 16)])
            ph.wait("sync", d3w, d3w.n)

        for ei, e8 in enumerate(P3_EXPERTS):
            for q in range(P3_QUARTERS):
                gq = (mi * NEXP + ei) * 4 + q
                par = gq % 2
                nxt = gq + 1
                nei, nq = divmod(nxt % (NEXP * 4), 4) if nmega > 1 else divmod(nxt, 4)
                if nxt < (mi + 1) * NEXP * 4:
                    ni = nxt - (mi * NEXP) * 4
                    wload(P3_EXPERTS[ni // 4], ni % 4, nxt % 2)
                elif mi + 1 < nmega:
                    wload(P3_EXPERTS[0], 0, nxt % 2)
                if e8 < 8 and q == 0:
                    for nt in range(NT):
                        btv = mmg(pR, [lambda e, e8=e8, nt=nt: e.matmul(
                            pR[:, 0:512], c["sel8"][:, e8, :],
                            comb8[:, 4 * nt:4 * nt + 4, :].rearrange("p a b -> p (a b)"),
                            start=True, stop=True)], extra_wait=[(vt, comb_vt)])
                        ev("vector", lambda e, nt=nt: e.tensor_copy(
                            out=cbc[:, nt * 512:(nt + 1) * 512], in_=pR[:, 0:512]),
                            bank=pR, wait=[(tv, btv)])
                cb_vt = vt.n
                wlw = [(wl[par], wl[par].n)]
                skip = P3_SKIP if (isinstance(P3_SKIP, tuple) and P3_SKIP and isinstance(P3_SKIP[0], str) and q == 1) else ()
                for mt in range(8):
                    for nt in range(NT):
                        bank = pA if (mt * NT + nt) % 2 == 0 else pB
                        gv = mmg(bank, [(lambda e, kc=kc, mt=mt, nt=nt, bank=bank,
                                         par=par: e.matmul(
                            bank[:, 0:512], w1b[par][:, kc, mt * 128:(mt + 1) * 128],
                            nrm[:, kc, nt * 512:(nt + 1) * 512],
                            start=(kc == 0), stop=(kc == 7))) for kc in range(8)],
                            extra_wait=wlw + wn)
                        if "relu" in skip:
                            guard[id(bank)] = (tv, gv)
                        elif e8 == 8:
                            ev("scalar", lambda e, mt=mt, nt=nt, bank=bank, q=q: e.activation(
                                out=hq[:, mt, nt * 512:(nt + 1) * 512], in_=bank[:, 0:512],
                                func=AF.Relu, bias=c["sb1c"][:, q * 8 + mt:q * 8 + mt + 1]),
                                bank=bank, wait=[(tv, gv)])
                        else:
                            hpar = (mt * NT + nt) % 2
                            ev("scalar", lambda e, mt=mt, nt=nt, bank=bank, hpar=hpar,
                               e8=e8, q=q: e.activation(
                                out=hr[:, hpar, :], in_=bank[:, 0:512], func=AF.Relu,
                                bias=c["eb1c"][:, e8, q * 8 + mt:q * 8 + mt + 1]),
                                bank=bank, wait=[(tv, gv), (vt, vt.n)])
                            ph.emit("vector", lambda e, mt=mt, nt=nt, hpar=hpar:
                                    e.tensor_tensor(
                                out=hq[:, mt, nt * 512:(nt + 1) * 512], in0=hr[:, hpar, :],
                                in1=cbc[:, nt * 512:(nt + 1) * 512], op=ALU.mult),
                                wait=[(st, st.n)], inc=[(vt, 1)])
                if e8 == 8:
                    h_ready = [(st, st.n)]
                else:
                    h_ready = [(vt, vt.n)]
                if P3_TRAP is not None and (e8, q) == P3_TRAP:
                    from concourse.bass_interp import add_trap
                    ph.emit("tensor", lambda e: add_trap(e, key="w2start"),
                            wait=h_ready)
                    ph.emit("vector", lambda e: add_trap(e, key="vq0end"))
                for nt in range(NT):
                    if "w2" in skip:
                        break
                    for ft in range(8):
                        if P3_W2MAX is not None and q == 1 and (nt * 8 + ft) >= P3_W2MAX:
                            continue
                        bank = pC if (nt * 8 + ft) % 2 == 0 else pD
                        mms = [(lambda e, kc=kc, ft=ft, nt=nt, bank=bank, par=par:
                                e.matmul(
                            bank[:, 0:512], w2b[par][:, kc, ft * 128:(ft + 1) * 128],
                            hq[:, kc, nt * 512:(nt + 1) * 512],
                            start=(kc == 0), stop=(kc == 7)))
                            for kc in range(8)]
                        gv = mmg(bank, mms, extra_wait=h_ready)
                        if P3_TRAP is not None and (e8, q, nt, ft) == P3_TRAP + (0, 0):
                            from concourse.bass_interp import add_trap
                            ph.emit("tensor", lambda e: add_trap(e, key="w2g0"))
                            trap_after_add = [True]
                        else:
                            trap_after_add = [False]
                        if nt == NT - 1 and ft == 7:
                            wfree[par] = gv
                        if first_racc and e8 == P3_EXPERTS[0] and q == 0:
                            ev("vector", lambda e, ft=ft, nt=nt, bank=bank: e.tensor_copy(
                                out=racc[:, ft, nt * 512:(nt + 1) * 512],
                                in_=bank[:, 0:512]), bank=bank, wait=[(tv, gv)])
                        else:
                            ev("vector", lambda e, ft=ft, nt=nt, bank=bank: e.tensor_tensor(
                                out=racc[:, ft, nt * 512:(nt + 1) * 512],
                                in0=racc[:, ft, nt * 512:(nt + 1) * 512],
                                in1=bank[:, 0:512], op=ALU.add), bank=bank,
                                wait=[(tv, gv)])
                        if trap_after_add[0]:
                            from concourse.bass_interp import add_trap
                            ph.emit("vector", lambda e: add_trap(e, key="postadd"))
                        if e8 == 8 and q == 0 and P3_EB2:
                            btv2 = mmg(pR, [lambda e, ft=ft, nt=nt: e.matmul(
                                pR[:, 0:512], c["eb2"][:, ft * 128:(ft + 1) * 128],
                                comb8[:, 4 * nt:4 * nt + 4, :].rearrange("p a b -> p (a b)"),
                                start=True, stop=True)], extra_wait=[(vt, comb_vt)])
                            ev("vector", lambda e, ft=ft, nt=nt: e.tensor_tensor(
                                out=racc[:, ft, nt * 512:(nt + 1) * 512],
                                in0=racc[:, ft, nt * 512:(nt + 1) * 512],
                                in1=pR[:, 0:512], op=ALU.add), bank=pR,
                                wait=[(tv, btv2)])
                if e8 == P3_EXPERTS[0] and q == 3:
                    first_racc = False
        racc_vt = vt.n
        first_racc = True

        for ft in range(8):
            for nt in range(NT):
                ph.emit("scalar", lambda e, ft=ft, nt=nt: e.activation(
                    out=gfm[:, ft, nt * 512:(nt + 1) * 512],
                    in_=racc[:, ft, nt * 512:(nt + 1) * 512], func=AF.Silu,
                    bias=c["sb2c"][:, ft:ft + 1]), wait=[(vt, racc_vt)],
                    inc=[(st, 1)])
        g_st = st.n
        ztr_done = [0, 0]
        for tt in range(NSUB):
            xpar = tt % 2
            ph.emit("sync", lambda e, t0=t0, tt=tt, xpar=xpar: e.dma_start(
                out=xt3[xpar][:], in_=io["x"][t0 + tt * 128:t0 + (tt + 1) * 128, :]),
                wait=[(owb[xpar], owb[xpar].n)], inc=[(xl[xpar], 16)])
            for ft in range(8):
                zpar = ft % 2
                w_z = [(st, g_st), (ssl, ssl.n), (tv, ztr_done[zpar])]
                ph.emit("vector", lambda e, ft=ft, tt=tt, zpar=zpar: e.tensor_tensor(
                    out=zt[:, zpar, :], in0=gfm[:, ft, tt * 128:(tt + 1) * 128],
                    in1=sslab[:, ft, tt * 128:(tt + 1) * 128], op=ALU.mult),
                    wait=w_z, inc=[(vt, 1)])
                ztv = mmg(ptp, [lambda e, ft=ft, zpar=zpar: e.transpose(
                    ptp[:, ft * 128:(ft + 1) * 128], zt[:, zpar, :], c["id128b"][:])],
                    extra_wait=[(vt, vt.n)], inc=True)
                ztr_done[zpar] = tv.n
            ztv = tv.n
            opar = tt % 2
            for ft in range(8):
                ev("vector", lambda e, ft=ft, opar=opar, xpar=xpar: e.tensor_tensor(
                    out=ot3[opar][:, ft * 128:(ft + 1) * 128],
                    in0=ptp[:, ft * 128:(ft + 1) * 128],
                    in1=xt3[xpar][:, ft * 128:(ft + 1) * 128], op=ALU.add),
                    bank=(ptp if ft == 7 else None),
                    wait=[(tv, ztv), (xl[xpar], xl[xpar].n),
                          (owb[opar], owb[opar].n)])
            ph.emit("sync", lambda e, t0=t0, tt=tt, opar=opar: e.dma_start(
                out=io["out"][t0 + tt * 128:t0 + (tt + 1) * 128, :], in_=ot3[opar][:]),
                wait=[(vt, vt.n)], inc=[(owb[opar], 16)])
    for s_ in owb:
        ph.wait("sync", s_, s_.n)
    ph.wait("vector", owb[0], owb[0].n)
    ph.wait("vector", owb[1], owb[1].n)
    return ph



def build_core_program(L, host, phases=("p1",)):
    nc = bass.Bass("TRN2", target_bir_lowering=False)
    io = declare_io(nc, L, host)
    with contextlib.ExitStack() as ges:
        co = Core(nc, ges, L)
        with contextlib.ExitStack() as es0:
            c, ph0 = build_consts(co, io, es0)
            ph0.run(nc)
        if "p1" in phases:
            with contextlib.ExitStack() as es1:
                build_p1(co, io, c, es1).run(nc)
        if "p2" in phases:
            with contextlib.ExitStack() as es2:
                build_p2(co, io, c, es2).run(nc)
        if "p3" in phases:
            with contextlib.ExitStack() as es3:
                build_p3(co, io, c, es3).run(nc)
        with nc.Block("sclr") as blk0:
            def _clr(g):
                for s_ in co.sems.values():
                    g.sem_clear(s_.h)
            blk0.gpsimd(_clr)
    return nc, io


def kernel(**inputs):
    L = 4096
    host = host_layout(inputs)
    x = np.ascontiguousarray(np.asarray(inputs["L0_attn_hidden_states"], dtype=np.float32))
    nc, io = build_core_program(L, host, phases=("p1", "p2", "p3"))
    in_maps = []
    for cid in range(NCORES):
        im = {"x": np.ascontiguousarray(x[cid])}
        im.update(host)
        in_maps.append(im)
    res = run_bass_kernel_spmd(nc, in_maps, core_ids=list(range(NCORES)))
    return np.stack([r["out"] for r in res.results], axis=0)


# version 06 (final)
# speedup vs baseline: 1.0500x; 1.0161x over previous
import contextlib
import numpy as np
import concourse.bass as bass
import concourse.mybir as mybir
from concourse.bass_utils import run_bass_kernel_spmd

F32 = mybir.dt.float32
BF16 = mybir.dt.bfloat16
AF = mybir.ActivationFunctionType
ALU = mybir.AluOpType

D = 1024
H = 4
DK = 256
CHUNK = 32
CDEV = 128
NCORES = 8

P3_EXPERTS = list(range(8)) + [8]
P3_QUARTERS = 4
P3_TRAP = None
P3_EB2 = True
P3_SKIP = ()
P3_W2MAX = None
DEBUG_SCRATCH = False
DEBUG_FEED = False
DEBUG_P3 = False
PARANOID = False
MISC_VALS = [1e-6, 1e-5, 1e-8, -0.5, 1.0 / D, 1.0 / DK, 1e-4, 0.05, 2.0, 1.0,
             0.5, -1.0, 0.0]
MI = {v: i for i, v in enumerate(MISC_VALS)}


class Sem:
    def __init__(self, nc, es, name):
        self.h = es.enter_context(nc.semaphore(name))
        self.n = 0


DEBUG_SYNC = False


class Phase:
    ENGINES = ("sync", "gpsimd", "vector", "scalar", "tensor")

    def __init__(self, name):
        self.name = name
        self.q = {k: [] for k in self.ENGINES}
        self.hwm = {k: {} for k in self.ENGINES}
        self.auto = None
        self.last_done = {}

    def emit(self, eng, f, wait=None, inc=None):
        if DEBUG_SYNC and self.auto and f is not None:
            a = self.auto[eng]
            d = 16 if eng in ("sync", "gpsimd") else 1
            if self.last_done.get(eng):
                wait = list(wait or []) + [self.last_done[eng]]
            if inc:
                s0, d0 = inc[0]
                self.last_done[eng] = (s0, s0.n + d0)
            else:
                inc = [(a, d)]
                self.last_done[eng] = (a, a.n + d)
        q = self.q[eng]
        if wait:
            hw = self.hwm[eng]
            for s, v in wait:
                if v > 0 and v > hw.get(id(s), -1):
                    hw[id(s)] = v
                    q.append(lambda e, s=s, v=v: e.wait_ge(s.h, v))
        if inc:
            for s, d in inc:
                s.n += d
            def g(e, f=f, incs=tuple(inc)):
                ins = f(e)
                for s, d in incs:
                    ins = ins.then_inc(s.h, d)
            q.append(g)
        elif f is not None:
            q.append(f)

    def wait(self, eng, sem, val):
        self.emit(eng, None, wait=[(sem, val)])

    def run(self, nc):
        with nc.Block(self.name) as block:
            for eng in self.ENGINES:
                cmds = self.q[eng]
                if not cmds:
                    continue
                def body(e, cmds=cmds):
                    for cf in cmds:
                        cf(e)
                getattr(block, eng)(body)


def host_layout(inputs):
    g = lambda k: np.ascontiguousarray(np.asarray(inputs[k], dtype=np.float32))
    d = {}
    d["wqT"] = np.ascontiguousarray(g("L0_attn_Wq").T)
    d["wkT"] = np.ascontiguousarray(g("L0_attn_Wk").T)
    d["wvT"] = np.ascontiguousarray(g("L0_attn_Wv").T)
    d["woT"] = np.ascontiguousarray(g("L0_attn_Wo").T)
    g1T = np.ascontiguousarray(g("L0_attn_gate_w1").T)
    d["g1T"] = np.ascontiguousarray(g1T[:D])
    d["g1Ts"] = np.ascontiguousarray(g1T[D:])
    d["g2T"] = np.ascontiguousarray(g("L0_attn_gate_w2").T)
    d["wbT"] = np.ascontiguousarray(g("L0_attn_Wb").T)
    d["rwT"] = np.ascontiguousarray(g("L0_ffn_router_w").T)
    d["sw1"] = g("L0_ffn_sw1")
    d["sw2"] = g("L0_ffn_sw2")
    d["ew1"] = g("L0_ffn_ew1")
    d["ew2"] = g("L0_ffn_ew2")
    d["eb2"] = g("L0_ffn_eb2")
    d["eb1c"] = np.ascontiguousarray(
        g("L0_ffn_eb1").reshape(8, 32, 128).transpose(2, 0, 1))
    d["sb1c"] = np.ascontiguousarray(g("L0_ffn_sb1").reshape(32, 128).T)
    d["sb2c"] = np.ascontiguousarray(g("L0_ffn_sb2").reshape(8, 128).T)
    d["gb1c"] = np.ascontiguousarray(g("L0_attn_gate_b1").reshape(16, 128).T)
    for k, nm in (("qcw", "L0_attn_qconv_w"), ("kcw", "L0_attn_kconv_w"),
                  ("vcw", "L0_attn_vconv_w")):
        d[k] = np.ascontiguousarray(g(nm).reshape(8, 128, 4).transpose(1, 0, 2))
    d["fsw"] = np.ascontiguousarray(
        g("L0_attn_fir_short_w").reshape(8, 128, 3).transpose(1, 0, 2))
    d["flw"] = np.ascontiguousarray(
        g("L0_attn_fir_long_w").reshape(8, 128, 63).transpose(1, 0, 2))
    d["nwb"] = np.ascontiguousarray(
        g("L0_ffn_norm_weight").reshape(8, 128).T)
    d["onb"] = np.ascontiguousarray(
        np.broadcast_to(g("L0_attn_onorm_w")[None, :], (128, DK)))
    d["bbb"] = np.ascontiguousarray(
        np.broadcast_to(g("L0_attn_base_bias").reshape(-1)[None, :], (128, 16)))
    d["lt"] = g("L0_attn_log_temp").reshape(1, 4)
    d["fr"] = g("L0_attn_floor_raw").reshape(1, 16)
    d["misc"] = np.ascontiguousarray(
        np.broadcast_to(np.array(MISC_VALS, dtype=np.float32)[None, :],
                        (128, len(MISC_VALS))))
    d["id128"] = np.eye(128, dtype=np.float32)
    tri = np.triu(np.ones((CDEV, CDEV), dtype=np.float32))
    d["mLSn"] = np.ascontiguousarray(-(1.0 - tri))
    d["mUSn"] = np.ascontiguousarray(-np.triu(tri, 1))
    d["mUI"] = tri.copy()
    d["ones1"] = np.ones((1, 128), dtype=np.float32)
    d["id4"] = np.eye(4, dtype=np.float32)
    d["id16"] = np.eye(16, dtype=np.float32)
    sel8 = np.zeros((8, 8, 128), dtype=np.float32)
    for e in range(8):
        sel8[e, e, :] = 1.0
    d["sel8"] = sel8
    sel4 = np.zeros((4, 4, CDEV), dtype=np.float32)
    for h in range(4):
        sel4[h, h, :] = 1.0
    d["sel4"] = sel4
    nsel = np.zeros((128, 4, 4), dtype=np.float32)
    for h in range(4):
        nsel[:, h, h] = 1.0
    d["nsel"] = nsel
    return d


class Core:
    def __init__(self, nc, es, L):
        self.nc = nc
        self.es = es
        self.L = L
        self.sems = {}
        self.gjk = es.enter_context(nc.sbuf_tensor("gjk", [128, 128], F32))

    def vpad(self, ph, n=4):
        for k in range(n):
            ph.emit("vector", lambda e: e.memset(self.gjk[:, 0:128], 0.0))

    def sem(self, name):
        if name not in self.sems:
            self.sems[name] = Sem(self.nc, self.es, name)
        return self.sems[name]

    def sb(self, es, name, shape, dtype):
        return es.enter_context(self.nc.sbuf_tensor(name, list(shape), dtype))

    def ps(self, es, name, shape, dtype=F32):
        return es.enter_context(self.nc.psum_tensor(name, list(shape), dtype))


def declare_io(nc, L, host):
    t = {}
    t["x"] = nc.dram_tensor("x", [L, D], F32, kind="ExternalInput").ap()
    for k, v in host.items():
        t[k] = nc.dram_tensor(k, list(v.shape), F32, kind="ExternalInput").ap()
    t["out"] = nc.dram_tensor("out", [L, D], F32, kind="ExternalOutput").ap()
    kind = "ExternalOutput" if DEBUG_SCRATCH else "Internal"
    if DEBUG_FEED:
        kind = "ExternalInput"
    if DEBUG_SCRATCH:
        t["dbg_d"] = nc.dram_tensor("dbg_d", [128, 32, 8], F32, kind="ExternalOutput").ap()
    if DEBUG_P3:
        t["dbg3_d"] = nc.dram_tensor("dbg3_d", [8, 4, 8, 128], BF16,
                                     kind="ExternalOutput").ap()
        t["dbg3p_d"] = nc.dram_tensor("dbg3p_d", [128, 4, 8, 32], F32,
                                      kind="ExternalOutput").ap()
    t["xn_d"] = nc.dram_tensor("xn_d", [8, 128, L], BF16, kind=kind).ap()
    t["nr_d"] = nc.dram_tensor("nr_d", [8, 128, L], BF16, kind=kind).ap()
    t["s_d"] = nc.dram_tensor("s_d", [8, 128, L], BF16, kind=kind).ap()
    return t



def build_consts(co, io, es):
    c = {}
    mk = lambda n, shp, dt=F32: c.__setitem__(n, co.sb(co.es, "c_" + n, shp, dt))
    mk("misc", [128, len(MISC_VALS)])
    mk("id128b", [128, 128], BF16)
    mk("id128f", [128, 128])
    mk("mLSn", [CDEV, CDEV]); mk("mUSn", [CDEV, CDEV]); mk("mUI", [CDEV, CDEV])
    mk("id4f", [4, 4]); mk("id16f", [16, 16])
    mk("ones1b", [1, 128], BF16); mk("ones1f", [1, 128])
    mk("sel8", [8, 8, 128], BF16); mk("sel4", [4, 4, CDEV]); mk("nsel", [128, 4, 4], BF16)
    mk("onb", [128, DK]); mk("bbb", [128, 16])
    mk("gb1c", [128, 16]); mk("sb1c", [128, 32]); mk("sb2c", [128, 8])
    mk("eb1c", [128, 8, 32]); mk("eb2", [8, 1024], BF16)
    mk("qcw", [128, 8, 4]); mk("kcw", [128, 8, 4]); mk("vcw", [128, 8, 4])
    mk("fsw", [128, 8, 3]); mk("flw", [128, 8, 63])
    mk("wbT", [128, 8, 4], BF16); mk("rwT", [128, 8, 8], BF16)
    mk("g2T", [128, 16, 16], BF16)
    mk("lt", [1, 4]); mk("fr", [1, 16])
    mk("itb", [128, 16]); mk("flb", [128, 16])
    mk("it16", [1, 16]); mk("fl16", [1, 16]); mk("sp_t", [1, 4])

    ph = Phase("consts")
    if DEBUG_SYNC:
        ph.auto = {e_: co.sem("consts_a" + e_) for e_ in Phase.ENGINES}
    ld = co.sem("c_ld")
    ldg = co.sem("c_ldg")
    cv = co.sem("c_cv")
    cs = co.sem("c_cs")
    mmc = co.sem("c_mmc")
    m = c["misc"]

    plain = ["misc", "id128f", "id4f", "id16f", "mLSn", "mUSn", "mUI", "onb", "bbb",
             "gb1c", "sb1c", "sb2c", "eb1c", "qcw", "kcw", "vcw", "fsw", "flw",
             "sel4", "lt", "fr"]
    for nm in plain:
        srcmap = {"ones1f": "ones1", "id128f": "id128", "id4f": "id4", "id16f": "id16"}
        src = io[srcmap.get(nm, nm)]
        ph.emit("sync", lambda e, o=c[nm], i=src: e.dma_start(out=o[:], in_=i),
                inc=[(ld, 16)])
    ph.emit("sync", lambda e: e.dma_start(out=c["ones1f"][:], in_=io["ones1"]), inc=[(ld, 16)])
    casting = [("id128b", io["id128"]), ("ones1b", io["ones1"]),
               ("sel8", io["sel8"]), ("nsel", io["nsel"]), ("eb2", io["eb2"]),
               ("wbT", io["wbT"].rearrange("(kc p) h -> p kc h", p=128)),
               ("rwT", io["rwT"].rearrange("(kc p) h -> p kc h", p=128)),
               ("g2T", io["g2T"].rearrange("(kc p) h -> p kc h", p=128))]
    for nm, src in casting:
        ph.emit("gpsimd", lambda e, o=c[nm], i=src: e.dma_start(out=o[:], in_=i),
                inc=[(ldg, 16)])
    for fn in (AF.Identity, AF.Relu, AF.Square, AF.Abs, AF.Exp, AF.Ln,
               AF.Sigmoid, AF.Silu, AF.Gelu):
        ph.emit("scalar", lambda e, fn=fn: e.activation(
            out=c["sp_t"][:], in_=c["misc"][0:1, MI[1.0]:MI[1.0] + 4], func=fn),
            wait=[(ld, ld.n)])
    sp = c["sp_t"]
    ph.emit("scalar", lambda e: e.activation(out=sp[:], in_=c["lt"][:], func=AF.Abs),
            wait=[(ld, ld.n), (ldg, ldg.n)])
    ph.emit("scalar", lambda e: e.activation(out=sp[:], in_=sp[:], func=AF.Exp,
                                             scale=m[0:1, MI[-1.0]:MI[-1.0] + 1]))
    ph.emit("scalar", lambda e: e.activation(out=sp[:], in_=sp[:], func=AF.Ln,
                                             bias=m[0:1, MI[1.0]:MI[1.0] + 1]),
            inc=[(cs, 1)])
    ph.emit("vector", lambda e: e.tensor_scalar(
        out=c["it16"][:, 0:4], in0=c["lt"][:], scalar1=m[0:1, MI[0.0]:MI[0.0] + 1],
        scalar2=None, op0=ALU.max), wait=[(cs, cs.n)])
    co.vpad(ph)
    ph.emit("vector", lambda e: e.tensor_tensor(out=c["it16"][:, 0:4],
                                                in0=c["it16"][:, 0:4], in1=sp[:], op=ALU.add))
    co.vpad(ph)
    ph.emit("vector", lambda e: e.tensor_scalar(
        out=c["it16"][:, 0:4], in0=c["it16"][:, 0:4],
        scalar1=m[0:1, MI[1e-4]:MI[1e-4] + 1], scalar2=None, op0=ALU.add))
    co.vpad(ph)
    ph.emit("vector", lambda e: e.reciprocal(out=c["it16"][:, 0:4], in_=c["it16"][:, 0:4]))
    co.vpad(ph)
    for h in range(3, -1, -1):
        for p in range(4):
            if h == 0 and p == 0:
                continue
            last = (h == 0 and p == 3)
            ph.emit("vector", lambda e, h=h, p=p: e.tensor_copy(
                out=c["it16"][:, h * 4 + p:h * 4 + p + 1], in_=c["it16"][:, h:h + 1]),
                inc=([(cv, 1)] if last else None))
    co.vpad(ph)
    ph.emit("scalar", lambda e: e.activation(out=c["fl16"][:], in_=c["fr"][:],
                                             func=AF.Sigmoid), wait=[(ld, ld.n)])
    ph.emit("scalar", lambda e: e.mul(c["fl16"][:], c["fl16"][:],
                                      m[0:1, MI[0.05]:MI[0.05] + 1]), inc=[(cs, 1)])
    pb = co.ps(es, "c_pb", [128, 32], F32)
    ph.emit("tensor", lambda e: e.matmul(pb[:, 0:16], c["ones1f"][:], c["it16"][:],
                                         start=True, stop=True),
            wait=[(cv, cv.n), (cs, cs.n)])
    ph.emit("tensor", lambda e: e.matmul(pb[:, 16:32], c["ones1f"][:], c["fl16"][:],
                                         start=True, stop=True), inc=[(mmc, 1)])
    ph.emit("vector", lambda e: e.tensor_copy(out=c["itb"][:], in_=pb[:, 0:16]),
            wait=[(mmc, mmc.n)])
    ph.emit("vector", lambda e: e.tensor_copy(out=c["flb"][:], in_=pb[:, 16:32]),
            inc=[(cv, 1)])
    ph.wait("sync", cv, cv.n)
    ph.wait("sync", ld, ld.n)
    ph.wait("gpsimd", ldg, ldg.n)
    return c, ph



def build_p1(co, io, c, es):
    L = co.L
    ph = Phase("p1")
    if DEBUG_SYNC:
        ph.auto = {e_: co.sem("p1_a" + e_) for e_ in Phase.ENGINES}
    m = c["misc"]
    nt = L // 128
    nwb = co.sb(es, "p1_nwb", [128, 8], F32)
    nwl = co.sem("p1nwl")
    ph.emit("sync", lambda e: e.dma_start(out=nwb[:], in_=io["nwb"]), inc=[(nwl, 16)])
    ph.wait("vector", nwl, 16)

    xld = [co.sem("p1x0"), co.sem("p1x1")]
    vv = co.sem("p1v")
    ss = co.sem("p1s")
    tt = co.sem("p1t")
    wbx = [co.sem("p1wx0"), co.sem("p1wx1")]
    wbn = [co.sem("p1wn0"), co.sem("p1wn1")]

    xt = [co.sb(es, f"p1_x{i}", [128, D], F32) for i in range(2)]
    jnk = co.sb(es, "p1_jnk", [128, D], F32)
    sumall = co.sb(es, "p1_suma", [128, nt], F32)
    sqall = co.sb(es, "p1_sqa", [128, nt], F32)
    meanall = co.sb(es, "p1_mna", [128, nt], F32)
    varall = co.sb(es, "p1_vra", [128, nt], F32)
    rstdall = co.sb(es, "p1_rsa", [128, nt], F32)
    rrall = co.sb(es, "p1_rra", [128, nt], F32)
    scr = co.sb(es, "p1_scr", [128, nt], F32)
    dbg = co.sb(es, "p1_dbg", [128, nt, 8], F32) if DEBUG_SCRATCH else None
    xn = co.sb(es, "p1_xn", [128, D], F32)
    xnb = co.sb(es, "p1_xnb", [128, D], BF16)
    nrb = co.sb(es, "p1_nrb", [128, D], BF16)
    fmx = [co.sb(es, f"p1_fmx{i}", [128, 8, 128], BF16) for i in range(2)]
    fmn = [co.sb(es, f"p1_fmn{i}", [128, 8, 128], BF16) for i in range(2)]
    pt_ = [co.ps(es, f"p1_pt{i}", [128, 1024], BF16) for i in range(2)]

    acc_done = [0, 0]
    sq_done = [0, 0]
    for i in range(nt):
        par = i % 2
        ph.emit("sync", lambda e, i=i, par=par: e.dma_start(
            out=xt[par][:], in_=io["x"][i * 128:(i + 1) * 128, :]),
            wait=([(vv, acc_done[par])] if acc_done[par] else None),
            inc=[(xld[par], 16)])
        ph.emit("vector", lambda e, i=i, par=par: e.reduce_sum(
            out=sumall[:, i:i + 1], in_=xt[par][:], axis=mybir.AxisListType.X),
            wait=[(xld[par], xld[par].n)])
        ph.emit("scalar", lambda e, i=i, par=par: e.activation(
            out=(jnk if par == 0 else xn)[:], in_=xt[par][:], func=AF.Square),
            wait=[(xld[par], xld[par].n), (vv, sq_done[par])], inc=[(ss, 1)])
        ph.emit("vector", lambda e, i=i, par=par: e.reduce_sum(
            out=sqall[:, i:i + 1], in_=(jnk if par == 0 else xn)[:],
            axis=mybir.AxisListType.X),
            wait=[(ss, ss.n)], inc=[(vv, 1)])
        acc_done[par] = vv.n
        sq_done[par] = vv.n
    if DEBUG_SCRATCH:
        ph.emit("vector", lambda e: e.tensor_copy(out=dbg[:, :, 4], in_=sumall[:]),
                wait=[(ss, ss.n)])
        ph.emit("vector", lambda e: e.tensor_copy(out=dbg[:, :, 5], in_=sqall[:]))
    co.vpad(ph)
    ph.emit("vector", lambda e: e.tensor_scalar(
        out=meanall[:], in0=sumall[:], scalar1=m[:, MI[1.0 / D]:MI[1.0 / D] + 1],
        scalar2=None, op0=ALU.mult), wait=[(ss, ss.n)])
    co.vpad(ph)
    ph.emit("vector", lambda e: e.tensor_tensor(
        out=scr[:], in0=meanall[:], in1=meanall[:], op=ALU.mult))
    co.vpad(ph)
    ph.emit("vector", lambda e: e.scalar_tensor_tensor(
        out=varall[:], in0=sqall[:], scalar=m[:, MI[1.0 / D]:MI[1.0 / D] + 1],
        in1=scr[:], op0=ALU.mult, op1=ALU.subtract), inc=[(vv, 1)])
    ph.emit("scalar", lambda e: e.activation(out=rstdall[:], in_=varall[:], func=AF.Ln,
                                             bias=m[:, MI[1e-5]:MI[1e-5] + 1]),
            wait=[(vv, vv.n)])
    ph.emit("scalar", lambda e: e.activation(out=rstdall[:], in_=rstdall[:], func=AF.Exp,
                                             scale=m[:, MI[-0.5]:MI[-0.5] + 1]),
            inc=[(ss, 1)])
    co.vpad(ph)
    ph.emit("vector", lambda e: e.tensor_tensor(
        out=scr[:], in0=rstdall[:], in1=rstdall[:], op=ALU.mult), wait=[(ss, ss.n)])
    co.vpad(ph)
    ph.emit("vector", lambda e: e.tensor_tensor(
        out=scr[:], in0=scr[:], in1=varall[:], op=ALU.mult), inc=[(vv, 1)])
    co.vpad(ph)
    ph.emit("scalar", lambda e: e.activation(out=rrall[:], in_=scr[:], func=AF.Ln,
                                             bias=m[:, MI[1e-8]:MI[1e-8] + 1]),
            wait=[(vv, vv.n)])
    ph.emit("scalar", lambda e: e.activation(out=rrall[:], in_=rrall[:], func=AF.Exp,
                                             scale=m[:, MI[-0.5]:MI[-0.5] + 1]),
            inc=[(ss, 1)])
    batch_ss = ss.n
    ph.wait("vector", ss, batch_ss)
    if DEBUG_SCRATCH:
        dwb = co.sem("p1dwb")
        ph.emit("vector", lambda e: e.tensor_copy(out=dbg[:, :, 0], in_=meanall[:]))
        ph.emit("vector", lambda e: e.tensor_copy(out=dbg[:, :, 1], in_=varall[:]))
        ph.emit("vector", lambda e: e.tensor_copy(out=dbg[:, :, 2], in_=rstdall[:]))
        ph.emit("vector", lambda e: e.tensor_copy(out=dbg[:, :, 3], in_=sqall[:]))
        ph.emit("vector", lambda e: e.tensor_copy(out=dbg[:, :, 6], in_=rrall[:]),
                inc=[(vv, 1)])
        ph.emit("sync", lambda e: e.dma_start(out=io["dbg_d"], in_=dbg[:]),
                wait=[(vv, vv.n)], inc=[(dwb, 16)])
        ph.wait("sync", dwb, 16)

    for i in range(nt):
        par = i % 2
        x_t = xt[par]
        pt = pt_[par]
        ph.emit("sync", lambda e, i=i, par=par: e.dma_start(
            out=xt[par][:], in_=io["x"][i * 128:(i + 1) * 128, :]),
            wait=([(ss, batch_ss)] if i < 2 else None), inc=[(xld[par], 16)])
        ph.emit("vector", lambda e, x_t=x_t, i=i: e.tensor_scalar(
            out=xn[:], in0=x_t[:], scalar1=meanall[:, i:i + 1],
            scalar2=rstdall[:, i:i + 1], op0=ALU.subtract, op1=ALU.mult),
            wait=[(xld[par], xld[par].n)])
        ph.emit("vector", lambda e, i=i: e.tensor_scalar(
            out=nrb[:], in0=xn[:], scalar1=rrall[:, i:i + 1], scalar2=None,
            op0=ALU.mult))
        ph.emit("vector", lambda e: e.tensor_copy(out=xnb[:], in_=xn[:]), inc=[(vv, 1)])
        for kc in range(8):
            ph.emit("tensor", lambda e, kc=kc, pt=pt: e.transpose(
                pt[:, kc * 128:(kc + 1) * 128], xnb[:, kc * 128:(kc + 1) * 128],
                c["id128b"][:]),
                wait=([(vv, vv.n)] if kc == 0 else None),
                inc=([(tt, 1)] if kc == 7 else None))
        ph.emit("vector", lambda e, pt=pt, par=par: e.tensor_copy(
            out=fmx[par][:].rearrange("p a b -> p (a b)"), in_=pt[:]),
            wait=[(tt, tt.n), (wbx[par], wbx[par].n)], inc=[(vv, 1)])
        for kc in range(8):
            ph.emit("tensor", lambda e, kc=kc, pt=pt: e.transpose(
                pt[:, kc * 128:(kc + 1) * 128], nrb[:, kc * 128:(kc + 1) * 128],
                c["id128b"][:]),
                wait=([(vv, vv.n)] if kc == 0 else None),
                inc=([(tt, 1)] if kc == 7 else None))
        for kc in range(8):
            ph.emit("vector", lambda e, pt=pt, par=par, kc=kc: e.tensor_scalar(
                out=fmn[par][:, kc, :], in0=pt[:, kc * 128:(kc + 1) * 128],
                scalar1=nwb[:, kc:kc + 1], scalar2=None, op0=ALU.mult),
                wait=([(tt, tt.n), (wbn[par], wbn[par].n)] if kc == 0 else None),
                inc=([(vv, 1)] if kc == 7 else None))
        ph.emit("sync", lambda e, i=i, par=par: e.dma_start(
            out=io["xn_d"][:, :, i * 128:(i + 1) * 128].rearrange("a p b -> p a b"),
            in_=fmx[par][:]), wait=[(vv, vv.n - 1)], inc=[(wbx[par], 16)])
        ph.emit("sync", lambda e, i=i, par=par: e.dma_start(
            out=io["nr_d"][:, :, i * 128:(i + 1) * 128].rearrange("a p b -> p a b"),
            in_=fmn[par][:]), wait=[(vv, vv.n)], inc=[(wbn[par], 16)])
    for s in wbx + wbn:
        ph.wait("sync", s, s.n)
    return ph



def build_p2(co, io, c, es):
    L = co.L
    ph = Phase("p2")
    if DEBUG_SYNC:
        ph.auto = {e_: co.sem("p2_a" + e_) for e_ in Phase.ENGINES}
    m = c["misc"]
    TILE = 256
    ntile = L // TILE

    wld = co.sem("p2wld")
    xnl = [co.sem("p2xn0"), co.sem("p2xn1")]
    swb = [co.sem("p2swb0"), co.sem("p2swb1")]
    tv = co.sem("p2tv")
    vt = co.sem("p2vt")
    st = co.sem("p2st")

    wq = co.sb(es, "p2_wq", [128, 8, 1024], BF16)
    wk = co.sb(es, "p2_wk", [128, 8, 1024], BF16)
    wv = co.sb(es, "p2_wv", [128, 8, 1024], BF16)
    wosb = co.sb(es, "p2_wo", [128, 8, 1024], BF16)
    g1 = co.sb(es, "p2_g1", [128, 8, 2048], BF16)
    g1s = co.sb(es, "p2_g1s", [32, 2048], BF16)
    for sb_t, nm in ((wq, "wqT"), (wk, "wkT"), (wv, "wvT"), (wosb, "woT"), (g1, "g1T")):
        ph.emit("gpsimd", lambda e, o=sb_t, i=io[nm]: e.dma_start(
            out=o[:], in_=i.rearrange("(kc p) mm -> p kc mm", p=128)), inc=[(wld, 16)])
    ph.emit("gpsimd", lambda e: e.dma_start(out=g1s[:], in_=io["g1Ts"]), inc=[(wld, 16)])

    xnt = [co.sb(es, f"p2_xn{i}", [128, 8, TILE], BF16) for i in range(2)]
    qf = co.sb(es, "p2_qf", [128, 8, TILE], BF16)
    kf = co.sb(es, "p2_kf", [128, 8, TILE], BF16)
    vf = co.sb(es, "p2_vf", [128, 8, 64 + TILE], BF16)
    pre = co.sb(es, "p2_pre", [128, 2, TILE + 3], F32)
    cvx = co.sb(es, "p2_cvx", [128, 2, TILE], F32)
    usb = cvx[:, 0, :]
    lgF = cvx[0:16, 1, :]
    halos = {nm: co.sb(es, f"p2_h{nm}", [128, 8, 3], F32) for nm in "qkv"}

    bP = co.sb(es, "p2_bP", [4, TILE], F32)
    qnP = co.sb(es, "p2_qnP", [4, TILE], F32)
    knP = co.sb(es, "p2_knP", [4, TILE], F32)
    brnP = co.sb(es, "p2_brnP", [4, TILE], F32)
    qncol = co.sb(es, "p2_qnc", [128, 2, 4], F32)
    kncol = co.sb(es, "p2_knc", [128, 2, 4], F32)
    bcol = co.sb(es, "p2_bc", [128, 2, 4], F32)
    brncol = co.sb(es, "p2_brc", [128, 2, 4], F32)
    kntm = co.sb(es, "p2_kntm", [128, 2, 4, 256], BF16)
    kbtm = co.sb(es, "p2_kbtm", [128, 2, 4, 256], BF16)
    vbtm = co.sb(es, "p2_vbtm", [128, 2, 4, 256], BF16)
    vtmT = co.sb(es, "p2_vtm", [128, 2, 8, 128], BF16)
    dtm = co.sb(es, "p2_dtm", [128, 2, 8, 128], BF16)
    stmT = co.sb(es, "p2_stm", [128, 2, 8, 128], BF16)
    ltmT = co.sb(es, "p2_ltm", [128, 2, 8, 128], BF16)
    Sf = co.sb(es, "p2_Sf", [128, 2, 4, 256], F32)
    Sb = co.sb(es, "p2_Sb", [128, 2, 4, 256], BF16)
    firfm = co.sb(es, "p2_firfm", [128, 2, 8, TILE], BF16)
    SF_ = firfm[:, 0, :, :]
    LF_ = firfm[:, 1, :, :]
    qsq = firfm[:, 0, :, :]
    Nb = co.sb(es, "p2_Nb", [CDEV, CDEV], F32)
    Npb = co.sb(es, "p2_Npb", [CDEV, CDEV], F32)
    nt1 = co.sb(es, "p2_nt1", [CDEV, CDEV], F32)
    Msl = co.sb(es, "p2_Msl", [CDEV, CDEV], F32)
    Mpsl = co.sb(es, "p2_Mpsl", [CDEV, CDEV], F32)
    TPf = co.sb(es, "p2_TPf", [CDEV, CDEV], F32)
    TPb = co.sb(es, "p2_TPb", [CDEV, CDEV], BF16)
    attnb = co.sb(es, "p2_attnb", [CDEV, CDEV], BF16)

    ub = co.sb(es, "p2_ub", [CDEV, 256], BF16)
    Wfm = co.sb(es, "p2_Wfm", [128, 2, CDEV], BF16)
    stats = co.sb(es, "p2_stats", [128, 2, 32], F32)
    statsb = co.sb(es, "p2_statsb", [128, 32], BF16)
    statsF = co.sb(es, "p2_stF", [32, 2, 128], BF16)
    sjnk = co.sb(es, "p2_sjnk", [128, 256], F32)
    ssum = co.sb(es, "p2_ssum", [128, 2, 16], F32)
    ssq2 = co.sb(es, "p2_ssq2", [128, 2, 16], F32)
    msq = co.sb(es, "p2_msq", [128, 2, 16], F32)


    lgt = co.sb(es, "p2_lgt", [128, 16], F32)
    mx4 = co.sb(es, "p2_mx4", [128, 4], F32)
    sm4 = co.sb(es, "p2_sm4", [128, 4], F32)
    rsm4 = co.sb(es, "p2_rsm4", [128, 4], F32)
    exv = co.sb(es, "p2_exv", [128, 16], F32)
    prb = co.sb(es, "p2_prb", [128, 2, 16], F32)

    rms4 = co.sb(es, "p2_rms4", [128, 4], F32)
    rr4 = co.sb(es, "p2_rr4", [128, 4], F32)

    cmb2 = co.sb(es, "p2_cmb2", [128, 8, 128], BF16)
    cmb = kntm[:].rearrange("p a b c -> p (a b c)").bitcast(F32).rearrange(
        "p (a b) -> p a b", a=8)
    cfm = kbtm[:].rearrange("p a b c -> p (a b c)").rearrange(
        "p (a b c) -> p a b c", a=8, b=2)
    sf = [co.sb(es, "p2_sf0", [128, 8, TILE], BF16)]
    sf = [sf[0], sf[0]]

    mark = co.sb(es, "p2_mark", [1, 4], F32)
    pA = co.ps(es, "p2_pA", [128, 512], F32)
    pB = co.ps(es, "p2_pB", [128, 512], F32)
    pN = co.ps(es, "p2_pN", [128, 512], F32)
    pT = co.ps(es, "p2_pT", [128, 512], F32)
    pU = co.ps(es, "p2_pU", [128, 512], F32)
    pW = co.ps(es, "p2_pW", [128, 512], F32)
    pS = co.ps(es, "p2_pS", [128, 512], F32)
    ptp = co.ps(es, "p2_pt", [128, 1024], BF16)

    guard = {}

    def mmg(bank, mms, extra_wait=None, inc=True):
        w = [guard[id(bank)]] if id(bank) in guard else []
        if extra_wait:
            w += extra_wait
        for j, f in enumerate(mms):
            ph.emit("tensor", f, wait=(w if j == 0 else None),
                    inc=([(tv, 1)] if (inc and j == len(mms) - 1) else None))
        return tv.n

    def ev(eng, f, bank=None, wait=None, sem=None):
        sem = sem or (vt if eng == "vector" else st)
        ph.emit(eng, f, wait=wait, inc=[(sem, 1)])
        if bank is not None:
            guard[id(bank)] = (sem, sem.n)
        return sem.n

    ph.emit("vector", lambda e: e.memset(Sf[:].rearrange("p a b c -> p (a b c)"), 0.0))
    ph.emit("vector", lambda e: e.memset(Sb[:].rearrange("p a b c -> p (a b c)"), 0.0))
    ph.emit("vector", lambda e: e.memset(vf[:, :, 0:64], 0.0))
    for nm in "qkv":
        ph.emit("vector", lambda e, t=halos[nm]: e.memset(
            t[:].rearrange("p a b -> p (a b)"), 0.0))
    ph.emit("vector", lambda e: e.memset(mark[:], 0.0), inc=[(vt, 1)])
    halo_vt = vt.n

    for ti in range(ntile):
        par = ti % 2
        t0 = ti * TILE
        xb = xnt[par]
        sfb = sf[par]
        ph.emit("sync", lambda e, t0=t0, xb=xb: e.dma_start(
            out=xb[:], in_=io["xn_d"][:, :, t0:t0 + TILE].rearrange("a p b -> p a b")),
            inc=[(xnl[par], 16)])
        wxn = [(xnl[par], xnl[par].n), (wld, wld.n)]

        bv = mmg(pA, [(lambda e, kc=kc, xb=xb: e.matmul(
            pA[0:4, 0:TILE], c["wbT"][:, kc, :], xb[:, kc, :],
            start=(kc == 0), stop=(kc == 7))) for kc in range(8)], extra_wait=wxn)
        ev("scalar", lambda e: e.activation(out=bP[:], in_=pA[0:4, 0:TILE],
                                            func=AF.Sigmoid), bank=pA, wait=[(tv, bv)])

        silu_st = [0, 0]
        conv_done = [0, 0]
        for (wT, cwn, hk, wrdst) in ((wq, "qcw", "q", ("qf", 0)),
                                     (wk, "kcw", "k", ("kf", 0)),
                                     (wv, "vcw", "v", ("vf", 64))):
            halo = halos[hk]
            cw = c[cwn]
            dstn, doff = wrdst
            dst = {"qf": qf, "kf": kf, "vf": vf}[dstn]
            for ot in range(8):
                bank = pA if ot % 2 == 0 else pB
                cpar = ot % 2
                gv = mmg(bank, [(lambda e, kc=kc, ot=ot, wT=wT, bank=bank, xb=xb: e.matmul(
                    bank[:, 0:TILE], wT[:, kc, ot * 128:(ot + 1) * 128], xb[:, kc, :],
                    start=(kc == 0), stop=(kc == 7))) for kc in range(8)],
                    extra_wait=wxn)
                w_pre = [(vt, conv_done[cpar] or halo_vt)]
                ev("scalar", lambda e, ot=ot, bank=bank, cpar=cpar: e.copy(
                    pre[:, cpar, 3:3 + TILE], bank[:, 0:TILE]), bank=bank,
                    wait=[(tv, gv)] + w_pre)
                ph.emit("scalar", lambda e, ot=ot, halo=halo, cpar=cpar: e.copy(
                    pre[:, cpar, 0:3], halo[:, ot, :]), inc=[(st, 1)])
                w_slot = [(st, st.n)] + ([(st, silu_st[cpar])] if silu_st[cpar] else [])
                ph.emit("vector", lambda e, ot=ot, cw=cw, cpar=cpar: e.tensor_scalar(
                    out=cvx[:, cpar, :], in0=pre[:, cpar, 0:TILE], scalar1=cw[:, ot, 0:1],
                    scalar2=None, op0=ALU.mult), wait=w_slot)
                for j in (1, 2, 3):
                    ph.emit("vector", lambda e, ot=ot, j=j, cw=cw, cpar=cpar:
                            e.scalar_tensor_tensor(
                        out=cvx[:, cpar, :], in0=pre[:, cpar, j:j + TILE],
                        scalar=cw[:, ot, j:j + 1],
                        in1=cvx[:, cpar, :], op0=ALU.mult, op1=ALU.add))
                ph.emit("vector", lambda e, ot=ot, halo=halo, cpar=cpar: e.tensor_copy(
                    out=halo[:, ot, :], in_=pre[:, cpar, TILE:TILE + 3]), inc=[(vt, 1)])
                conv_done[cpar] = vt.n
                ph.emit("scalar", lambda e, ot=ot, dst=dst, doff=doff, cpar=cpar:
                        e.activation(
                    out=dst[:, ot, doff:doff + TILE], in_=cvx[:, cpar, :], func=AF.Silu),
                    wait=[(vt, vt.n)], inc=[(st, 1)])
                silu_st[cpar] = st.n
        qkv_st = st.n
        halo_vt = vt.n

        ph.emit("scalar", lambda e: e.activation(out=qsq.rearrange("p a b -> p (a b)"),
                                                 in_=qf[:].rearrange("p a b -> p (a b)"),
                                                 func=AF.Square), inc=[(st, 1)])
        sq_q = st.n
        qmm = []
        for h in range(4):
            for j in range(2):
                idx = len(qmm)
                qmm.append(lambda e, h=h, j=j, first=(idx == 0), last=(idx == 7): e.matmul(
                    pN[0:4, 0:TILE], c["nsel"][:, h, :], qsq[:, 2 * h + j, :],
                    start=first, stop=last))
        nv = mmg(pN, qmm, extra_wait=[(st, sq_q)])
        ev("scalar", lambda e: e.activation(out=qnP[:], in_=pN[0:4, 0:TILE], func=AF.Ln,
                                            bias=m[0:4, MI[1e-6]:MI[1e-6] + 1]),
           wait=[(tv, nv)])
        ph.emit("scalar", lambda e: e.activation(out=qnP[:], in_=qnP[:], func=AF.Exp,
                                                 scale=m[0:4, MI[-0.5]:MI[-0.5] + 1]))
        ph.emit("scalar", lambda e: e.activation(out=qsq.rearrange("p a b -> p (a b)"),
                                                 in_=kf[:].rearrange("p a b -> p (a b)"),
                                                 func=AF.Square), inc=[(st, 1)])
        sq_k = st.n
        kmm = []
        for h in range(4):
            for j in range(2):
                idx = len(kmm)
                kmm.append(lambda e, h=h, j=j, first=(idx == 0), last=(idx == 7): e.matmul(
                    pN[0:4, TILE:2 * TILE], c["nsel"][:, h, :], qsq[:, 2 * h + j, :],
                    start=first, stop=last))
        nv2 = mmg(pN, kmm, extra_wait=[(st, sq_k)])
        ev("scalar", lambda e: e.activation(out=knP[:], in_=pN[0:4, TILE:2 * TILE],
                                            func=AF.Ln,
                                            bias=m[0:4, MI[1e-6]:MI[1e-6] + 1]),
           bank=pN, wait=[(tv, nv2)])
        ph.emit("scalar", lambda e: e.activation(out=knP[:], in_=knP[:], func=AF.Exp,
                                                 scale=m[0:4, MI[-0.5]:MI[-0.5] + 1]),
                inc=[(st, 1)])
        rows_st = st.n
        ph.emit("vector", lambda e: e.tensor_tensor(out=brnP[:], in0=bP[:], in1=knP[:],
                                                    op=ALU.mult),
                wait=[(st, rows_st)], inc=[(vt, 1)])
        rows_vt = vt.n
        colmap = ((qnP, qncol), (knP, kncol), (bP, bcol), (brnP, brncol))
        for ca in range(2):
            for qi, (rowt, colt) in enumerate(colmap):
                cvv = mmg(pN, [lambda e, rowt=rowt, ca=ca, qi=qi: e.transpose(
                    pN[:, 256 + 16 * (4 * ca + qi):256 + 16 * (4 * ca + qi) + 4],
                    rowt[0:4, ca * 128:(ca + 1) * 128], c["id4f"][:])],
                    extra_wait=[(vt, rows_vt), (st, rows_st)])
        cols_tv = tv.n
        for ca in range(2):
            for qi, (rowt, colt) in enumerate(colmap):
                ev("scalar", lambda e, ca=ca, qi=qi, colt=colt: e.copy(
                    colt[:, ca, :],
                    pN[:, 256 + 16 * (4 * ca + qi):256 + 16 * (4 * ca + qi) + 4]),
                    bank=(pN if (ca == 1 and qi == 3) else None),
                    wait=[(tv, cols_tv)])
        cols_st = st.n

        for ca in range(2):
            ktv = mmg(ptp, [(lambda e, kc=kc, ca=ca: e.transpose(
                ptp[:, kc * 128:(kc + 1) * 128], kf[:, kc, ca * 128:(ca + 1) * 128],
                c["id128b"][:])) for kc in range(8)],
                extra_wait=[(st, max(qkv_st, cols_st))])
            for h in range(4):
                ph.emit("vector", lambda e, ca=ca, h=h: e.tensor_scalar(
                    out=kntm[:, ca, h, :],
                    in0=ptp[:, 2 * h * 128:(2 * h + 2) * 128], scalar1=kncol[:, ca, h:h + 1],
                    scalar2=None, op0=ALU.mult), wait=[(tv, ktv)])
                ph.emit("vector", lambda e, ca=ca, h=h: e.tensor_scalar(
                    out=kbtm[:, ca, h, :], in0=ptp[:, 2 * h * 128:(2 * h + 2) * 128],
                    scalar1=kncol[:, ca, h:h + 1], scalar2=bcol[:, ca, h:h + 1],
                    op0=ALU.mult, op1=ALU.mult))
            ph.emit("vector", lambda e: e.memset(mark[:], 0.0), inc=[(vt, 1)])
            guard[id(ptp)] = (vt, vt.n)
            vtv = mmg(ptp, [(lambda e, kc=kc, ca=ca: e.transpose(
                ptp[:, kc * 128:(kc + 1) * 128], vf[:, kc, 64 + ca * 128:64 + (ca + 1) * 128],
                c["id128b"][:])) for kc in range(8)])
            for h in range(4):
                ph.emit("vector", lambda e, ca=ca, h=h: e.tensor_scalar(
                    out=vbtm[:, ca, h, :], in0=ptp[:, 2 * h * 128:(2 * h + 2) * 128],
                    scalar1=bcol[:, ca, h:h + 1], scalar2=None, op0=ALU.mult),
                    wait=[(tv, vtv)])
            ph.emit("vector", lambda e, ca=ca: e.tensor_copy(
                out=vtmT[:, ca, :, :].rearrange("p a b -> p (a b)"), in_=ptp[:]),
                inc=[(vt, 1)])
            guard[id(ptp)] = (vt, vt.n)

            for h in range(4):
                gv1 = mmg(pN, [(lambda e, j=j, ca=ca, h=h: e.matmul(
                    pN[:, 0:128], kf[:, 2 * h + j, ca * 128:(ca + 1) * 128],
                    kf[:, 2 * h + j, ca * 128:(ca + 1) * 128],
                    start=(j == 0), stop=(j == 1))) for j in range(2)], inc=False)
                gv2 = mmg(pN, [(lambda e, j=j, ca=ca, h=h: e.matmul(
                    pN[:, 128:256], kf[:, 2 * h + j, ca * 128:(ca + 1) * 128],
                    qf[:, 2 * h + j, ca * 128:(ca + 1) * 128],
                    start=(j == 0), stop=(j == 1))) for j in range(2)], inc=False)
                gv3 = mmg(pN, [lambda e, ca=ca, h=h: e.matmul(
                    pN[:, 256:384], c["sel4"][:, h, :], knP[0:4, ca * 128:(ca + 1) * 128],
                    start=True, stop=True)], inc=False)
                gv4 = mmg(pN, [lambda e, ca=ca, h=h: e.matmul(
                    pN[:, 384:512], c["sel4"][:, h, :], brnP[0:4, ca * 128:(ca + 1) * 128],
                    start=True, stop=True)])
                gv = tv.n
                ph.emit("vector", lambda e, ca=ca, h=h: e.scalar_tensor_tensor(
                    out=nt1[:], in0=pN[:, 0:128], scalar=brncol[:, ca, h:h + 1],
                    in1=c["mLSn"][:], op0=ALU.mult, op1=ALU.mult), wait=[(tv, gv)])
                ph.emit("vector", lambda e, ca=ca, h=h: e.scalar_tensor_tensor(
                    out=attnb[:], in0=pN[:, 128:256], scalar=kncol[:, ca, h:h + 1],
                    in1=c["mUI"][:], op0=ALU.mult, op1=ALU.mult))
                ph.emit("vector", lambda e: e.tensor_tensor(
                    out=Nb[:], in0=nt1[:], in1=pN[:, 256:384], op=ALU.mult))
                ph.emit("vector", lambda e, ca=ca, h=h: e.scalar_tensor_tensor(
                    out=nt1[:], in0=pN[:, 0:128], scalar=kncol[:, ca, h:h + 1],
                    in1=c["mUSn"][:], op0=ALU.mult, op1=ALU.mult))
                co.vpad(ph, 1)
                ph.emit("vector", lambda e: e.tensor_tensor(
                    out=Npb[:], in0=nt1[:], in1=pN[:, 384:512], op=ALU.mult))
                co.vpad(ph, 1)
                ph.emit("vector", lambda e: e.tensor_tensor(
                    out=TPf[:], in0=Npb[:], in1=c["id128f"][:], op=ALU.add),
                        inc=[(vt, 1)])
                guard[id(pN)] = (vt, vt.n)
                nb_vt = vt.n
                curM = lambda: Nb[:]
                curMp = lambda: Npb[:]
                for lev in range(6):
                    sq_tv = mmg(pT, [
                        lambda e, curM=curM, curMp=curMp: e.matmul(
                            pT[:, 0:128], curMp(), curM(), start=True, stop=True),
                        lambda e, curM=curM, curMp=curMp: e.matmul(
                            pT[:, 128:256], curM(), curMp(), start=True, stop=True)],
                        extra_wait=[(vt, nb_vt)])
                    ph.emit("vector", lambda e: e.tensor_copy(
                        out=Msl[:], in_=pT[:, 0:128]), wait=[(tv, sq_tv)])
                    ph.emit("vector", lambda e: e.tensor_copy(
                        out=Mpsl[:], in_=pT[:, 128:256]), inc=[(vt, 1)])
                    mev = vt.n
                    acc_tv = mmg(pT, [lambda e: e.matmul(
                        pT[:, 256:384], Msl[:], TPf[:], start=True, stop=True)],
                        extra_wait=[(vt, mev)])
                    ph.emit("vector", lambda e: e.tensor_tensor(
                        out=TPf[:], in0=TPf[:], in1=pT[:, 256:384], op=ALU.add),
                        inc=[(vt, 1)], wait=[(tv, acc_tv)])
                    guard[id(pT)] = (vt, vt.n)
                    nb_vt = vt.n
                    curM = lambda: Msl[:]
                    curMp = lambda: Mpsl[:]
                co.vpad(ph, 1)
                ph.emit("vector", lambda e: e.tensor_copy(out=TPb[:], in_=TPf[:]),
                        inc=[(vt, 1)])
                tb_vt = vt.n
                u_tv = mmg(pU, [lambda e, ca=ca, h=h: e.matmul(
                    pU[:, 0:256], TPb[:], vbtm[:, ca, h, :], start=True, stop=True)],
                    extra_wait=[(vt, tb_vt)], inc=False)
                w_tv = mmg(pW, [
                    lambda e, ca=ca, h=h: e.matmul(
                        pW[:, 0:128], kbtm[:, ca, h, 0:128], TPb[:], start=True, stop=True),
                    lambda e, ca=ca, h=h: e.matmul(
                        pW[:, 128:256], kbtm[:, ca, h, 128:256], TPb[:],
                        start=True, stop=True)])
                ph.emit("vector", lambda e: e.tensor_copy(out=usb[:], in_=pU[:, 0:256]),
                        wait=[(tv, w_tv)])
                ph.emit("vector", lambda e: e.tensor_copy(
                    out=Wfm[:].rearrange("p a b -> p (a b)"), in_=pW[:, 0:256]),
                    inc=[(vt, 1)])
                guard[id(pU)] = (vt, vt.n)
                guard[id(pW)] = (vt, vt.n)
                wev = vt.n
                ws_tv = mmg(pW, [(lambda e, kc=kc, h=h: e.matmul(
                    pW[:, 256:512], Wfm[:, kc, :], Sb[:, kc, h, :],
                    start=(kc == 0), stop=(kc == 1))) for kc in range(2)],
                    extra_wait=[(vt, wev)])
                ph.emit("vector", lambda e: e.tensor_tensor(
                    out=ub[:], in0=usb[:], in1=pW[:, 256:512], op=ALU.subtract),
                    wait=[(tv, ws_tv)], inc=[(vt, 1)])
                guard[id(pW)] = (vt, vt.n)
                ub_vt = vt.n
                o_tv = mmg(pU, [
                    lambda e, ca=ca, h=h: e.matmul(
                        pU[:, 256:512], qf[:, 2 * h, ca * 128:(ca + 1) * 128],
                        Sb[:, 0, h, :], start=True, stop=False),
                    lambda e, ca=ca, h=h: e.matmul(
                        pU[:, 256:512], qf[:, 2 * h + 1, ca * 128:(ca + 1) * 128],
                        Sb[:, 1, h, :], start=False, stop=False),
                    lambda e: e.matmul(
                        pU[:, 256:512], attnb[:], ub[:], start=False, stop=True)],
                    extra_wait=[(vt, ub_vt)])
                ph.emit("vector", lambda e, ca=ca, h=h: e.tensor_scalar(
                    out=dtm[:, ca, 2 * h:2 * h + 2, :].rearrange("p a b -> p (a b)"),
                    in0=pU[:, 256:512], scalar1=qncol[:, ca, h:h + 1], scalar2=None,
                    op0=ALU.mult), wait=[(tv, o_tv)], inc=[(vt, 1)])
                guard[id(pU)] = (vt, vt.n)
                s_tv = mmg(pS, [
                    lambda e, ca=ca, h=h: e.matmul(
                        pS[:, 0:256], kntm[:, ca, h, 0:128], ub[:], start=True, stop=True),
                    lambda e, ca=ca, h=h: e.matmul(
                        pS[:, 256:512], kntm[:, ca, h, 128:256], ub[:],
                        start=True, stop=True)], extra_wait=[(vt, ub_vt)])
                for kc in range(2):
                    ph.emit("vector", lambda e, kc=kc, h=h: e.tensor_tensor(
                        out=Sf[:, kc, h, :], in0=Sf[:, kc, h, :],
                        in1=pS[:, kc * 256:(kc + 1) * 256], op=ALU.add),
                        wait=[(tv, s_tv)])
                for kc in range(2):
                    ph.emit("vector", lambda e, kc=kc, h=h: e.tensor_copy(
                        out=Sb[:, kc, h, :], in_=Sf[:, kc, h, :]),
                        inc=([(vt, 1)] if kc == 1 else None))
                guard[id(pS)] = (vt, vt.n)

        fir_wait = [(st, qkv_st), (tv, tv.n)]
        for kc in range(8):
            ph.emit("vector", lambda e, kc=kc: e.tensor_scalar(
                out=SF_[:, kc, :], in0=vf[:, kc, 62:62 + TILE],
                scalar1=c["fsw"][:, kc, 0:1], scalar2=None, op0=ALU.mult),
                wait=(fir_wait if kc == 0 else None))
        for j in (1, 2):
            for kc in range(8):
                ph.emit("vector", lambda e, kc=kc, j=j: e.scalar_tensor_tensor(
                    out=SF_[:, kc, :], in0=vf[:, kc, 62 + j:62 + j + TILE],
                    scalar=c["fsw"][:, kc, j:j + 1], in1=SF_[:, kc, :],
                    op0=ALU.mult, op1=ALU.add))
        for kc in range(8):
            ph.emit("vector", lambda e, kc=kc: e.tensor_scalar(
                out=LF_[:, kc, :], in0=vf[:, kc, 2:2 + TILE],
                scalar1=c["flw"][:, kc, 0:1], scalar2=None, op0=ALU.mult))
        for j in range(1, 63):
            for kc in range(8):
                ph.emit("vector", lambda e, kc=kc, j=j: e.scalar_tensor_tensor(
                    out=LF_[:, kc, :], in0=vf[:, kc, 2 + j:2 + j + TILE],
                    scalar=c["flw"][:, kc, j:j + 1], in1=LF_[:, kc, :],
                    op0=ALU.mult, op1=ALU.add))
        ph.emit("vector", lambda e: e.tensor_copy(
            out=vf[:, :, 0:64], in_=vf[:, :, TILE:TILE + 64]), inc=[(vt, 1)])
        fir_vt = vt.n

        for ca in range(2):
            for (src, dstt) in ((SF_, stmT), (LF_, ltmT)):
                ptv = mmg(ptp, [(lambda e, kc=kc, ca=ca, src=src: e.transpose(
                    ptp[:, kc * 128:(kc + 1) * 128], src[:, kc, ca * 128:(ca + 1) * 128],
                    c["id128b"][:])) for kc in range(8)],
                    extra_wait=[(vt, fir_vt)])
                ev("vector", lambda e, ca=ca, dstt=dstt: e.tensor_copy(
                    out=dstt[:, ca, :, :].rearrange("p a b -> p (a b)"), in_=ptp[:]),
                    bank=ptp, wait=[(tv, ptv)])
        paths_vt = vt.n

        for ca in range(2):
            for pi, patht in enumerate((stmT, ltmT, dtm, vtmT)):
                for h in range(4):
                    col = pi * 4 + h
                    ph.emit("scalar", lambda e, ca=ca, h=h, patht=patht, col=col:
                            e.activation(
                        out=sjnk[:], in_=patht[:, ca, 2 * h:2 * h + 2, :]
                        .rearrange("p a b -> p (a b)"), func=AF.Identity,
                        accum_out=ssum[:, ca, col:col + 1]), wait=[(vt, paths_vt)])
                    ph.emit("scalar", lambda e, ca=ca, h=h, patht=patht, col=col:
                            e.activation(
                        out=sjnk[:], in_=patht[:, ca, 2 * h:2 * h + 2, :]
                        .rearrange("p a b -> p (a b)"), func=AF.Square,
                        accum_out=ssq2[:, ca, col:col + 1]))
            ph.emit("scalar", lambda e, ca=ca: e.activation(
                out=sjnk[:, 0:16], in_=ssum[:, ca, :], func=AF.Identity))
            ph.emit("scalar", lambda e, ca=ca: e.activation(
                out=sjnk[:, 0:16], in_=ssq2[:, ca, :], func=AF.Identity),
                inc=[(st, 1)])
            ph.emit("vector", lambda e, ca=ca: e.tensor_scalar(
                out=stats[:, ca, 0:32:2], in0=ssum[:, ca, :],
                scalar1=m[:, MI[1.0 / DK]:MI[1.0 / DK] + 1], scalar2=None,
                op0=ALU.mult), wait=[(st, st.n)], inc=[(vt, 1)])
            ph.emit("scalar", lambda e, ca=ca: e.activation(
                out=msq[:, ca, :], in_=stats[:, ca, 0:32:2], func=AF.Square),
                wait=[(vt, vt.n)], inc=[(st, 1)])
            ph.emit("vector", lambda e, ca=ca: e.scalar_tensor_tensor(
                out=stats[:, ca, 1:32:2], in0=ssq2[:, ca, :],
                scalar=m[:, MI[1.0 / DK]:MI[1.0 / DK] + 1], in1=msq[:, ca, :],
                op0=ALU.mult, op1=ALU.subtract), wait=[(st, st.n)])
        co.vpad(ph, 1)
        ph.emit("vector", lambda e: e.memset(mark[:], 0.0), inc=[(vt, 1)])
        stats_vt = vt.n
        statsb2 = statsb
        for ca in range(2):
            prev_tv = tv.n
            ph.emit("vector", lambda e, ca=ca: e.tensor_copy(
                out=statsb2[:], in_=stats[:, ca, :]),
                wait=[(vt, stats_vt), (tv, prev_tv)], inc=[(vt, 1)])
            stv = mmg(ptp, [lambda e: e.transpose(ptp[0:32, 0:128], statsb2[:],
                                                  c["id128b"][:])],
                      extra_wait=[(vt, vt.n)])
            ev("vector", lambda e, ca=ca: e.tensor_copy(
                out=statsF[0:32, ca, :], in_=ptp[0:32, 0:128]), bank=ptp,
                wait=[(tv, stv)])
        stats_f_vt = vt.n

        hmid = firfm[:].rearrange("p f a b -> p (f a) b")
        for mt in range(16):
            bank = pA if mt % 2 == 0 else pB
            wg = [guard[id(bank)]] if id(bank) in guard else []
            for kc in range(8):
                ph.emit("tensor", lambda e, kc=kc, mt=mt, bank=bank, xb=xb: e.matmul(
                    bank[:, 0:TILE], g1[:, kc, mt * 128:(mt + 1) * 128], xb[:, kc, :],
                    start=(kc == 0), stop=False), wait=(wg if kc == 0 else None))
            ph.emit("tensor", lambda e, mt=mt, bank=bank: e.matmul(
                bank[:, 0:TILE], g1s[:, mt * 128:(mt + 1) * 128],
                statsF[:].rearrange("p a b -> p (a b)"), start=False, stop=True),
                wait=[(vt, stats_f_vt)], inc=[(tv, 1)])
            gv = tv.n
            ev("scalar", lambda e, mt=mt, bank=bank: e.activation(
                out=hmid[:, mt, :], in_=bank[:, 0:TILE], func=AF.Gelu,
                bias=c["gb1c"][:, mt:mt + 1]), bank=bank, wait=[(tv, gv)])
        hmid_st = st.n
        lmm = [(lambda e, kc=kc: e.matmul(
            pN[0:16, 0:TILE], c["g2T"][:, kc, :], hmid[:, kc, :],
            start=(kc == 0), stop=(kc == 15))) for kc in range(16)]
        lg_tv = mmg(pN, lmm, extra_wait=[(st, hmid_st)])
        ev("vector", lambda e: e.tensor_copy(out=lgF[:], in_=pN[0:16, 0:TILE]),
           bank=pN, wait=[(tv, lg_tv)])
        lg_vt = vt.n
        for ca in range(2):
            lt_tv = mmg(pU, [lambda e, ca=ca: e.transpose(
                pU[:, 0:16], lgF[0:16, ca * 128:(ca + 1) * 128], c["id16f"][:])],
                extra_wait=[(vt, lg_vt)])
            ph.emit("vector", lambda e: e.tensor_tensor(
                out=lgt[:], in0=pU[:, 0:16], in1=c["bbb"][:], op=ALU.add),
                wait=[(tv, lt_tv)], inc=[(vt, 1)])
            guard[id(pU)] = (vt, vt.n)
            co.vpad(ph, 1)
            ph.emit("vector", lambda e: e.tensor_tensor(
                out=lgt[:], in0=lgt[:], in1=c["itb"][:], op=ALU.mult))
            co.vpad(ph, 1)
            for h in range(4):
                ph.emit("vector", lambda e, h=h: e.reduce_max(
                    out=mx4[:, h:h + 1], in_=lgt[:, 4 * h:4 * h + 4],
                    axis=mybir.AxisListType.X))
            co.vpad(ph, 4)
            for h in range(4):
                ph.emit("vector", lambda e, h=h: e.tensor_scalar(
                    out=exv[:, 4 * h:4 * h + 4], in0=lgt[:, 4 * h:4 * h + 4],
                    scalar1=mx4[:, h:h + 1], scalar2=None, op0=ALU.subtract),
                    inc=([(vt, 1)] if h == 3 else None))
            ph.emit("scalar", lambda e: e.activation(out=exv[:], in_=exv[:], func=AF.Exp),
                    wait=[(vt, vt.n)], inc=[(st, 1)])
            ph.wait("vector", st, st.n)
            for h in range(4):
                ph.emit("vector", lambda e, h=h: e.reduce_sum(
                    out=sm4[:, h:h + 1], in_=exv[:, 4 * h:4 * h + 4],
                    axis=mybir.AxisListType.X), inc=([(vt, 1)] if h == 3 else None))
            ph.emit("scalar", lambda e: e.activation(out=rsm4[:], in_=sm4[:],
                                                     func=AF.Ln), wait=[(vt, vt.n)])
            ph.emit("scalar", lambda e: e.activation(
                out=rsm4[:], in_=rsm4[:], func=AF.Exp,
                scale=m[:, MI[-1.0]:MI[-1.0] + 1]), inc=[(st, 1)])
            ph.wait("vector", st, st.n)
            for h in range(4):
                ph.emit("vector", lambda e, h=h, ca=ca: e.scalar_tensor_tensor(
                    out=prb[:, ca, 4 * h:4 * h + 4], in0=exv[:, 4 * h:4 * h + 4],
                    scalar=rsm4[:, h:h + 1], in1=c["flb"][:, 4 * h:4 * h + 4],
                    op0=ALU.mult, op1=ALU.max))
            co.vpad(ph, 1)
            for h in range(4):
                ph.emit("vector", lambda e, h=h, ca=ca: e.reduce_sum(
                    out=sm4[:, h:h + 1], in_=prb[:, ca, 4 * h:4 * h + 4],
                    axis=mybir.AxisListType.X), inc=([(vt, 1)] if h == 3 else None))
            ph.emit("scalar", lambda e: e.activation(out=rsm4[:], in_=sm4[:],
                                                     func=AF.Ln), wait=[(vt, vt.n)])
            ph.emit("scalar", lambda e: e.activation(
                out=rsm4[:], in_=rsm4[:], func=AF.Exp,
                scale=m[:, MI[-1.0]:MI[-1.0] + 1]), inc=[(st, 1)])
            ph.wait("vector", st, st.n)
            co.vpad(ph, 1)
            for h in range(4):
                ph.emit("vector", lambda e, h=h, ca=ca: e.tensor_scalar(
                    out=prb[:, ca, 4 * h:4 * h + 4], in0=prb[:, ca, 4 * h:4 * h + 4],
                    scalar1=rsm4[:, h:h + 1], scalar2=None, op0=ALU.mult))
            co.vpad(ph, 1)
            for h in range(4):
                for pi, patht in enumerate((stmT, ltmT, dtm, vtmT)):
                    if pi == 0:
                        ph.emit("vector", lambda e, h=h, ca=ca, patht=patht: e.tensor_scalar(
                            out=cmb[:, 2 * h:2 * h + 2, :].rearrange("p a b -> p (a b)"),
                            in0=patht[:, ca, 2 * h:2 * h + 2, :].rearrange("p a b -> p (a b)"),
                            scalar1=prb[:, ca, 4 * h:4 * h + 1], scalar2=None,
                            op0=ALU.mult))
                    else:
                        ph.emit("vector", lambda e, h=h, ca=ca, pi=pi, patht=patht:
                                e.scalar_tensor_tensor(
                            out=cmb[:, 2 * h:2 * h + 2, :].rearrange("p a b -> p (a b)"),
                            in0=patht[:, ca, 2 * h:2 * h + 2, :].rearrange("p a b -> p (a b)"),
                            scalar=prb[:, ca, 4 * h + pi:4 * h + pi + 1],
                            in1=cmb[:, 2 * h:2 * h + 2, :].rearrange("p a b -> p (a b)"),
                            op0=ALU.mult, op1=ALU.add))
            ph.emit("vector", lambda e: e.memset(mark[:], 0.0), inc=[(vt, 1)])
            cmb_vt = vt.n
            for h in range(4):
                ph.emit("scalar", lambda e, h=h: e.activation(
                    out=sjnk[:], in_=cmb[:, 2 * h:2 * h + 2, :].rearrange("p a b -> p (a b)"),
                    func=AF.Square, accum_out=rms4[:, h:h + 1]), wait=[(vt, cmb_vt)])
            ph.emit("scalar", lambda e: e.activation(
                out=sjnk[:, 0:4], in_=rms4[:], func=AF.Identity))
            ph.emit("scalar", lambda e: e.activation(
                out=rr4[:], in_=rms4[:], func=AF.Ln,
                scale=m[:, MI[1.0 / DK]:MI[1.0 / DK] + 1],
                bias=m[:, MI[1e-5]:MI[1e-5] + 1]))
            ph.emit("scalar", lambda e: e.activation(
                out=rr4[:], in_=rr4[:], func=AF.Exp,
                scale=m[:, MI[-0.5]:MI[-0.5] + 1]), inc=[(st, 1)])
            for h in range(4):
                ph.emit("vector", lambda e, h=h: e.scalar_tensor_tensor(
                    out=cmb2[:, 2 * h:2 * h + 2, :].rearrange("p a b -> p (a b)"),
                    in0=cmb[:, 2 * h:2 * h + 2, :].rearrange("p a b -> p (a b)"),
                    scalar=rr4[:, h:h + 1],
                    in1=c["onb"][:], op0=ALU.mult, op1=ALU.mult),
                    wait=[(st, st.n)],
                    inc=([(vt, 1)] if h == 3 else None))
            cmb2_vt = vt.n
            ctv = mmg(ptp, [(lambda e, kc=kc: e.transpose(
                ptp[:, kc * 128:(kc + 1) * 128], cmb2[:, kc, :], c["id128b"][:]))
                for kc in range(8)], extra_wait=[(vt, cmb2_vt)])
            ev("vector", lambda e, ca=ca: e.tensor_copy(
                out=cfm[:, :, ca, :],
                in_=ptp[:].rearrange("p (a b) -> p a b", a=8)),
                bank=ptp, wait=[(tv, ctv)])
        cfm_vt = vt.n
        for ot in range(8):
            bank = pA if ot % 2 == 0 else pB
            gv = mmg(bank, [(lambda e, kc=kc, ot=ot, bank=bank: e.matmul(
                bank[:, 0:TILE], wosb[:, kc, ot * 128:(ot + 1) * 128],
                cfm[:, kc, :, :].rearrange("p a b -> p (a b)"),
                start=(kc == 0), stop=(kc == 7))) for kc in range(8)],
                extra_wait=[(vt, cfm_vt)])
            ev("vector", lambda e, ot=ot, bank=bank, sfb=sfb: e.tensor_copy(
                out=sfb[:, ot, :], in_=bank[:, 0:TILE]), bank=bank,
                wait=[(tv, gv), (swb[par], swb[par].n)])
        ph.emit("sync", lambda e, t0=t0, sfb=sfb: e.dma_start(
            out=io["s_d"][:, :, t0:t0 + TILE].rearrange("a p b -> p a b"), in_=sfb[:]),
            wait=[(vt, vt.n)], inc=[(swb[par], 16)])
    for s in swb:
        ph.wait("sync", s, s.n)
    return ph




def build_p3(co, io, c, es):
    L = co.L
    ph = Phase("p3")
    if DEBUG_SYNC:
        ph.auto = {e_: co.sem("p3_a" + e_) for e_ in Phase.ENGINES}
    m = c["misc"]
    TM = min(1024, L)
    nmega = L // TM
    NT = TM // 512
    NSUB = TM // 128

    tv = co.sem("p3tv")
    vt = co.sem("p3vt")
    st = co.sem("p3st")
    wl = [co.sem("p3wl0"), co.sem("p3wl1")]
    nrl = co.sem("p3nrl")
    ssl = co.sem("p3ssl")
    xl = [co.sem("p3xl0"), co.sem("p3xl1")]
    owb = [co.sem("p3ow0"), co.sem("p3ow1")]

    nrm = co.sb(es, "p3_nrm", [128, 8, TM], BF16)
    w1b = [co.sb(es, f"p3_w1{i}", [128, 8, 1024], BF16) for i in range(2)]
    w2b = [co.sb(es, f"p3_w2{i}", [128, 8, 1024], BF16) for i in range(2)]
    hq = co.sb(es, "p3_hq", [128, 8, TM], BF16)
    hr = co.sb(es, "p3_hr", [128, 2, 512], BF16)
    racc = co.sb(es, "p3_racc", [128, 8, TM], F32)
    cbc = co.sb(es, "p3_cbc", [128, TM], BF16)
    comb8 = co.sb(es, "p3_c8", [8, NSUB, 128], BF16)
    probs = co.sb(es, "p3_probs", [128, 8], F32)
    ge88 = co.sb(es, "p3_ge", [128, 8, 8], F32)
    cnt8 = co.sb(es, "p3_cnt", [128, 8], F32)
    den = co.sb(es, "p3_den", [128, 1], F32)
    rden = co.sb(es, "p3_rden", [128, 1], F32)
    cmbb = co.sb(es, "p3_cmbb", [128, 8], BF16)
    dbg3p = co.sb(es, "p3_d3p", [128, 4, 8, 32], F32) if DEBUG_P3 else None
    gfm = co.sb(es, "p3_gfm", [128, 8, TM], BF16)
    sslab = co.sb(es, "p3_ss", [128, 8, TM], BF16)
    xt3 = [co.sb(es, f"p3_x{i}", [128, 1024], F32) for i in range(2)]
    zt = co.sb(es, "p3_zt", [128, 2, 128], BF16)
    ot3 = [co.sb(es, f"p3_o{i}", [128, 1024], F32) for i in range(2)]

    pA = co.ps(es, "p3_pA", [128, 512], F32)
    pB = co.ps(es, "p3_pB", [128, 512], F32)
    pC = co.ps(es, "p3_pC", [128, 512], F32)
    pD = co.ps(es, "p3_pD", [128, 512], F32)
    pR = co.ps(es, "p3_pR", [128, 512], F32)
    ptp = co.ps(es, "p3_pt", [128, 1024], BF16)

    guard = {}

    def mmg(bank, mms, extra_wait=None, inc=True):
        w = [guard[id(bank)]] if id(bank) in guard else []
        if extra_wait:
            w += extra_wait
        if PARANOID:
            w = w + [(vt, vt.n), (st, st.n)]
        for j, f in enumerate(mms):
            ph.emit("tensor", f, wait=(w if j == 0 else None),
                    inc=([(tv, 1)] if (inc and j == len(mms) - 1) else None))
        return tv.n

    def ev(eng, f, bank=None, wait=None):
        sem = vt if eng == "vector" else st
        if PARANOID:
            wait = (wait or []) + [(tv, tv.n), (vt if eng != "vector" else st,
                                                (vt if eng != "vector" else st).n)]
        ph.emit(eng, f, wait=wait, inc=[(sem, 1)])
        if bank is not None:
            guard[id(bank)] = (sem, sem.n)
        return sem.n

    wfree = [0, 0]
    def wload(e, q, par):
        if e == 8:
            src1 = io["sw1"][:, q * 1024:(q + 1) * 1024]
            src2 = io["sw2"][q * 1024:(q + 1) * 1024, :]
        else:
            src1 = io["ew1"][e, :, q * 1024:(q + 1) * 1024]
            src2 = io["ew2"][e, q * 1024:(q + 1) * 1024, :]
        w = [(tv, wfree[par])] if wfree[par] else None
        if PARANOID:
            w = [(tv, tv.n), (vt, vt.n), (st, st.n)]
        ph.emit("gpsimd", lambda e_, s=src1, par=par: e_.dma_start(
            out=w1b[par][:], in_=s.rearrange("(kc p) mm -> p kc mm", p=128)),
            wait=w, inc=[(wl[par], 16)])
        ph.emit("gpsimd", lambda e_, s=src2, par=par: e_.dma_start(
            out=w2b[par][:], in_=s.rearrange("(kc p) mm -> p kc mm", p=128)),
            inc=[(wl[par], 16)])

    first_racc = True
    for mi in range(nmega):
        t0 = mi * TM
        ph.emit("sync", lambda e, t0=t0: e.dma_start(
            out=nrm[:], in_=io["nr_d"][:, :, t0:t0 + TM].rearrange("a p b -> p a b")),
            wait=[(vt, vt.n), (st, st.n), (tv, tv.n)], inc=[(nrl, 16)])
        ph.emit("sync", lambda e, t0=t0: e.dma_start(
            out=sslab[:], in_=io["s_d"][:, :, t0:t0 + TM].rearrange("a p b -> p a b")),
            inc=[(ssl, 16)])
        wn = [(nrl, nrl.n)]
        NEXP = len(P3_EXPERTS)
        if mi == 0:
            wload(P3_EXPERTS[0], 0, 0)
        for sub in range(NSUB):
            rtv = mmg(pR, [(lambda e, kc=kc, sub=sub: e.matmul(
                pR[:, 0:8], nrm[:, kc, sub * 128:(sub + 1) * 128], c["rwT"][:, kc, :],
                start=(kc == 0), stop=(kc == 7))) for kc in range(8)], extra_wait=wn)
            ev("scalar", lambda e: e.activation(out=probs[:], in_=pR[:, 0:8],
                                                func=AF.Sigmoid), bank=pR,
               wait=[(tv, rtv)])
            sp = st.n
            for e8 in range(8):
                ph.emit("vector", lambda e, e8=e8: e.tensor_scalar(
                    out=ge88[:, e8, :], in0=probs[:], scalar1=probs[:, e8:e8 + 1],
                    scalar2=None, op0=ALU.is_ge), wait=([(st, sp)] if e8 == 0 else None))
            co.vpad(ph, 1)
            ph.emit("vector", lambda e: e.reduce_sum(
                out=cnt8[:].rearrange("p (a b) -> p a b", b=1), in_=ge88[:],
                axis=mybir.AxisListType.X))
            co.vpad(ph, 4)
            ph.emit("vector", lambda e: e.scalar_tensor_tensor(
                out=cnt8[:], in0=cnt8[:], scalar=m[:, MI[2.0]:MI[2.0] + 1],
                in1=probs[:], op0=ALU.is_le, op1=ALU.mult))
            co.vpad(ph, 4)
            ph.emit("vector", lambda e: e.reduce_sum(
                out=den[:], in_=cnt8[:], axis=mybir.AxisListType.X), inc=[(vt, 1)])
            ph.emit("scalar", lambda e: e.activation(
                out=rden[:], in_=den[:], func=AF.Ln,
                bias=m[:, MI[1e-6]:MI[1e-6] + 1]), wait=[(vt, vt.n)])
            ph.emit("scalar", lambda e: e.activation(
                out=rden[:], in_=rden[:], func=AF.Exp,
                scale=m[:, MI[-1.0]:MI[-1.0] + 1]), inc=[(st, 1)])
            ph.emit("vector", lambda e: e.tensor_scalar(
                out=cmbb[:], in0=cnt8[:], scalar1=rden[:, 0:1], scalar2=None,
                op0=ALU.mult), wait=[(st, st.n)], inc=[(vt, 1)])
            if DEBUG_P3:
                d3w = co.sem("p3d3w")
                ph.emit("vector", lambda e, mi=mi, sub=sub: e.tensor_copy(
                    out=dbg3p[:, mi, sub, 0:8], in_=probs[:]))
                ph.emit("vector", lambda e, mi=mi, sub=sub: e.tensor_copy(
                    out=dbg3p[:, mi, sub, 8:16], in_=cnt8[:]))
                ph.emit("vector", lambda e, mi=mi, sub=sub: e.tensor_copy(
                    out=dbg3p[:, mi, sub, 16:24], in_=cmbb[:]))
                ph.emit("vector", lambda e, mi=mi, sub=sub: e.tensor_copy(
                    out=dbg3p[:, mi, sub, 24:25], in_=den[:]))
                ph.emit("vector", lambda e, mi=mi, sub=sub: e.tensor_copy(
                    out=dbg3p[:, mi, sub, 25:26], in_=rden[:]), inc=[(vt, 1)])
            ctv = mmg(ptp, [lambda e: e.transpose(ptp[0:8, 0:128], cmbb[:],
                                                  c["id128b"][:])],
                      extra_wait=[(vt, vt.n)])
            ev("vector", lambda e, sub=sub: e.tensor_copy(
                out=comb8[:, sub, :], in_=ptp[0:8, 0:128]), bank=ptp, wait=[(tv, ctv)])
        comb_vt = vt.n
        if DEBUG_P3:
            ph.emit("sync", lambda e, mi=mi: e.dma_start(
                out=io["dbg3_d"][:, mi, :, :], in_=comb8[:]),
                wait=[(vt, comb_vt)], inc=[(d3w, 16)])
            ph.emit("sync", lambda e, mi=mi: e.dma_start(
                out=io["dbg3p_d"][:, mi, :, :], in_=dbg3p[:, mi, :, :]),
                inc=[(d3w, 16)])
            ph.wait("sync", d3w, d3w.n)

        for ei, e8 in enumerate(P3_EXPERTS):
            for q in range(P3_QUARTERS):
                gq = (mi * NEXP + ei) * 4 + q
                par = gq % 2
                nxt = gq + 1
                nei, nq = divmod(nxt % (NEXP * 4), 4) if nmega > 1 else divmod(nxt, 4)
                if nxt < (mi + 1) * NEXP * 4:
                    ni = nxt - (mi * NEXP) * 4
                    wload(P3_EXPERTS[ni // 4], ni % 4, nxt % 2)
                elif mi + 1 < nmega:
                    wload(P3_EXPERTS[0], 0, nxt % 2)
                if e8 < 8 and q == 0:
                    for nt in range(NT):
                        btv = mmg(pR, [lambda e, e8=e8, nt=nt: e.matmul(
                            pR[:, 0:512], c["sel8"][:, e8, :],
                            comb8[:, 4 * nt:4 * nt + 4, :].rearrange("p a b -> p (a b)"),
                            start=True, stop=True)], extra_wait=[(vt, comb_vt)])
                        ev("vector", lambda e, nt=nt: e.tensor_copy(
                            out=cbc[:, nt * 512:(nt + 1) * 512], in_=pR[:, 0:512]),
                            bank=pR, wait=[(tv, btv)])
                cb_vt = vt.n
                wlw = [(wl[par], wl[par].n)]
                skip = P3_SKIP if (isinstance(P3_SKIP, tuple) and P3_SKIP and isinstance(P3_SKIP[0], str) and q == 1) else ()
                for mt in range(8):
                    for nt in range(NT):
                        bank = pA if (mt * NT + nt) % 2 == 0 else pB
                        gv = mmg(bank, [(lambda e, kc=kc, mt=mt, nt=nt, bank=bank,
                                         par=par: e.matmul(
                            bank[:, 0:512], w1b[par][:, kc, mt * 128:(mt + 1) * 128],
                            nrm[:, kc, nt * 512:(nt + 1) * 512],
                            start=(kc == 0), stop=(kc == 7))) for kc in range(8)],
                            extra_wait=wlw + wn)
                        if "relu" in skip:
                            guard[id(bank)] = (tv, gv)
                        elif e8 == 8:
                            ev("scalar", lambda e, mt=mt, nt=nt, bank=bank, q=q: e.activation(
                                out=hq[:, mt, nt * 512:(nt + 1) * 512], in_=bank[:, 0:512],
                                func=AF.Relu, bias=c["sb1c"][:, q * 8 + mt:q * 8 + mt + 1]),
                                bank=bank, wait=[(tv, gv)])
                        else:
                            hpar = (mt * NT + nt) % 2
                            ev("scalar", lambda e, mt=mt, nt=nt, bank=bank, hpar=hpar,
                               e8=e8, q=q: e.activation(
                                out=hr[:, hpar, :], in_=bank[:, 0:512], func=AF.Relu,
                                bias=c["eb1c"][:, e8, q * 8 + mt:q * 8 + mt + 1]),
                                bank=bank, wait=[(tv, gv), (vt, vt.n)])
                            ph.emit("vector", lambda e, mt=mt, nt=nt, hpar=hpar:
                                    e.tensor_tensor(
                                out=hq[:, mt, nt * 512:(nt + 1) * 512], in0=hr[:, hpar, :],
                                in1=cbc[:, nt * 512:(nt + 1) * 512], op=ALU.mult),
                                wait=[(st, st.n)], inc=[(vt, 1)])
                if e8 == 8:
                    h_ready = [(st, st.n)]
                else:
                    h_ready = [(vt, vt.n)]
                if P3_TRAP is not None and (e8, q) == P3_TRAP:
                    from concourse.bass_interp import add_trap
                    ph.emit("tensor", lambda e: add_trap(e, key="w2start"),
                            wait=h_ready)
                    ph.emit("vector", lambda e: add_trap(e, key="vq0end"))
                for nt in range(NT):
                    if "w2" in skip:
                        break
                    for ft in range(8):
                        if P3_W2MAX is not None and q == 1 and (nt * 8 + ft) >= P3_W2MAX:
                            continue
                        bank = pC if (nt * 8 + ft) % 2 == 0 else pD
                        mms = [(lambda e, kc=kc, ft=ft, nt=nt, bank=bank, par=par:
                                e.matmul(
                            bank[:, 0:512], w2b[par][:, kc, ft * 128:(ft + 1) * 128],
                            hq[:, kc, nt * 512:(nt + 1) * 512],
                            start=(kc == 0), stop=(kc == 7)))
                            for kc in range(8)]
                        gv = mmg(bank, mms, extra_wait=h_ready)
                        if P3_TRAP is not None and (e8, q, nt, ft) == P3_TRAP + (0, 0):
                            from concourse.bass_interp import add_trap
                            ph.emit("tensor", lambda e: add_trap(e, key="w2g0"))
                            trap_after_add = [True]
                        else:
                            trap_after_add = [False]
                        if nt == NT - 1 and ft == 7:
                            wfree[par] = gv
                        if first_racc and e8 == P3_EXPERTS[0] and q == 0:
                            ev("vector", lambda e, ft=ft, nt=nt, bank=bank: e.tensor_copy(
                                out=racc[:, ft, nt * 512:(nt + 1) * 512],
                                in_=bank[:, 0:512]), bank=bank, wait=[(tv, gv)])
                        else:
                            ev("vector", lambda e, ft=ft, nt=nt, bank=bank: e.tensor_tensor(
                                out=racc[:, ft, nt * 512:(nt + 1) * 512],
                                in0=racc[:, ft, nt * 512:(nt + 1) * 512],
                                in1=bank[:, 0:512], op=ALU.add), bank=bank,
                                wait=[(tv, gv)])
                        if trap_after_add[0]:
                            from concourse.bass_interp import add_trap
                            ph.emit("vector", lambda e: add_trap(e, key="postadd"))
                        if e8 == 8 and q == 0 and P3_EB2:
                            btv2 = mmg(pR, [lambda e, ft=ft, nt=nt: e.matmul(
                                pR[:, 0:512], c["eb2"][:, ft * 128:(ft + 1) * 128],
                                comb8[:, 4 * nt:4 * nt + 4, :].rearrange("p a b -> p (a b)"),
                                start=True, stop=True)], extra_wait=[(vt, comb_vt)])
                            ev("vector", lambda e, ft=ft, nt=nt: e.tensor_tensor(
                                out=racc[:, ft, nt * 512:(nt + 1) * 512],
                                in0=racc[:, ft, nt * 512:(nt + 1) * 512],
                                in1=pR[:, 0:512], op=ALU.add), bank=pR,
                                wait=[(tv, btv2)])
                if e8 == P3_EXPERTS[0] and q == 3:
                    first_racc = False
        racc_vt = vt.n
        first_racc = True

        for ft in range(8):
            for nt in range(NT):
                ph.emit("scalar", lambda e, ft=ft, nt=nt: e.activation(
                    out=gfm[:, ft, nt * 512:(nt + 1) * 512],
                    in_=racc[:, ft, nt * 512:(nt + 1) * 512], func=AF.Silu,
                    bias=c["sb2c"][:, ft:ft + 1]), wait=[(vt, racc_vt)],
                    inc=[(st, 1)])
        g_st = st.n
        ztr_done = [0, 0]
        for tt in range(NSUB):
            xpar = tt % 2
            ph.emit("sync", lambda e, t0=t0, tt=tt, xpar=xpar: e.dma_start(
                out=xt3[xpar][:], in_=io["x"][t0 + tt * 128:t0 + (tt + 1) * 128, :]),
                wait=[(owb[xpar], owb[xpar].n)], inc=[(xl[xpar], 16)])
            for ft in range(8):
                zpar = ft % 2
                w_z = [(st, g_st), (ssl, ssl.n), (tv, ztr_done[zpar])]
                ph.emit("vector", lambda e, ft=ft, tt=tt, zpar=zpar: e.tensor_tensor(
                    out=zt[:, zpar, :], in0=gfm[:, ft, tt * 128:(tt + 1) * 128],
                    in1=sslab[:, ft, tt * 128:(tt + 1) * 128], op=ALU.mult),
                    wait=w_z, inc=[(vt, 1)])
                ztv = mmg(ptp, [lambda e, ft=ft, zpar=zpar: e.transpose(
                    ptp[:, ft * 128:(ft + 1) * 128], zt[:, zpar, :], c["id128b"][:])],
                    extra_wait=[(vt, vt.n)], inc=True)
                ztr_done[zpar] = tv.n
            ztv = tv.n
            opar = tt % 2
            for ft in range(8):
                ev("vector", lambda e, ft=ft, opar=opar, xpar=xpar: e.tensor_tensor(
                    out=ot3[opar][:, ft * 128:(ft + 1) * 128],
                    in0=ptp[:, ft * 128:(ft + 1) * 128],
                    in1=xt3[xpar][:, ft * 128:(ft + 1) * 128], op=ALU.add),
                    bank=(ptp if ft == 7 else None),
                    wait=[(tv, ztv), (xl[xpar], xl[xpar].n),
                          (owb[opar], owb[opar].n)])
            ph.emit("sync", lambda e, t0=t0, tt=tt, opar=opar: e.dma_start(
                out=io["out"][t0 + tt * 128:t0 + (tt + 1) * 128, :], in_=ot3[opar][:]),
                wait=[(vt, vt.n)], inc=[(owb[opar], 16)])
    for s_ in owb:
        ph.wait("sync", s_, s_.n)
    ph.wait("vector", owb[0], owb[0].n)
    ph.wait("vector", owb[1], owb[1].n)
    return ph



def build_core_program(L, host, phases=("p1",)):
    nc = bass.Bass("TRN2", target_bir_lowering=False)
    io = declare_io(nc, L, host)
    with contextlib.ExitStack() as ges:
        co = Core(nc, ges, L)
        with contextlib.ExitStack() as es0:
            c, ph0 = build_consts(co, io, es0)
            ph0.run(nc)
        if "p1" in phases:
            with contextlib.ExitStack() as es1:
                build_p1(co, io, c, es1).run(nc)
        if "p2" in phases:
            with contextlib.ExitStack() as es2:
                build_p2(co, io, c, es2).run(nc)
        if "p3" in phases:
            with contextlib.ExitStack() as es3:
                build_p3(co, io, c, es3).run(nc)
        with nc.Block("sclr") as blk0:
            def _clr(g):
                for s_ in co.sems.values():
                    g.sem_clear(s_.h)
            blk0.gpsimd(_clr)
    return nc, io


def kernel(**inputs):
    L = 4096
    host = host_layout(inputs)
    x = np.ascontiguousarray(np.asarray(inputs["L0_attn_hidden_states"], dtype=np.float32))
    nc, io = build_core_program(L, host, phases=("p1", "p2", "p3"))
    in_maps = []
    for cid in range(NCORES):
        im = {"x": np.ascontiguousarray(x[cid])}
        im.update(host)
        in_maps.append(im)
    res = run_bass_kernel_spmd(nc, in_maps, core_ids=list(range(NCORES)))
    return np.stack([r["out"] for r in res.results], axis=0)
